# version 01
# baseline (speedup 1.0000x reference)
import numpy as np
import concourse.bass as bass
import concourse.mybir as mybir
from concourse.bass_utils import run_bass_kernel_spmd

F32 = mybir.dt.float32
BF16 = mybir.dt.bfloat16
AF = mybir.ActivationFunctionType
ALU = mybir.AluOpType
AXX = mybir.AxisListType.X

T = 2048
D = 1024
NT = 16
KD = 8
H, DKD, DVD = 4, 192, 384
CH = 128
NSEG, SEG = 8, 256
NE = 64
NU = 68
NBUF = 6
GACC = 4
EPS_LN = 1e-5
EPS_L2 = 1e-6
EPS_GN = 1e-5

ENGS = ("tensor", "vector", "scalar", "gpsimd", "sync")


class EM:

    def __init__(self, nc):
        self.nc = nc
        self.q = {e: [] for e in ENGS}
        self.sem = {}
        self.cnt = {}
        self._cm = []

    def new_sem(self, name):
        cm = self.nc.semaphore(name)
        h = cm.__enter__()
        self._cm.append(cm)
        self.sem[name] = h
        self.cnt[name] = 0

    def op(self, eng, fn, wait=(), inc=None, n=1):
        val = None
        if inc is not None:
            self.cnt[inc] += n
            val = self.cnt[inc]
        wait = tuple((s, v) for (s, v) in wait if v)
        self.q[eng].append((fn, wait, inc, n))
        return val

    def v(self, name):
        return self.cnt[name]

    def snap(self):
        return dict(self.cnt)

    def close_sems(self):
        for cm in reversed(self._cm):
            cm.__exit__(None, None, None)

    def emit(self, block):
        for eng in ENGS:
            deco = getattr(block, eng)

            def make(eng):
                items = self.q[eng]
                sems = self.sem

                def body(handle):
                    for fn, waits, inc, n in items:
                        for (s, val) in waits:
                            handle.wait_ge(sems[s], val)
                        if fn is not None:
                            ins = fn()
                            if inc is not None:
                                ins.then_inc(sems[inc], n)
                return body

            deco(make(eng))


class Scope:
    def __init__(self, nc, side="left"):
        self.nc = nc
        self.side = side
        self._cms = []

    def sb(self, name, shape, dtype):
        cm = self.nc.sbuf_tensor(name, shape, dtype, side=self.side)
        t = cm.__enter__()
        self._cms.append(cm)
        return t

    def ps(self, name, shape, dtype=F32):
        cm = self.nc.psum_tensor(name, shape, dtype)
        t = cm.__enter__()
        self._cms.append(cm)
        return t

    def close(self):
        for cm in reversed(self._cms):
            cm.__exit__(None, None, None)
        self._cms = []


DEBUG_TAPS = ()
RUN_ATTN = True
RUN_MOE = True
MOE_NG = None
A2_ON = True
NSEG_RUN = NSEG
A1_LVL = 3


def build_nc(debug_taps=(), run_attn=None):
    run_attn = RUN_ATTN if run_attn is None else run_attn
    nc = bass.Bass("TRN2", target_bir_lowering=False)
    em = EM(nc)
    for s in ("DW", "DS", "TE", "VE", "SE", "GC", "DO"):
        em.new_sem(s)
    st = {"nc": nc, "em": em, "taps": set(debug_taps), "run_attn": run_attn}

    def din(name, shape):
        return nc.dram_tensor(name, shape, F32, kind="ExternalInput").ap()
    st["dx"] = din("x", [T, D])
    st["dW_in"] = din("W_in", [D, 2048])
    st["dconvw"] = din("convw", [D, 8])
    st["dconvb"] = din("convb", [D, 2])
    st["dWqkvg"] = din("Wqkvg", [D, 4616])
    st["dW_so"] = din("W_so", [1536, D])
    st["dW_out"] = din("W_out", [2048, D])
    st["dAlog"] = din("Alog", [4, 1])
    st["ddtbb"] = din("dtbb", [8, 1])
    st["dgnw"] = din("gnw", [1, DVD])
    st["drw"] = din("rw", [NE, D])
    st["dcorr"] = din("corr", [1, NE])
    st["dup"] = din("up_w", [NE, D, 512])
    st["ddn"] = din("down_w", [NE, 512, D])
    st["dshu"] = din("sh_up", [D, 2048])
    st["dshd"] = din("sh_down", [2048, D])
    st["dy"] = nc.dram_tensor("y", [T, D], F32, kind="ExternalOutput").ap()
    st["dbg"] = {}
    for name, shp in (("xn", [128, NT * 1024]), ("combT", [64, T]),
                      ("ffn", [128, NT * 1024]), ("xs", [128, NT * 1024]),
                      ("zs", [128, NT * 1024]), ("rs0", [128, 512]),
                      ("a2d", [128, 28672]), ("invd", [128, 2176])):
        if name in st["taps"]:
            st["dbg"][name] = nc.dram_tensor("dbg_" + name, shp, F32, kind="ExternalOutput").ap()

    g = Scope(nc)
    st["g"] = g
    st["xacc"] = g.sb("xacc", [128, NT * 1024], F32)
    st["ident_b"] = g.sb("ident_b", [128, 128], BF16)
    st["onesb"] = g.sb("onesb", [1, 128], BF16)
    st["epss"] = g.sb("epss", [128, 4], F32)
    xnfs = Scope(nc)
    st["xnf_scope"] = xnfs
    st["xnf"] = xnfs.sb("xnf", [128, NT * 1024], BF16)
    pm = Scope(nc)
    st["pm"] = pm
    st["combT"] = pm.sb("combT", [64, T], BF16)
    st["id64"] = pm.sb("id64", [64, 64], BF16)
    st["ones8"] = pm.sb("ones8", [128, 8], F32)
    st["corr_rep"] = pm.sb("corr_rep", [128, 64], F32)
    st["onesf"] = pm.sb("onesf", [1, 128], F32)
    st["zscr"] = nc.dram_tensor("zscr", [KD, 128, T], BF16, kind="Internal").ap()
    st["gscr"] = nc.dram_tensor("gscr", [4, 256], F32, kind="Internal").ap()

    build_consts(st)
    build_p0(st)
    if RUN_MOE:
        build_moe(st)
    if run_attn and RUN_MOE:
        build_attn(st)
    else:
        st["fin"] = em.snap()

    fin = st["fin"]
    em.op("sync", lambda: nc.sync.dma_start(
        out=st["dy"].rearrange("(t p) j -> p t j", p=128),
        in_=st["xacc"][:, :].rearrange("p (t j) -> p t j", j=1024)),
        wait=[("VE", fin["VE"]), ("SE", fin["SE"]), ("TE", fin["TE"])], inc="DO", n=16)
    em.op("sync", None, wait=[("DO", em.v("DO"))])

    with nc.Block() as block:
        em.emit(block)
    em.close_sems()
    return nc


def build_consts(st):
    nc, em, G = st["nc"], st["em"], st
    ds0 = em.op("sync", lambda: nc.sync.dma_start(out=G["corr_rep"][:1, :], in_=G["dcorr"][:, :]),
                inc="DS", n=16)
    st["ds_corr"] = ds0

    def asel(t, cmp, fill, cm, pat, sl=None):
        ap = t[:, :] if sl is None else t[:, sl[0]:sl[1]]
        return lambda: nc.gpsimd.affine_select(out=ap, in_=ap, compare_op=cmp,
                                               fill=fill, base=0, pattern=pat, channel_multiplier=cm)
    em.op("gpsimd", lambda: nc.gpsimd.memset(G["ident_b"][:, :], 0.0))
    em.op("gpsimd", asel(G["ident_b"], ALU.not_equal, 1.0, 1, [[-1, 128]]))
    em.op("gpsimd", lambda: nc.gpsimd.memset(G["id64"][:, :], 0.0))
    em.op("gpsimd", lambda: nc.gpsimd.affine_select(out=G["id64"][:, :], in_=G["id64"][:, :],
          compare_op=ALU.not_equal, fill=1.0, base=0, pattern=[[-1, 64]], channel_multiplier=1))
    em.op("gpsimd", lambda: nc.gpsimd.memset(G["onesb"][:, :], 1.0))
    em.op("gpsimd", lambda: nc.gpsimd.memset(G["onesf"][:, :], 1.0))
    em.op("gpsimd", lambda: nc.gpsimd.memset(G["ones8"][:, :], 1.0))
    em.op("gpsimd", lambda: nc.gpsimd.memset(G["epss"][:, 0:1], EPS_LN))
    em.op("gpsimd", lambda: nc.gpsimd.memset(G["epss"][:, 1:2], EPS_L2))
    gc = em.op("gpsimd", lambda: nc.gpsimd.memset(G["epss"][:, 2:3], -40.0), inc="GC")
    st["gc_consts"] = gc


def build_p0(st):
    nc, em, G = st["nc"], st["em"], st
    xacc = st["xacc"]
    xnf = st["xnf"]
    p0 = Scope(nc)
    ident_f = p0.sb("ident_f_p0", [128, 128], F32)
    xnf32 = p0.sb("xnf32", [128, NT * 1024], F32)
    xn_t = p0.sb("xn_t", [128, 1024], F32)
    stat = p0.sb("stat", [128, 8], F32)
    rwf = p0.sb("rwf", [64, 1024], F32)
    rwT = p0.sb("rwT", [128, 512], F32)
    rs = p0.sb("rs", [128, 512], F32)
    rA = p0.sb("rA", [128, 1], F32)
    rB = p0.sb("rB", [128, 1], F32)
    cbf = p0.sb("cbf", [128, 64], BF16)
    pt0 = p0.ps("pt0", [128, 512], F32)
    pt1 = p0.ps("pt1", [128, 512], F32)
    pr0 = p0.ps("pr0", [128, 512], F32)
    pr1 = p0.ps("pr1", [128, 512], F32)
    pcb = p0.ps("pcb", [128, 128], BF16)

    em.op("gpsimd", lambda: nc.gpsimd.memset(ident_f[:, :], 0.0))
    gcf = em.op("gpsimd", lambda: nc.gpsimd.affine_select(
        out=ident_f[:, :], in_=ident_f[:, :], compare_op=ALU.not_equal,
        fill=1.0, base=0, pattern=[[-1, 128]], channel_multiplier=1), inc="GC")

    ld_x = em.op("sync", lambda: nc.sync.dma_start(
        out=xacc[:, :].rearrange("p (t j) -> p t j", j=1024),
        in_=st["dx"].rearrange("(t p) j -> p t j", p=128)), inc="DS", n=16)
    ld_rw = em.op("sync", lambda: nc.sync.dma_start(out=rwf[:, :], in_=st["drw"][:, :]), inc="DS", n=16)

    for c in range(KD):
        te = em.op("tensor", lambda c=c: nc.tensor.transpose(
            pt0[:, 64 * c:64 * (c + 1)], rwf[:, 128 * c:128 * (c + 1)], ident_f[:64, :64]),
            wait=([("DS", ld_rw), ("GC", gcf)] if c == 0 else ()), inc="TE" if c == 7 else None)
    ve_rwT = em.op("vector", lambda: nc.vector.tensor_copy(rwT[:, :], pt0[:, :]),
                   wait=[("TE", te)], inc="VE")
    te_cr = em.op("tensor", lambda: nc.tensor.matmul(
        pt1[:, 0:64], st["onesf"][:1, :], st["corr_rep"][:1, :], start=True, stop=True),
        wait=[("DS", st["ds_corr"]), ("GC", st["gc_consts"])], inc="TE")
    ve_corr = em.op("vector", lambda: nc.vector.tensor_copy(st["corr_rep"][:, :], pt1[:, 0:64]),
                    wait=[("TE", te_cr)], inc="VE")
    st["ve_corr"] = ve_corr

    S0, SF, SW, SG, SGW, SGM, SEM_, SSEL, SWS, SX = 0, 64, 128, 192, 200, 208, 216, 280, 344, 408
    te_tr2_prev = 0
    se_prev = 0
    se_sig_bank = [0, 0]
    v6_prev = 0
    tec_prev = 0
    vec_prev = 0
    ve_comb = [0] * NT
    for t in range(NT):
        xt = xacc[:, 1024 * t:1024 * (t + 1)]
        em.op("vector", lambda xt=xt: nc.vector.reduce_sum(stat[:, 0:1], xt, axis=AXX),
              wait=[("DS", ld_x), ("SE", se_prev)] if t == 0 else [("SE", se_prev)])
        em.op("vector", lambda xt=xt: nc.vector.tensor_tensor(xn_t[:, :], xt, xt, ALU.mult),
              wait=[("TE", te_tr2_prev)])
        v1 = em.op("vector", lambda: nc.vector.reduce_sum(stat[:, 2:3], xn_t[:, :], axis=AXX), inc="VE")
        em.op("scalar", lambda: nc.scalar.activation(stat[:, 1:2], stat[:, 0:1], AF.Copy, scale=-1.0 / D),
              wait=[("VE", v1)])
        s1 = em.op("scalar", lambda: nc.scalar.activation(stat[:, 3:4], stat[:, 0:1], AF.Square, scale=1.0 / D),
                   inc="SE")
        v2 = em.op("vector", lambda: nc.vector.scalar_tensor_tensor(
            stat[:, 4:5], stat[:, 2:3], 1.0 / D, stat[:, 3:4], ALU.mult, ALU.subtract),
            wait=[("SE", s1)], inc="VE")
        em.op("scalar", lambda: nc.scalar.activation(stat[:, 5:6], stat[:, 4:5], AF.Ln,
                                                     bias=st["epss"][:, 0:1]),
              wait=[("VE", v2)] + ([("GC", st["gc_consts"])] if t == 0 else []))
        s2 = em.op("scalar", lambda: nc.scalar.activation(stat[:, 6:7], stat[:, 5:6], AF.Exp, scale=-0.5),
                   inc="SE")
        se_prev = s2
        v3 = em.op("vector", lambda xt=xt: nc.vector.tensor_scalar(
            xn_t[:, :], xt, stat[:, 1:2], stat[:, 6:7], ALU.add, ALU.mult),
            wait=[("SE", s2)], inc="VE")
        pt = pt0 if t % 2 == 0 else pt1
        wtr = [("VE", v3)]
        if t == 0:
            wtr.append(("VE", ve_rwT))
        if t == 1:
            wtr.append(("VE", st["ve_corr"]))
        for c in range(4):
            te1 = em.op("tensor", lambda c=c, pt=pt: nc.tensor.transpose(
                pt[:, 128 * c:128 * (c + 1)], xn_t[:, 128 * c:128 * (c + 1)], ident_f[:, :]),
                wait=wtr if c == 0 else (), inc="TE" if c == 3 else None)
        for c in range(4):
            v4 = em.op("vector", lambda c=c, t=t, pt=pt: nc.vector.tensor_copy(
                xnf32[:, 2048 * c + 128 * t:2048 * c + 128 * (t + 1)], pt[:, 128 * c:128 * (c + 1)]),
                wait=[("TE", te1)] if c == 0 else (), inc="VE" if c == 3 else None)
        for c in range(4, 8):
            te2 = em.op("tensor", lambda c=c, pt=pt: nc.tensor.transpose(
                pt[:, 128 * (c - 4):128 * (c - 3)], xn_t[:, 128 * c:128 * (c + 1)], ident_f[:, :]),
                wait=[("VE", v4)] if c == 4 else (), inc="TE" if c == 7 else None)
        te_tr2_prev = te2
        for c in range(4, 8):
            v5 = em.op("vector", lambda c=c, t=t, pt=pt: nc.vector.tensor_copy(
                xnf32[:, 2048 * c + 128 * t:2048 * c + 128 * (t + 1)], pt[:, 128 * (c - 4):128 * (c - 3)]),
                wait=[("TE", te2)] if c == 4 else (), inc="VE" if c == 7 else None)
        pr = pr0 if t % 2 == 0 else pr1
        for c in range(KD):
            ter = em.op("tensor", lambda c=c, t=t, pr=pr: nc.tensor.matmul(
                pr[:, 0:64],
                xnf32[:, 2048 * c + 128 * t:2048 * c + 128 * (t + 1)],
                rwT[:, 64 * c:64 * (c + 1)],
                start=(c == 0), stop=(c == 7)),
                wait=[("VE", v5), ("VE", ve_rwT), ("SE", se_sig_bank[t % 2])] if c == 0 else (),
                inc="TE" if c == 7 else None)
        se_sc = em.op("scalar", lambda pr=pr: nc.scalar.activation(
            rs[:, S0:S0 + 64], pr[:, 0:64], AF.Sigmoid),
            wait=[("TE", ter), ("VE", v6_prev)], inc="SE")
        se_sig_bank[t % 2] = se_sc
        def vop(fn, wait=(), inc=None):
            em.op("vector", lambda: nc.vector.drain())
            return em.op("vector", fn, wait=wait, inc=inc)

        def eq_of(src_ap, width, dst_off):
            vop(lambda: nc.vector.tensor_scalar(
                rs[:, dst_off:dst_off + width], src_ap, rA[:, 0:1], None, ALU.subtract))
            vop(lambda: nc.vector.tensor_scalar(
                rs[:, dst_off:dst_off + width], rs[:, dst_off:dst_off + width],
                1e12, 1.0, ALU.mult, ALU.add))
            vop(lambda: nc.vector.tensor_relu(
                rs[:, dst_off:dst_off + width], rs[:, dst_off:dst_off + width]))

        vop(lambda: nc.vector.tensor_tensor(rs[:, SF:SF + 64], rs[:, S0:S0 + 64], st["corr_rep"][:, :], ALU.add),
            wait=[("SE", se_sc), ("VE", st["ve_corr"])])
        for gi in range(8):
            sfc_g = rs[:, SF + 8 * gi:SF + 8 * (gi + 1)]
            vop(lambda sfc_g=sfc_g: nc.vector.reduce_max(rA[:, 0:1], sfc_g, axis=AXX))
            eq_of(sfc_g, 8, SX + 2)
            vop(lambda sfc_g=sfc_g: nc.vector.scalar_tensor_tensor(
                rs[:, SX + 16:SX + 24], rs[:, SX + 2:SX + 10], -1e30, sfc_g, ALU.mult, ALU.add))
            vop(lambda: nc.vector.reduce_max(rB[:, 0:1], rs[:, SX + 16:SX + 24], axis=AXX))
            vop(lambda gi=gi: nc.vector.tensor_tensor(
                rs[:, SG + gi:SG + gi + 1], rA[:, 0:1], rB[:, 0:1], ALU.add))
        vop(lambda: nc.vector.tensor_copy(rs[:, SGW:SGW + 8], rs[:, SG:SG + 8]))
        vop(lambda: nc.vector.memset(rs[:, SGM:SGM + 8], 0.0))
        for _ in range(4):
            vop(lambda: nc.vector.reduce_max(rA[:, 0:1], rs[:, SGW:SGW + 8], axis=AXX))
            eq_of(rs[:, SGW:SGW + 8], 8, SX + 2)
            vop(lambda: nc.vector.tensor_tensor(
                rs[:, SGM:SGM + 8], rs[:, SGM:SGM + 8], rs[:, SX + 2:SX + 10], ALU.add))
            vop(lambda: nc.vector.scalar_tensor_tensor(
                rs[:, SGW:SGW + 8], rs[:, SX + 2:SX + 10], -1e30, rs[:, SGW:SGW + 8], ALU.mult, ALU.add))
        for gi in range(8):
            vop(lambda gi=gi: nc.vector.tensor_scalar(
                rs[:, SEM_ + 8 * gi:SEM_ + 8 * (gi + 1)], st["ones8"][:, :],
                rs[:, SGM + gi:SGM + gi + 1], None, ALU.mult))
        vop(lambda: nc.vector.tensor_tensor(rs[:, SW:SW + 64], rs[:, SF:SF + 64], rs[:, SEM_:SEM_ + 64], ALU.mult))
        vop(lambda: nc.vector.memset(rs[:, SSEL:SSEL + 64], 0.0))
        for _ in range(8):
            vop(lambda: nc.vector.reduce_max(rA[:, 0:1], rs[:, SW:SW + 64], axis=AXX))
            eq_of(rs[:, SW:SW + 64], 64, SX + 2)
            vop(lambda: nc.vector.tensor_tensor(
                rs[:, SSEL:SSEL + 64], rs[:, SSEL:SSEL + 64], rs[:, SX + 2:SX + 66], ALU.add))
            vop(lambda: nc.vector.scalar_tensor_tensor(
                rs[:, SW:SW + 64], rs[:, SX + 2:SX + 66], -1e30, rs[:, SW:SW + 64], ALU.mult, ALU.add))
        vop(lambda: nc.vector.tensor_tensor(rs[:, SWS:SWS + 64], rs[:, SSEL:SSEL + 64], rs[:, S0:S0 + 64], ALU.mult))
        vop(lambda: nc.vector.reduce_sum(rA[:, 0:1], rs[:, SWS:SWS + 64], axis=AXX))
        vop(lambda: nc.vector.tensor_scalar(rA[:, 0:1], rA[:, 0:1], 1e-20, None, ALU.add))
        vop(lambda: nc.vector.reciprocal(rB[:, 0:1], rA[:, 0:1]))
        v6 = vop(lambda: nc.vector.tensor_scalar(
            cbf[:, :], rs[:, SWS:SWS + 64], rB[:, 0:1], 2.5, ALU.mult, ALU.mult),
            wait=[("TE", tec_prev)], inc="VE")
        v6_prev = v6
        tec = em.op("tensor", lambda: nc.tensor.transpose(pcb[:64, :], cbf[:, :], st["ident_b"][:, :]),
                    wait=[("VE", v6), ("VE", vec_prev)], inc="TE")
        tec_prev = tec
        vec = em.op("vector", lambda t=t: nc.vector.tensor_copy(
            st["combT"][:, 128 * t:128 * (t + 1)], pcb[:64, :]), wait=[("TE", tec)], inc="VE")
        vec_prev = vec
        ve_comb[t] = vec
        if t == 0 and "rs0" in st["taps"]:
            em.op("gpsimd", lambda: nc.gpsimd.dma_start(out=st["dbg"]["rs0"], in_=rs[:, :]),
                  wait=[("VE", v6)], inc="DW", n=16)

    ve_xnf = em.op("vector", lambda: nc.vector.tensor_copy(xnf[:, :], xnf32[:, :]), inc="VE")
    st["ve_xnf"] = ve_xnf
    st["ve_comb"] = ve_comb
    st["te_p0_end"] = em.v("TE")
    st["ve_p0_end"] = em.v("VE")

    if "xn" in st["taps"]:
        em.op("gpsimd", lambda: nc.gpsimd.dma_start(out=st["dbg"]["xn"], in_=xnf[:, :]),
              wait=[("VE", ve_xnf)], inc="DW", n=16)
    if "combT" in st["taps"]:
        em.op("gpsimd", lambda: nc.gpsimd.dma_start(out=st["dbg"]["combT"], in_=st["combT"][:, :]),
              wait=[("VE", ve_comb[-1])], inc="DW", n=16)
    p0.close()


def build_moe(st):
    nc, em, G = st["nc"], st["em"], st
    xacc, xnf, combT, id64 = st["xacc"], st["xnf"], st["combT"], st["id64"]
    m = Scope(nc)
    wu = [m.sb("wu%d" % i, [128, 4096], BF16) for i in range(NBUF)]
    wd = [m.sb("wd%d" % i, [128, 4096], BF16) for i in range(NBUF)]
    tb = m.sb("tb", [128, 1024], BF16)
    sqb = m.sb("sqb", [128, 1024], BF16)
    h2cb = [m.sb("h2cb%d" % i, [128, 1024], BF16) for i in range(2)]
    crep_sb = m.sb("crep_sb", [128, 256], BF16)
    ph = m.ps("ph", [128, 1024], F32)
    pd = [m.ps("pd%d" % i, [128, 1024], F32) for i in range(2)]
    pcrep = [m.ps("pcrep%d" % i, [128, 512], F32) for i in range(2)]

    def up_src(u):
        if u < 4:
            return G["dshu"][:, 512 * u:512 * (u + 1)].rearrange("(c p) w -> p c w", p=128)
        return G["dup"][u - 4].rearrange("(c p) w -> p c w", p=128)

    def dn_src(u):
        if u < 4:
            return G["dshd"][512 * u:512 * (u + 1), :].rearrange("(c p) w -> p c w", p=128)
        return G["ddn"][u - 4].rearrange("(c p) w -> p c w", p=128)

    dw_done = [None] * (NU + NBUF)
    te_dn_done = [0] * (NU + NBUF)

    def emit_load(u):
        b = u % NBUF
        w = ([("TE", st["te_p0_end"]), ("VE", st["ve_p0_end"])] if u < NBUF
             else [("TE", te_dn_done[u - NBUF])])
        em.op("gpsimd", lambda u=u, b=b: nc.gpsimd.dma_start(
            out=wu[b][:, :].rearrange("p (c w) -> p c w", w=512), in_=up_src(u)),
            wait=w, inc="DW", n=16)
        dw_done[u] = em.op("gpsimd", lambda u=u, b=b: nc.gpsimd.dma_start(
            out=wd[b][:, :].rearrange("p (c w) -> p c w", w=1024), in_=dn_src(u)),
            inc="DW", n=16)

    for u in range(NBUF):
        emit_load(u)

    n_groups = NU // GACC if MOE_NG is None else MOE_NG
    ve_evacadd = 0
    se_relu_prev = 0
    ve_h2c_prev = 0
    ve_crep_prev = [0, 0]
    loads_emitted = NBUF

    for grp in range(n_groups):
        for r in range(8):
            for gi in range(GACC):
                u = grp * GACC + gi
                e = u - 4
                b = u % NBUF
                first = gi == 0
                last = gi == GACC - 1
                te_crep = None
                ve_crep = None
                if e >= 0:
                    wcr = [("VE", st["ve_comb"][2 * r + 1]), ("VE", ve_crep_prev[u % 2])]
                    if u in (4, 5) and r == 0:
                        wcr.append(("VE", st["ve_p0_end"]))
                    te_crep = em.op("tensor", lambda e=e, r=r, u=u: nc.tensor.matmul(
                        pcrep[u % 2][:, 0:256], id64[:, e:e + 1].broadcast_to([64, 128]),
                        combT[:, 256 * r:256 * (r + 1)], start=True, stop=True),
                        wait=wcr, inc="TE")
                    ve_crep = em.op("vector", lambda u=u: nc.vector.tensor_copy(
                        crep_sb[:, :], pcrep[u % 2][:, 0:256]), wait=[("TE", te_crep)], inc="VE")
                    ve_crep_prev[u % 2] = ve_crep
                wup = [("DW", dw_done[u]), ("SE", se_relu_prev)]
                if u == 0 and r == 0:
                    wup.append(("VE", st["ve_xnf"]))
                te_up = None
                for mi in range(4):
                    for kc in range(KD):
                        te_up = em.op("tensor", lambda mi=mi, kc=kc, b=b, r=r: nc.tensor.matmul(
                            ph[:, 256 * mi:256 * (mi + 1)],
                            wu[b][:, 512 * kc + 128 * mi:512 * kc + 128 * (mi + 1)],
                            xnf[:, 2048 * kc + 256 * r:2048 * kc + 256 * (r + 1)],
                            start=(kc == 0), stop=(kc == KD - 1)),
                            wait=wup if (mi == 0 and kc == 0) else (),
                            inc="TE" if (mi == 3 and kc == KD - 1) else None)
                se_relu = em.op("scalar", lambda: nc.scalar.activation(
                    tb[:, :], ph[:, :], AF.Relu), wait=[("TE", te_up)], inc="SE")
                se_relu_prev = se_relu
                wsq = [("TE", te_dn_done[u - 2] if u >= 2 else 0), ("VE", ve_h2c_prev)]
                if e >= 0:
                    se_sq = em.op("scalar", lambda: nc.scalar.activation(
                        sqb[:, :], tb[:, :], AF.Square), wait=wsq, inc="SE")
                    em.op("vector", lambda: nc.vector.drain())
                    for mi in range(4):
                        ve_h2c = em.op("vector", lambda u=u, mi=mi: nc.vector.tensor_tensor(
                            h2cb[u % 2][:, 256 * mi:256 * (mi + 1)],
                            sqb[:, 256 * mi:256 * (mi + 1)], crep_sb[:, :], ALU.mult),
                            wait=([("SE", se_sq), ("VE", ve_crep),
                                   ("TE", te_dn_done[u - 2] if u >= 2 else 0)] if mi == 0 else ()),
                            inc="VE" if mi == 3 else None)
                    dn_wait = ("VE", ve_h2c)
                    ve_h2c_prev = ve_h2c
                else:
                    se_sq = em.op("scalar", lambda u=u: nc.scalar.activation(
                        h2cb[u % 2][:, :], tb[:, :], AF.Square), wait=wsq, inc="SE")
                    dn_wait = ("SE", se_sq)
                wdn = [dn_wait]
                if first:
                    wdn.append(("VE", ve_evacadd))
                if u == 0 and r == 0:
                    wdn.append(("VE", st["ve_p0_end"]))
                te_dn = None
                for t2 in range(2):
                    for ich in range(4):
                        for nh in range(2):
                            te_dn = em.op("tensor", lambda t2=t2, ich=ich, nh=nh, u=u, b=b, first=first, last=last: nc.tensor.matmul(
                                pd[t2][:, 512 * nh:512 * (nh + 1)],
                                h2cb[u % 2][:, 256 * ich + 128 * t2:256 * ich + 128 * (t2 + 1)],
                                wd[b][:, 1024 * ich + 512 * nh:1024 * ich + 512 * (nh + 1)],
                                start=(first and ich == 0), stop=(last and ich == 3)),
                                wait=wdn if (t2 == 0 and ich == 0 and nh == 0) else (),
                                inc="TE" if (t2 == 1 and ich == 3 and nh == 1) else None)
                te_dn_done[u] = te_dn
                if last:
                    for t2 in range(2):
                        tile = 2 * r + t2
                        ve_evacadd = em.op("vector", lambda t2=t2, tile=tile: nc.vector.tensor_tensor(
                            xacc[:, 1024 * tile:1024 * (tile + 1)],
                            pd[t2][:, :], xacc[:, 1024 * tile:1024 * (tile + 1)], ALU.add),
                            wait=[("TE", te_dn)], inc="VE")
        for _ in range(GACC):
            if loads_emitted < NU:
                emit_load(loads_emitted)
                loads_emitted += 1

    st["moe_fin"] = em.snap()
    if "ffn" in st["taps"]:
        em.op("gpsimd", lambda: nc.gpsimd.dma_start(out=st["dbg"]["ffn"], in_=xacc[:, :]),
              wait=[("VE", em.v("VE"))], inc="DW", n=16)
    m.close()
    st["pm"].close()


def head_kslices(h):
    return [(256 * h, 128, 0), (256 * h + 128, 64, 128)]


def build_attn(st):
    nc, em, G = st["nc"], st["em"], st
    xacc, xnf = st["xacc"], st["xnf"]
    moe_fin = st["moe_fin"]
    if A1_LVL < 0:
        st["xnf_scope"].close()
        st["fin"] = em.snap()
        return

    a = Scope(nc, side="right")
    xs = a.sb("xs", [128, NT * 1024], BF16)
    zseg = a.sb("zseg", [128, 2048], BF16)
    identf = a.sb("identf2", [128, 128], F32)
    lms01 = a.sb("lms01", [128, 128], F32)
    umi01 = a.sb("umi01", [128, 128], F32)
    bdm4 = a.sb("bdm4", [128, 512], BF16)
    i4b = a.sb("i4b", [128, 512], BF16)
    grw = a.sb("grw", [128, DVD], BF16)
    convw_sb = a.sb("convw_sb", [128, 64], F32)
    convb_sb = a.sb("convb_sb", [128, 16], F32)
    nA = a.sb("nA_sb", [4, 1], F32)
    biasv = a.sb("biasv_sb", [4, 1], F32)
    bbv = a.sb("bbv_sb", [4, 1], F32)
    gnw = a.sb("gnw_sb", [1, DVD], BF16)

    def gp(fn, wait=(), inc=None):
        return em.op("gpsimd", fn, wait=wait, inc=inc)

    def asel(ap, cmp, fill, cm, pat):
        return lambda: nc.gpsimd.affine_select(out=ap, in_=ap, compare_op=cmp,
                                               fill=fill, base=0, pattern=pat, channel_multiplier=cm)
    gp(lambda: nc.gpsimd.memset(identf[:, :], 0.0),
       wait=[("TE", moe_fin["TE"]), ("VE", moe_fin["VE"]), ("SE", moe_fin["SE"])])
    gp(asel(identf[:, :], ALU.not_equal, 1.0, 1, [[-1, 128]]))
    gp(lambda: nc.gpsimd.memset(lms01[:, :], 1.0))
    gp(asel(lms01[:, :], ALU.is_gt, 0.0, 1, [[-1, 128]]))
    gp(lambda: nc.gpsimd.memset(umi01[:, :], 1.0))
    gp(asel(umi01[:, :], ALU.is_ge, 0.0, -1, [[1, 128]]))
    for cp in range(4):
        for blk in range(4):
            sl_ = bdm4[:, 128 * cp + 32 * blk:128 * cp + 32 * (blk + 1)]
            gp(lambda sl_=sl_: nc.gpsimd.memset(sl_, 1.0))
            gp(lambda sl_=sl_, blk=blk: nc.gpsimd.affine_select(
                out=sl_, in_=sl_, compare_op=ALU.is_ge, fill=0.0,
                base=-32 * blk, pattern=[[0, 32]], channel_multiplier=1))
            gp(lambda sl_=sl_, blk=blk: nc.gpsimd.affine_select(
                out=sl_, in_=sl_, compare_op=ALU.is_ge, fill=0.0,
                base=32 * blk + 31, pattern=[[0, 32]], channel_multiplier=-1))
    for i in range(4):
        gp(lambda i=i: nc.gpsimd.memset(i4b[:, 128 * i:128 * (i + 1)], 0.0))
        gp(asel(i4b[:, 0:512][:, 128 * i:128 * (i + 1)], ALU.not_equal, 1.0, 1, [[-1, 128]]))
    gc_att = em.op("gpsimd", lambda: nc.gpsimd.memset(nA[:, :], 0.0), inc="GC")
    st["gc_att"] = gc_att

    wfence = [("TE", moe_fin["TE"]), ("VE", moe_fin["VE"]), ("SE", moe_fin["SE"])]
    ds = em.op("sync", lambda: nc.sync.dma_start(
        out=convw_sb[:, :].rearrange("p (c w) -> p c w", w=8),
        in_=st["dconvw"].rearrange("(c p) w -> p c w", p=128)), wait=wfence, inc="DS", n=16)
    ds = em.op("sync", lambda: nc.sync.dma_start(
        out=convb_sb[:, :].rearrange("p (c w) -> p c w", w=2),
        in_=st["dconvb"].rearrange("(c p) w -> p c w", p=128)), inc="DS", n=16)
    st["ds_conv"] = ds
    ds = em.op("sync", lambda: nc.sync.dma_start(out=nA[:, :], in_=st["dAlog"][:, :]),
               wait=[("GC", gc_att)], inc="DS", n=16)
    ds = em.op("sync", lambda: nc.sync.dma_start(out=biasv[:, :], in_=st["ddtbb"][0:4, :]), inc="DS", n=16)
    ds_sync_small = em.op("sync", lambda: nc.sync.dma_start(out=bbv[:, :], in_=st["ddtbb"][4:8, :]), inc="DS", n=16)
    ds_small = em.op("gpsimd", lambda: nc.gpsimd.dma_start(out=gnw[:1, :], in_=st["dgnw"][:, :]),
                     inc="DW", n=16)
    em.op("scalar", lambda: nc.scalar.activation(nA[:, :], nA[:, :], AF.Exp),
          wait=[("DS", ds_sync_small)])
    se_nA = em.op("scalar", lambda: nc.scalar.mul(nA[:, :], nA[:, :], -1.0), inc="SE")
    st["se_nA"] = se_nA

    build_a1(st, a, locals())
    if A2_ON:
        build_a2(st, a, locals())
    else:
        a.close()
    st["fin"] = em.snap()
    if "attn" in st["taps"]:
        em.op("gpsimd", lambda: nc.gpsimd.dma_start(out=st["dbg"]["attn"], in_=xacc[:, :]),
              wait=[("VE", em.v("VE"))], inc="DW", n=16)


def build_a1(st, a, env):
    nc, em = st["nc"], st["em"]
    xnf = st["xnf"]
    xs = env["xs"]
    convw_sb, convb_sb = env["convw_sb"], env["convb_sb"]
    moe_fin = st["moe_fin"]
    a1 = Scope(nc, side="right")
    zch = [a1.sb("zch%d" % i, [128, 2048], BF16) for i in range(2)]
    winb = [a1.sb("winb%d" % i, [128, 1024], BF16) for i in range(2)]
    xzst = [a1.sb("xzst%d" % i, [128, 2048], F32) for i in range(2)]
    cvacc = a1.sb("cvacc", [128, 2048], F32)
    pxz = [a1.ps("pxz%d" % i, [128, 2048], F32) for i in range(2)]

    dw_win = [0] * NT
    te_mm = [0] * NT
    se_evac = [0] * NT
    ve_taps = [0] * NT
    se_silu = [0] * NT
    ds_conv = st["ds_conv"]

    for m in range(NT if A1_LVL >= 1 else 0):
        b = m % 2
        w = [("TE", moe_fin["TE"]), ("VE", moe_fin["VE"])] if m < 2 else [("TE", te_mm[m - 2])]
        dw_win[m] = em.op("gpsimd", lambda m=m, b=b: nc.gpsimd.dma_start(
            out=winb[b][:, :].rearrange("p (c w) -> p c w", w=128),
            in_=st["dW_in"][:, 128 * m:128 * (m + 1)].rearrange("(c p) w -> p c w", p=128)),
            wait=w, inc="DW", n=16)
        wmm = [("DW", dw_win[m]), ("SE", se_evac[m - 2] if m >= 2 else 0)]
        if m < 2:
            wmm += [("VE", moe_fin["VE"]), ("SE", moe_fin["SE"])]
        for n in range(4):
            for kc in range(KD):
                te = em.op("tensor", lambda m=m, b=b, n=n, kc=kc: nc.tensor.matmul(
                    pxz[b][:, 512 * n:512 * (n + 1)],
                    winb[b][:, 128 * kc:128 * (kc + 1)],
                    xnf[:, 2048 * kc + 512 * n:2048 * kc + 512 * (n + 1)],
                    start=(kc == 0), stop=(kc == KD - 1)),
                    wait=wmm if (n == 0 and kc == 0) else (),
                    inc="TE" if (n == 3 and kc == KD - 1) else None)
        te_mm[m] = te
        se_evac[m] = em.op("scalar", lambda b=b: nc.scalar.activation(
            xzst[b][:, :], pxz[b][:, :], AF.Copy),
            wait=[("TE", te), ("VE", ve_taps[m - 2] if m >= 2 else 0)], inc="SE")
        if A1_LVL < 2:
            ve_taps[m] = em.op("vector", lambda b=b: nc.vector.tensor_copy(
                cvacc[:, 0:8], xzst[b][:, 0:8]), wait=[("SE", se_evac[m])], inc="VE")
            se_silu[m] = se_evac[m]
            continue
        cw = m % 8
        wofs = 32 * (m // 8)
        col = lambda j, m=m: convw_sb[:, 8 * (m % 8) + (0 if m < 8 else 4) + j:
                                      8 * (m % 8) + (0 if m < 8 else 4) + j + 1]
        em.op("vector", lambda b=b, col=col: nc.vector.tensor_scalar(
            cvacc[:, :], xzst[b][:, :], col(1), None, ALU.mult),
            wait=[("SE", se_evac[m]), ("DS", ds_conv), ("SE", env["se_nA"])])
        em.op("vector", lambda: nc.vector.drain())
        em.op("vector", lambda b=b, col=col: nc.vector.scalar_tensor_tensor(
            cvacc[:, 1:2048], xzst[b][:, 0:2047], col(0), cvacc[:, 1:2048], ALU.mult, ALU.add))
        em.op("vector", lambda: nc.vector.drain())
        em.op("vector", lambda b=b, col=col: nc.vector.scalar_tensor_tensor(
            cvacc[:, 0:2047], xzst[b][:, 1:2048], col(2), cvacc[:, 0:2047], ALU.mult, ALU.add))
        em.op("vector", lambda: nc.vector.drain())
        ve_taps[m] = em.op("vector", lambda b=b, col=col: nc.vector.scalar_tensor_tensor(
            cvacc[:, 0:2046], xzst[b][:, 2:2048], col(3), cvacc[:, 0:2046], ALU.mult, ALU.add),
            inc="VE")
        bcol = convb_sb[:, 2 * (m % 8) + (0 if m < 8 else 1):2 * (m % 8) + (0 if m < 8 else 1) + 1]
        if m < 8:
            se_silu[m] = em.op("scalar", lambda m=m, bcol=bcol: nc.scalar.activation(
                xs[:, 2048 * m:2048 * (m + 1)], cvacc[:, :], AF.Silu, bias=bcol),
                wait=[("VE", ve_taps[m])], inc="SE")
        else:
            zb = m % 2
            se_silu[m] = em.op("scalar", lambda zb=zb, bcol=bcol: nc.scalar.activation(
                zch[zb][:, :], cvacc[:, :], AF.Silu, bias=bcol),
                wait=[("VE", ve_taps[m]), ("DS", st.get("ds_zch_%d" % (m - 2), 0))], inc="SE")
            if A1_LVL >= 3:
                st["ds_zch_%d" % m] = em.op("sync", lambda m=m, zb=zb: nc.sync.dma_start(
                    out=st["zscr"][m - 8], in_=zch[zb][:, :]),
                    wait=[("SE", se_silu[m])], inc="DS", n=16)
    st["a1_se_last"] = em.v("SE")
    st["a1_te_last"] = em.v("TE")
    st["a1_ve_last"] = em.v("VE")
    st["a1_ds_last"] = em.v("DS")

    if "xs" in st["taps"]:
        em.op("gpsimd", lambda: nc.gpsimd.dma_start(out=st["dbg"]["xs"], in_=xs[:, :]),
              wait=[("SE", st["a1_se_last"])], inc="DW", n=16)
    if "zs" in st["taps"]:
        em.op("gpsimd", lambda: nc.gpsimd.dma_start(
            out=st["dbg"]["zs"].rearrange("p (c w) -> c p w", w=2048), in_=st["zscr"][:, :, :]),
              wait=[("DS", st["a1_ds_last"])], inc="DW", n=16)
    a1.close()
    st["xnf_scope"].close()


def build_a2(st, a, env):
    nc, em = st["nc"], st["em"]
    xacc = st["xacc"]
    xs, zseg = env["xs"], env["zseg"]
    identf, i4b = env["identf"], env["i4b"]
    lms01, umi01, bdm4 = env["lms01"], env["umi01"], env["bdm4"]
    grw, nA, biasv, gnw = env["grw"], env["nA"], env["biasv"], env["gnw"]
    bbv = env["bbv"]
    ident_b = st["ident_b"]

    a2 = Scope(nc, side="right")
    wqs = [a2.sb("wqs%d" % i, [128, 2048], BF16) for i in range(2)]
    idbg = None
    q_tm = a2.sb("q_tm", [128, 1536], BF16)
    k_tm = a2.sb("k_tm", [128, 1536], BF16)
    v_tm = a2.sb("v_tm", [128, 3072], BF16)
    g_tm = a2.sb("g_tm", [128, 3072], BF16)

    betf = a2.sb("betf", [4, 256], F32)
    gam = a2.sb("gam", [4, 256], F32)
    ngam = a2.sb("ngam", [4, 256], F32)
    bex = a2.sb("bex", [4, 256], F32)
    spt = a2.sb("spt", [4, 256], F32)
    bb4c = a2.sb("bb4c", [4, 128], F32)
    dec4c = a2.sb("dec4c", [4, 128], F32)
    onesfA = a2.sb("onesfA", [1, 512], F32)
    grow = a2.sb("grow", [1, 512], F32)
    colst = a2.sb("colst", [128, 24], F32)
    bC4 = a2.sb("bC4", [128, 4], F32)
    rq = a2.sb("rq", [128, 16], F32)
    rtmp = a2.sb("rtmp", [128, 448], F32)
    kf_s = a2.sb("kf_s", [128, 1024], BF16)
    qf_s = a2.sb("qf_s", [128, 1024], BF16)
    qbf_s = a2.sb("qbf_s", [128, 1024], BF16)
    qb_tm = a2.sb("qb_tm", [128, 768], BF16)
    A4 = a2.sb("A4", [128, 512], BF16)
    AT4 = a2.sb("AT4", [128, 512], BF16)
    X4 = a2.sb("X4", [128, 512], BF16)
    dtmp = a2.sb("dtmp", [128, 128], F32)
    Dsf = a2.sb("Dsf", [128, 128], F32)
    MT4 = a2.sb("MT4", [128, 512], BF16)
    ktl = a2.sb("ktl", [128, 192], BF16)
    kdec = a2.sb("kdec", [128, 192], BF16)
    bVb = a2.sb("bVb", [128, 384], BF16)
    Ub = a2.sb("Ub", [128, 384], BF16)
    Wnegb = a2.sb("Wnegb", [128, 256], BF16)
    Sa = [a2.sb("Sa%d" % h, [128, 384], F32) for h in range(4)]
    Sb = [a2.sb("Sb%d" % h, [64, 384], F32) for h in range(4)]
    S0bA = a2.sb("S0bA", [128, 4 * 384], BF16)
    S0bB = a2.sb("S0bB", [64, 4 * 384], BF16)
    sgb = a2.sb("sgb", [128, 384], BF16)
    sgg = sgb
    obf = a2.sb("obf", [128, 384], F32)
    ro = a2.sb("ro", [128, 4], F32)
    o4tm = a2.sb("o4tm", [128, 1536], BF16)
    ofm2 = a2.sb("ofm2", [128, 3072], BF16)
    ytmp = o4tm
    yfm2 = a2.sb("yfm2", [128, 2048], BF16)
    wsob = [a2.sb("wsob%d" % i, [128, 1024], BF16) for i in range(2)]
    wout = [a2.sb("wout%d" % i, [128, 1024], BF16) for i in range(2)]
    P2a = a2.ps("P2a", [128, 1024], F32)
    P2b = a2.ps("P2b", [128, 1024], F32)
    P1a = a2.ps("P1a", [128, 512], F32)
    P1b = a2.ps("P1b", [128, 512], F32)
    P1c = a2.ps("P1c", [128, 512], F32)
    PT = a2.ps("PTb", [128, 1024], BF16)

    te = em.op("tensor", lambda: nc.tensor.matmul(
        P1a[:, 0:DVD], st["onesb"][:1, :], gnw[:1, :], start=True, stop=True),
        wait=[("DW", env["ds_small"]), ("TE", st["a1_te_last"]), ("SE", st["a1_se_last"]),
              ("VE", st["a1_ve_last"])], inc="TE")
    ve_grw = em.op("vector", lambda: nc.vector.tensor_copy(grw[:, :], P1a[:, 0:DVD]),
                   wait=[("TE", te)], inc="VE")
    gz = None
    a1fence = [("SE", st["a1_se_last"]), ("VE", st["a1_ve_last"]), ("TE", st["a1_te_last"])]
    for h in range(4):
        em.op("gpsimd", lambda h=h: nc.gpsimd.memset(Sa[h][:, :], 0.0),
              wait=a1fence if h == 0 else ())
        em.op("gpsimd", lambda h=h: nc.gpsimd.memset(Sb[h][:, :], 0.0))
    em.op("gpsimd", lambda: nc.gpsimd.memset(S0bA[:, :], 0.0))
    em.op("gpsimd", lambda: nc.gpsimd.memset(S0bB[:, :], 0.0))
    gz0 = em.op("gpsimd", lambda: nc.gpsimd.memset(onesfA[:, :], 1.0), inc="GC")
    gz = gz0
    st["gz_ds"] = 0

    def V(fn, wait=(), inc=None, drain=True):
        if drain:
            em.op("vector", lambda: nc.vector.drain())
        return em.op("vector", fn, wait=wait, inc=inc)

    SLICES = [(0, 512, q_tm, 0), (512, 256, q_tm, 512),
              (768, 512, k_tm, 0), (1280, 256, k_tm, 512),
              (1536, 512, v_tm, 0), (2048, 512, v_tm, 512), (2560, 512, v_tm, 1024),
              (3072, 512, g_tm, 0), (3584, 512, g_tm, 512), (4096, 512, g_tm, 1024)]
    PJ = [P1a, P1b, P1c, P2a]

    ve_sched = {"proj_evac": 0}
    st_fin = None
    prev = {
        "te_qk_last": 0,
        "ve_state": 0,
        "ve_out": 0,
        "te_out": [0, 0],
        "ve_ytmp": 0,
        "dw_wq": [0, 0],
        "te_proj": [0, 0],
        "ve_pj": [0, 0, 0, 0],
        "te_pj": [0, 0, 0, 0],
        "se_gkb": 0,
        "ve_gam": 0,
        "ds_stk": 0,
        "ve_colst": 0,
        "se_dec": 0,
        "ve_kf": 0, "ve_qf": 0, "ve_qbf": 0,
        "ve_A": 0, "ve_MT": 0, "ve_AT": 0,
        "ve_X": 0,
        "ve_Ub": 0, "ve_o4": 0,
        "te_state": 0, "te_O": 0, "te_U": 0,
        "ve_ofm": 0,
        "dw_wso": [0, 0], "dw_wout": [0, 0],
        "te_y": [0, 0],
    }

    for s in range(NSEG_RUN):
        tau = [2 * s, 2 * s + 1]
        ds_zs = em.op("sync", lambda s=s: nc.sync.dma_start(
            out=zseg[:, :].rearrange("p (c w) -> p c w", w=256),
            in_=st["zscr"][:, :, 256 * s:256 * (s + 1)].rearrange("c p w -> p c w")),
            wait=[("TE", prev["te_out"][0]), ("TE", prev["te_out"][1]),
                  ("DS", st["a1_ds_last"] if s == 0 else 0)], inc="DS", n=16)
        prev["ds_zs"] = ds_zs
        for j, (c0, wd, dst, dbase) in enumerate(SLICES):
            dwh = [None, None]
            for half in range(2):
                bq = half
                wld = [("TE", prev["te_proj"][bq])]
                if s == 0 and j == 0:
                    wld += [("TE", st["a1_te_last"]), ("VE", st["a1_ve_last"])]
                dwh[half] = em.op("gpsimd", lambda c0=c0, wd=wd, bq=bq, half=half: nc.gpsimd.dma_start(
                    out=wqs[bq][:, 0:4 * wd].rearrange("p (c w) -> p c w", w=wd),
                    in_=st["dWqkvg"][512 * half:512 * (half + 1), c0:c0 + wd]
                        .rearrange("(c p) w -> p c w", p=128)),
                    wait=wld, inc="DW", n=16)
                prev["dw_wq"][bq] = dwh[half]
            for c2 in range(2):
                pj = PJ[(2 * j + c2) % 4]
                pidx = (2 * j + c2) % 4
                wmm = [("DW", dwh[0]), ("VE", prev["ve_pj"][pidx])]
                if s == 0 and j == 0:
                    wmm.append(("VE", st["a1_ve_last"]))
                    wmm.append(("SE", st["a1_se_last"]))
                if s > 0 and j < 4:
                    wmm.append(("VE", prev["ve_out"]))
                for kc in range(KD):
                    w2 = wmm if kc == 0 else ([("DW", dwh[1])] if kc == 4 else ())
                    te = em.op("tensor", lambda c2=c2, kc=kc, wd=wd, pj=pj, s=s: nc.tensor.matmul(
                        pj[:, 0:wd],
                        xs[:, 2048 * kc + 128 * (2 * s + c2):2048 * kc + 128 * (2 * s + c2) + 128],
                        wqs[kc // 4][:, wd * (kc % 4):wd * (kc % 4) + wd],
                        start=(kc == 0), stop=(kc == KD - 1)),
                        wait=w2, inc="TE" if kc == KD - 1 else None)
                prev["te_pj"][pidx] = te
                prev["ve_pj"][pidx] = em.op("vector", lambda dst=dst, dbase=dbase, c2=c2, wd=wd, pj=pj:
                    nc.vector.tensor_copy(
                        dst[:, (dst.shape[1] // 2) * c2 + dbase:(dst.shape[1] // 2) * c2 + dbase + wd],
                        pj[:, 0:wd]), wait=[("TE", te)], inc="VE")
            prev["te_proj"] = [prev["te_pj"][(2 * j + 1) % 4], prev["te_pj"][(2 * j + 1) % 4]]
        dwg = em.op("gpsimd", lambda: nc.gpsimd.dma_start(
            out=wqs[0][:, 0:KD * 8].rearrange("p (c w) -> p c w", w=8),
            in_=st["dWqkvg"][:, 4608:4616].rearrange("(c p) w -> p c w", p=128)),
            wait=[("TE", prev["te_proj"][0])], inc="DW", n=16)
        wmm = [("DW", dwg), ("SE", prev["se_gkb"]), ("VE", prev["ve_out"])]
        for kc in range(KD):
            em.op("tensor", lambda kc=kc, s=s: nc.tensor.matmul(
                P2b[0:4, 0:256], wqs[0][:, 8 * kc:8 * kc + 4],
                xs[:, 2048 * kc + 256 * s:2048 * kc + 256 * s + 256],
                start=(kc == 0), stop=(kc == KD - 1)),
                wait=wmm if kc == 0 else ())
            te = em.op("tensor", lambda kc=kc, s=s: nc.tensor.matmul(
                P2b[0:4, 512:768], wqs[0][:, 8 * kc + 4:8 * kc + 8],
                xs[:, 2048 * kc + 256 * s:2048 * kc + 256 * s + 256],
                start=(kc == 0), stop=(kc == KD - 1)),
                inc="TE" if kc == KD - 1 else None)
        prev["te_proj"] = [te, te]
        em.op("scalar", lambda: nc.scalar.activation(spt[:, :], P2b[0:4, 0:256], AF.Exp,
                                                     bias=biasv[:, 0:1]),
              wait=[("TE", te), ("SE", env["se_nA"]), ("VE", prev["ve_gam"])])
        em.op("scalar", lambda: nc.scalar.activation(spt[:, :], spt[:, :], AF.Ln, bias=1.0))
        se1 = em.op("scalar", lambda: nc.scalar.activation(betf[:, :], P2b[0:4, 512:768], AF.Sigmoid,
                                                           bias=bbv[:, 0:1]), inc="SE")
        prev["se_gkb"] = se1
        gkf = spt
        V(lambda: nc.vector.tensor_scalar(gkf[:, :], spt[:, :], nA[0:4, 0:1], None, ALU.mult),
          wait=[("SE", se1)])
        for c2 in range(2):
            V(lambda c2=c2: nc.vector.tensor_tensor_scan(
                gam[:, 128 * c2:128 * (c2 + 1)], gkf[:, 128 * c2:128 * (c2 + 1)],
                gkf[:, 128 * c2:128 * (c2 + 1)], 0.0, ALU.add, ALU.bypass))
        ve_gam = V(lambda: nc.vector.tensor_scalar(ngam[:, :], gam[:, :], -1.0, None, ALU.mult),
                   inc="VE")
        se_bex = em.op("scalar", lambda: nc.scalar.activation(bex[:, :], gam[:, :], AF.Exp),
                       wait=[("VE", ve_gam)], inc="SE")
        prev["ve_gam"] = ve_gam
        ds_gw = em.op("sync", lambda: nc.sync.dma_start(out=st["gscr"][:, :], in_=gam[:, :]),
                      wait=[("VE", ve_gam), ("DS", prev.get("ds_stk", 0))], inc="DS", n=16)

        for c2 in range(2):
            tt_ = tau[c2]
            ofmbase = 1536 * c2
            wstk = [("DS", ds_gw), ("GC", gz), ("TE", prev.get("te_gm", 0))]
            ds_stk = em.op("sync", lambda c2=c2: nc.sync.dma_start(
                out=grow[0:1, :].rearrange("p (h t) -> p h t", t=128),
                in_=st["gscr"][:, 128 * c2:128 * (c2 + 1)]),
                wait=wstk, inc="DS", n=16)
            prev["ds_stk"] = ds_stk
            ve_bb = V(lambda c2=c2: nc.vector.tensor_tensor(
                bb4c[:, :], betf[:, 128 * c2:128 * (c2 + 1)], bex[:, 128 * c2:128 * (c2 + 1)],
                ALU.mult), wait=[("SE", se_bex)], inc="VE")
            se_dec = em.op("scalar", lambda c2=c2: nc.scalar.activation(
                dec4c[:, :], gam[:, 128 * c2:128 * (c2 + 1)], AF.Exp, scale=-1.0,
                bias=gam[:, 128 * c2 + 127:128 * c2 + 128]), wait=[("SE", se_bex)], inc="SE")
            wtr = [("VE", ve_bb), ("SE", se_dec), ("VE", prev["ve_state"]), ("TE", prev["te_state"])]
            srcs = [betf[:, 128 * c2:128 * (c2 + 1)], bex[:, 128 * c2:128 * (c2 + 1)],
                    bb4c[:, :], dec4c[:, :], gam[:, 128 * c2:128 * (c2 + 1)],
                    ngam[:, 128 * c2:128 * (c2 + 1)]]
            for i, src in enumerate(srcs):
                te = em.op("tensor", lambda src=src, i=i: nc.tensor.transpose(
                    P1c[:, 4 * i:4 * (i + 1)], src, identf[:4, :4]),
                    wait=wtr if i == 0 else (), inc="TE" if i == 5 else None)
            ve_colst = V(lambda: nc.vector.tensor_copy(colst[:, :], P1c[:, 0:24]),
                         wait=[("TE", te)], inc="VE", drain=False)
            V(lambda: nc.vector.tensor_tensor(bC4[:, :], colst[:, 4:8], colst[:, 12:16], ALU.mult))
            prev["ve_colst"] = ve_colst
            qsl = q_tm[:, 768 * c2:768 * (c2 + 1)]
            ksl = k_tm[:, 768 * c2:768 * (c2 + 1)]
            wsq0 = [("VE", prev["ve_pj"][0]), ("VE", prev["ve_pj"][1]),
                    ("VE", prev["ve_pj"][2]), ("VE", prev["ve_pj"][3])]
            for h in range(4):
                V(lambda h=h, qsl=qsl: nc.vector.tensor_tensor(
                    rtmp[:, 0:192], qsl[:, 192 * h:192 * (h + 1)], qsl[:, 192 * h:192 * (h + 1)],
                    ALU.mult), wait=wsq0 if h == 0 else (), drain=True)
                V(lambda h=h: nc.vector.reduce_sum(rq[:, h:h + 1], rtmp[:, 0:192], axis=AXX),
                  drain=True)
            for h in range(4):
                V(lambda h=h, ksl=ksl: nc.vector.tensor_tensor(
                    rtmp[:, 0:192], ksl[:, 192 * h:192 * (h + 1)], ksl[:, 192 * h:192 * (h + 1)],
                    ALU.mult), drain=True)
                ve_n = V(lambda h=h: nc.vector.reduce_sum(rq[:, 4 + h:5 + h], rtmp[:, 0:192], axis=AXX),
                         drain=True, inc="VE" if h == 3 else None)
            em.op("scalar", lambda: nc.scalar.activation(rq[:, 8:16], rq[:, 0:8], AF.Ln,
                                                         bias=st["epss"][:, 1:2]),
                  wait=[("VE", ve_n)])
            se_rq = em.op("scalar", lambda: nc.scalar.activation(rq[:, 0:8], rq[:, 8:16], AF.Exp,
                                                                 scale=-0.5), inc="SE")
            for h in range(4):
                V(lambda h=h, qsl=qsl: nc.vector.tensor_scalar(
                    qsl[:, 192 * h:192 * (h + 1)], qsl[:, 192 * h:192 * (h + 1)],
                    rq[:, h:h + 1], None, ALU.mult), wait=[("SE", se_rq)] if h == 0 else (),
                    drain=(h == 0))
            for h in range(4):
                ve_nk = V(lambda h=h, ksl=ksl: nc.vector.tensor_scalar(
                    ksl[:, 192 * h:192 * (h + 1)], ksl[:, 192 * h:192 * (h + 1)],
                    rq[:, 4 + h:5 + h], None, ALU.mult), drain=False,
                    inc="VE" if h == 3 else None)
            for h in range(4):
                ve_qb = V(lambda h=h, qsl=qsl: nc.vector.tensor_scalar(
                    qb_tm[:, 192 * h:192 * (h + 1)], qsl[:, 192 * h:192 * (h + 1)],
                    colst[:, 4 + h:5 + h], None, ALU.mult), drain=(h == 0),
                    inc="VE" if h == 3 else None)
            for name, src, dst in (("ve_kf", ksl, kf_s), ("ve_qf", qsl, qf_s), ("ve_qbf", qb_tm, qbf_s)):
                wtp = [("VE", ve_nk if name != "ve_qbf" else ve_qb), ("VE", ve_grw)]
                if name == "ve_kf":
                    wtp.append(("VE", prev["ve_ofm"]))
                else:
                    wtp.append(("VE", prev[{"ve_qf": "ve_kf", "ve_qbf": "ve_qf"}[name]]))
                nt_ = 0
                for h4 in range(4):
                    for (col, nr, loc) in head_kslices(h4):
                        te = em.op("tensor", lambda src=src, h4=h4, col=col, nr=nr, loc=loc: nc.tensor.transpose(
                            PT[0:nr, col:col + 128], src[:, 192 * h4 + loc:192 * h4 + loc + nr],
                            ident_b[:, :]),
                            wait=wtp if nt_ == 0 else (), inc="TE" if nt_ == 7 else None)
                        nt_ += 1
                prev[name] = em.op("vector", lambda dst=dst: nc.vector.tensor_copy(
                    dst[:, :], PT[:, 0:1024]), wait=[("TE", te)], inc="VE")
            for h in range(4):
                ks_ = head_kslices(h)
                wk2 = [("VE", prev["ve_kf"]), ("VE", prev["ve_A"])]
                te_k2 = None
                for (cc, nr, loc) in ks_:
                    te_k2 = em.op("tensor", lambda cc=cc, nr=nr, first=(te_k2 is None): nc.tensor.matmul(
                        P1a[:, 0:128], kf_s[0:nr, cc:cc + 128], kf_s[0:nr, cc:cc + 128],
                        start=first, stop=not first),
                        wait=wk2 if te_k2 is None else (), inc="TE")
                wqk = [("VE", prev["ve_qf"]), ("VE", prev["ve_MT"])]
                te_qk = None
                for (cc, nr, loc) in ks_:
                    te_qk = em.op("tensor", lambda cc=cc, nr=nr, first=(te_qk is None): nc.tensor.matmul(
                        P1b[:, 0:128], kf_s[0:nr, cc:cc + 128], qf_s[0:nr, cc:cc + 128],
                        start=first, stop=not first),
                        wait=wqk if te_qk is None else (), inc="TE")
                prev["te_qk_last"] = te_qk
                wgm = [("DS", prev["ds_stk"]), ("VE", prev["ve_colst"])]
                te_gm = em.op("tensor", lambda h=h: nc.tensor.matmul(
                    P1c[:, 128:256], onesfA[:1, 0:128], grow[0:1, 128 * h:128 * (h + 1)],
                    start=True, stop=True), wait=wgm, inc="TE")
                prev["te_gm"] = te_gm
                V(lambda h=h: nc.vector.tensor_scalar(
                    dtmp[:, :], P1c[:, 128:256], -1.0, colst[:, 16 + h:17 + h],
                    ALU.mult, ALU.add), wait=[("TE", te_gm)], drain=False)
                ve_d = V(lambda: nc.vector.scalar_tensor_tensor(
                    dtmp[:, :], dtmp[:, :], 40.0, lms01[:, :], ALU.add, ALU.mult),
                    drain=True, inc="VE")
                se_ds = em.op("scalar", lambda: nc.scalar.activation(
                    Dsf[:, :], dtmp[:, :], AF.Exp, bias=st["epss"][:, 2:3]),
                    wait=[("VE", ve_d)], inc="SE")
                ve_A = V(lambda h=h: nc.vector.scalar_tensor_tensor(
                    A4[:, 128 * h:128 * (h + 1)], P1a[:, 0:128], colst[:, h:h + 1], Dsf[:, :],
                    ALU.mult, ALU.mult), wait=[("SE", se_ds), ("TE", te_k2)], inc="VE", drain=False)
                prev["ve_A"] = ve_A
                if "invd" in st["taps"] and s == 0 and c2 == 1 and h == 3:
                    idbg = v_tm
                    V(lambda: nc.vector.tensor_copy(idbg[:, 0:128], P1c[:, 128:256]), drain=True)
                    V(lambda: nc.vector.tensor_copy(idbg[:, 128:256], Dsf[:, :]), drain=False)
                    V(lambda: nc.vector.tensor_copy(idbg[:, 256:384], P1a[:, 0:128]), drain=False)
                    V(lambda h=h: nc.vector.tensor_copy(idbg[:, 384:512], A4[:, 128 * h:128 * (h + 1)]),
                      drain=False)
                    ve_idbg = V(lambda: nc.vector.tensor_copy(idbg[:, 512:536], colst[:, :]),
                                drain=False, inc="VE")
                    em.op("gpsimd", lambda: nc.gpsimd.dma_start(
                        out=st["dbg"]["invd"], in_=idbg[:, 0:2176]),
                        wait=[("VE", ve_idbg)], inc="DW", n=16)
                    prev["ve_MT"] = ve_idbg
                V(lambda h=h: nc.vector.tensor_scalar(
                    dtmp[:, :], P1c[:, 128:256], 1.0, colst[:, 20 + h:21 + h],
                    ALU.mult, ALU.add), wait=[("SE", se_ds)], drain=False)
                ve_d2 = V(lambda: nc.vector.scalar_tensor_tensor(
                    dtmp[:, :], dtmp[:, :], 40.0, umi01[:, :], ALU.add, ALU.mult),
                    drain=True, inc="VE")
                se_di = em.op("scalar", lambda: nc.scalar.activation(
                    Dsf[:, :], dtmp[:, :], AF.Exp, bias=st["epss"][:, 2:3]),
                    wait=[("VE", ve_d2)], inc="SE")
                ve_MT = V(lambda h=h: nc.vector.tensor_tensor(
                    MT4[:, 128 * h:128 * (h + 1)], P1b[:, 0:128], Dsf[:, :], ALU.mult),
                    wait=[("SE", se_di), ("TE", te_qk)], inc="VE", drain=False)
                prev["ve_MT"] = ve_MT
                te_at = em.op("tensor", lambda h=h: nc.tensor.transpose(
                    PT[:, 0:128], A4[:, 128 * h:128 * (h + 1)], ident_b[:, :]),
                    wait=[("VE", ve_A), ("VE", prev["ve_qbf"] if h == 0 else prev["ve_AT"])],
                    inc="TE")
                prev["ve_AT"] = em.op("vector", lambda h=h: nc.vector.tensor_copy(
                    AT4[:, 128 * h:128 * (h + 1)], PT[:, 0:128]), wait=[("TE", te_at)], inc="VE")
            Db4 = kf_s[:, 0:512]
            DbT4 = kf_s[:, 512:1024]
            F4 = qb_tm[:, 0:512]
            Hb = qf_s[:, 0:512]
            Gb = qf_s[:, 512:1024]
            XC = o4tm[:, 0:512]
            Tdb = kf_s[:, 0:512]
            V(lambda: nc.vector.tensor_tensor(Db4, A4[:, :], bdm4[:, :], ALU.mult),
              wait=[("VE", prev["ve_A"]), ("TE", prev["te_qk_last"]), ("VE", prev["ve_MT"])],
              drain=False)
            V(lambda: nc.vector.tensor_tensor(DbT4, AT4[:, :], bdm4[:, :], ALU.mult),
              wait=[("VE", prev["ve_AT"])], drain=False)
            V(lambda: nc.vector.tensor_tensor(F4, A4[:, :], Db4, ALU.subtract), drain=True)
            ve_X = V(lambda: nc.vector.tensor_tensor(X4[:, :], i4b[:, 0:512], DbT4, ALU.subtract),
                     drain=True, inc="VE")
            ve_sq = ve_X
            for j in range(4):
                wpow = [("VE", ve_sq)]
                if j == 0:
                    wpow.append(("VE", prev["ve_pj"][3]))
                te_pow = None
                for h in range(4):
                    te_pow = em.op("tensor", lambda h=h: nc.tensor.matmul(
                        P2a[:, 128 * h:128 * (h + 1)], DbT4[:, 128 * h:128 * (h + 1)],
                        Db4[:, 128 * h:128 * (h + 1)], start=True, stop=True),
                        wait=wpow if h == 0 else (), inc="TE" if h == 3 else None)
                if j < 3:
                    for h in range(4):
                        te_pow = em.op("tensor", lambda h=h: nc.tensor.matmul(
                            P2a[:, 512 + 128 * h:512 + 128 * (h + 1)], Db4[:, 128 * h:128 * (h + 1)],
                            DbT4[:, 128 * h:128 * (h + 1)], start=True, stop=True),
                            inc="TE" if h == 3 else None)
                if j < 3:
                    V(lambda: nc.vector.tensor_copy(Db4, P2a[:, 0:512]),
                      wait=[("TE", te_pow)], drain=False)
                    ve_sq = V(lambda: nc.vector.tensor_copy(DbT4, P2a[:, 512:1024]),
                              drain=False, inc="VE")
                else:
                    ve_sq = V(lambda: nc.vector.tensor_copy(Db4, P2a[:, 0:512]),
                              wait=[("TE", te_pow)], drain=False, inc="VE")
                te_prod = None
                for h in range(4):
                    te_prod = em.op("tensor", lambda h=h: nc.tensor.matmul(
                        P2b[:, 128 * h:128 * (h + 1)], Db4[:, 128 * h:128 * (h + 1)],
                        X4[:, 128 * h:128 * (h + 1)], start=True, stop=True),
                        wait=[("VE", ve_sq), ("VE", ve_X)] if h == 0 else (),
                        inc="TE" if h == 3 else None)
                ve_X = V(lambda: nc.vector.tensor_tensor(X4[:, :], X4[:, :], P2b[:, 0:512], ALU.add),
                         wait=[("TE", te_prod)], drain=False, inc="VE")
            te_H = None
            for h in range(4):
                te_H = em.op("tensor", lambda h=h: nc.tensor.matmul(
                    P2b[:, 128 * h:128 * (h + 1)], F4[:, 128 * h:128 * (h + 1)],
                    X4[:, 128 * h:128 * (h + 1)], start=True, stop=True),
                    wait=[("VE", ve_X)] if h == 0 else (), inc="TE" if h == 3 else None)
            te_G = None
            for h in range(4):
                te_G = em.op("tensor", lambda h=h: nc.tensor.matmul(
                    P2a[:, 128 * h:128 * (h + 1)], X4[:, 128 * h:128 * (h + 1)],
                    F4[:, 128 * h:128 * (h + 1)], start=True, stop=True),
                    inc="TE" if h == 3 else None)
            V(lambda: nc.vector.tensor_copy(Hb, P2b[:, 0:512]), wait=[("TE", te_H), ("TE", te_G)],
              drain=False)
            ve_HG = V(lambda: nc.vector.tensor_copy(Gb, P2a[:, 0:512]), drain=False, inc="VE")
            te_h2 = None
            for h in range(4):
                te_h2 = em.op("tensor", lambda h=h: nc.tensor.matmul(
                    P2b[:, 512 + 128 * h:512 + 128 * (h + 1)], Gb[:, 128 * h:128 * (h + 1)],
                    Hb[:, 128 * h:128 * (h + 1)], start=True, stop=True),
                    wait=[("VE", ve_HG)] if h == 0 else (), inc="TE" if h == 3 else None)
            te_g2 = None
            for h in range(4):
                te_g2 = em.op("tensor", lambda h=h: nc.tensor.matmul(
                    P2a[:, 512 + 128 * h:512 + 128 * (h + 1)], Hb[:, 128 * h:128 * (h + 1)],
                    Gb[:, 128 * h:128 * (h + 1)], start=True, stop=True),
                    inc="TE" if h == 3 else None)
            ve_g2 = V(lambda: nc.vector.tensor_copy(F4, P2a[:, 512:1024]),
                      wait=[("TE", te_g2)], drain=False, inc="VE")
            te_h3 = None
            for h in range(4):
                te_h3 = em.op("tensor", lambda h=h: nc.tensor.matmul(
                    P2a[:, 128 * h:128 * (h + 1)], F4[:, 128 * h:128 * (h + 1)],
                    Hb[:, 128 * h:128 * (h + 1)], start=True, stop=True),
                    wait=[("VE", ve_g2)] if h == 0 else (), inc="TE" if h == 3 else None)
            V(lambda: nc.vector.tensor_tensor(XC, i4b[:, 0:512], Hb, ALU.subtract),
              wait=[("TE", te_h3)], drain=True)
            V(lambda: nc.vector.tensor_tensor(XC, XC, P2b[:, 512:1024], ALU.add), drain=True)
            ve_xc = V(lambda: nc.vector.tensor_tensor(XC, XC, P2a[:, 0:512], ALU.subtract),
                      drain=True, inc="VE")
            te_tdt = None
            for h in range(4):
                te_tdt = em.op("tensor", lambda h=h: nc.tensor.transpose(
                    PT[:, 128 * h:128 * (h + 1)], X4[:, 128 * h:128 * (h + 1)], ident_b[:, :]),
                    wait=[("VE", ve_xc)] if h == 0 else (), inc="TE" if h == 3 else None)
            ve_td = V(lambda: nc.vector.tensor_copy(Tdb, PT[:, 0:512]),
                      wait=[("TE", te_tdt)], drain=False, inc="VE")
            te_fin = None
            for h in range(4):
                te_fin = em.op("tensor", lambda h=h: nc.tensor.matmul(
                    P2b[:, 128 * h:128 * (h + 1)], Tdb[:, 128 * h:128 * (h + 1)],
                    XC[:, 128 * h:128 * (h + 1)], start=True, stop=True),
                    wait=[("VE", ve_td)] if h == 0 else (), inc="TE" if h == 3 else None)
            ve_X = V(lambda: nc.vector.tensor_copy(X4[:, :], P2b[:, 0:512]),
                     wait=[("TE", te_fin)], drain=False, inc="VE")
            for h in range(4):
                ks_ = head_kslices(h)
                vsl = v_tm[:, 1536 * c2 + 384 * h:1536 * c2 + 384 * (h + 1)]
                gsl = g_tm[:, 1536 * c2 + 384 * h:1536 * c2 + 384 * (h + 1)]
                ve_bv = V(lambda vsl=vsl, h=h: nc.vector.tensor_scalar(
                    bVb[:, :], vsl, colst[:, h:h + 1], None, ALU.mult),
                    wait=[("VE", ve_X), ("TE", prev["te_U"])], inc="VE", drain=True)
                V(lambda h=h, ksl=ksl: nc.vector.tensor_scalar(
                    ktl[:, :], ksl[:, 192 * h:192 * (h + 1)], colst[:, 8 + h:9 + h], None, ALU.mult),
                  drain=False)
                ve_kd = V(lambda h=h, ksl=ksl: nc.vector.tensor_scalar(
                    kdec[:, :], ksl[:, 192 * h:192 * (h + 1)], colst[:, 12 + h:13 + h], None, ALU.mult),
                    drain=False, inc="VE")
                te_u1 = em.op("tensor", lambda h=h: nc.tensor.matmul(
                    P1a[:, 0:384], X4[:, 128 * h:128 * (h + 1)], bVb[:, :], start=True, stop=False),
                    wait=[("VE", ve_bv), ("VE", prev["ve_Ub"])], inc="TE")
                em.op("tensor", lambda h=h: nc.tensor.matmul(
                    P1c[:, 0:128], ktl[:, 0:128], X4[:, 128 * h:128 * (h + 1)],
                    start=True, stop=True),
                    wait=[("VE", ve_kd), ("TE", prev["te_state"]), ("VE", prev["ve_state"])])
                te_w = em.op("tensor", lambda h=h: nc.tensor.matmul(
                    P1c[0:64, 128:256], ktl[:, 128:192], X4[:, 128 * h:128 * (h + 1)],
                    start=True, stop=True), inc="TE")
                ve_wn = V(lambda: nc.vector.tensor_scalar(
                    Wnegb[:, :], P1c[:, 0:256], -1.0, None, ALU.mult),
                    wait=[("TE", te_w)], inc="VE", drain=False)
                te_u = em.op("tensor", lambda h=h: nc.tensor.matmul(
                    P1a[:, 0:384], Wnegb[:, 0:128], S0bA[:, 384 * h:384 * (h + 1)],
                    start=False, stop=False),
                    wait=[("VE", ve_wn), ("VE", prev["ve_state"])])
                te_u = em.op("tensor", lambda h=h: nc.tensor.matmul(
                    P1a[:, 0:384], Wnegb[0:64, 128:256], S0bB[:, 384 * h:384 * (h + 1)],
                    start=False, stop=True), inc="TE")
                prev["te_U"] = te_u
                ve_ub = V(lambda: nc.vector.tensor_copy(Ub[:, :], P1a[:, 0:384]),
                          wait=[("TE", te_u)], inc="VE", drain=False)
                prev["ve_Ub"] = ve_ub
                te_o = em.op("tensor", lambda h=h: nc.tensor.matmul(
                    P1b[:, 0:384], MT4[:, 128 * h:128 * (h + 1)], Ub[:, :], start=True, stop=False),
                    wait=[("VE", ve_ub), ("VE", prev["ve_o4"])])
                for i, (cc, nr, loc) in enumerate(ks_):
                    s0 = S0bA if loc == 0 else S0bB
                    te_o = em.op("tensor", lambda cc=cc, nr=nr, h=h, i=i, s0=s0: nc.tensor.matmul(
                        P1b[:, 0:384], qbf_s[0:nr, cc:cc + 128],
                        s0[0:nr, 384 * h:384 * (h + 1)],
                        start=False, stop=(i == 1)), inc="TE" if i == 1 else None)
                prev["te_O"] = te_o
                te_s1 = em.op("tensor", lambda h=h: nc.tensor.matmul(
                    P2b[:, 512:896], kdec[:, 0:128], Ub[:, :], start=True, stop=True),
                    wait=[("VE", ve_ub)])
                te_s2 = em.op("tensor", lambda h=h: nc.tensor.matmul(
                    P1c[0:64, 0:384], kdec[:, 128:192], Ub[:, :], start=True, stop=True),
                    wait=[("VE", ve_wn)], inc="TE")
                prev["te_state"] = te_s2
                V(lambda h=h: nc.vector.scalar_tensor_tensor(
                    Sa[h][:, :], Sa[h][:, :], bC4[:, h:h + 1], P2b[:, 512:896], ALU.mult, ALU.add),
                  wait=[("TE", te_s2)], drain=True)
                V(lambda h=h: nc.vector.scalar_tensor_tensor(
                    Sb[h][:, :], Sb[h][:, :], bC4[0:64, h:h + 1], P1c[0:64, 0:384], ALU.mult, ALU.add),
                  drain=False)
                V(lambda h=h: nc.vector.tensor_copy(S0bA[:, 384 * h:384 * (h + 1)], Sa[h][:, :]),
                  drain=True)
                ve_st = V(lambda h=h: nc.vector.tensor_copy(S0bB[:, 384 * h:384 * (h + 1)], Sb[h][:, :]),
                          drain=False, inc="VE")
                prev["ve_state"] = ve_st
                se_sg = em.op("scalar", lambda gsl=gsl: nc.scalar.activation(sgb[:, :], gsl, AF.Silu),
                              wait=[("VE", prev["ve_pj"][3])], inc="SE")
                V(lambda: nc.vector.tensor_tensor(sgg[:, :], sgb[:, :], grw[:, :], ALU.mult),
                  wait=[("SE", se_sg), ("VE", ve_grw)], drain=False)
                V(lambda: nc.vector.tensor_copy(obf[:, :], P1b[:, 0:384]),
                  wait=[("TE", te_o)], drain=False)
                V(lambda: nc.vector.tensor_tensor(rtmp[:, 0:384], obf[:, :], obf[:, :], ALU.mult),
                  drain=True)
                ve_ro = V(lambda: nc.vector.reduce_sum(ro[:, 0:1], rtmp[:, 0:384], axis=AXX),
                          drain=True, inc="VE")
                em.op("scalar", lambda: nc.scalar.activation(ro[:, 1:2], ro[:, 0:1], AF.Ln,
                                                             scale=1.0 / DVD, bias=st["epss"][:, 0:1]),
                      wait=[("VE", ve_ro)])
                se_ro = em.op("scalar", lambda: nc.scalar.activation(ro[:, 2:3], ro[:, 1:2], AF.Exp,
                                                                     scale=-0.5), inc="SE")
                ve_o4 = V(lambda h=h: nc.vector.scalar_tensor_tensor(
                    o4tm[:, 384 * h:384 * (h + 1)], obf[:, :], ro[:, 2:3], sgg[:, :],
                    ALU.mult, ALU.mult), wait=[("SE", se_ro)], inc="VE", drain=True)
                prev["ve_o4"] = ve_o4
            wtp = [("VE", prev["ve_o4"]), ("VE", prev["ve_AT"])]
            for i in range(8):
                te = em.op("tensor", lambda i=i: nc.tensor.transpose(
                    PT[:, 128 * i:128 * (i + 1)], o4tm[:, 128 * i:128 * (i + 1)], ident_b[:, :]),
                    wait=wtp if i == 0 else (), inc="TE" if i == 7 else None)
            ve_of = em.op("vector", lambda ofmbase=ofmbase: nc.vector.tensor_copy(
                ofm2[:, ofmbase:ofmbase + 1024], PT[:, 0:1024]), wait=[("TE", te)], inc="VE")
            for i in range(4):
                te = em.op("tensor", lambda i=i: nc.tensor.transpose(
                    PT[:, 128 * i:128 * (i + 1)], o4tm[:, 1024 + 128 * i:1024 + 128 * (i + 1)],
                    ident_b[:, :]), wait=[("VE", ve_of)] if i == 0 else (),
                    inc="TE" if i == 3 else None)
            ve_of = em.op("vector", lambda ofmbase=ofmbase: nc.vector.tensor_copy(
                ofm2[:, ofmbase + 1024:ofmbase + 1536], PT[:, 0:512]), wait=[("TE", te)], inc="VE")
            prev["ve_ofm"] = ve_of
        PY = [P2a, P2b]
        te_y = [None, None]
        for kc in range(12):
            bq = kc % 2
            prev["dw_wso"][bq] = em.op("gpsimd", lambda kc=kc, bq=bq: nc.gpsimd.dma_start(
                out=wsob[bq][:, :], in_=st["dW_so"][128 * kc:128 * (kc + 1), :]),
                wait=[("TE", te_y[bq] if te_y[bq] is not None else prev["te_y"][bq])], inc="DW", n=16)
            for c2 in range(2):
                wmm = [("DW", prev["dw_wso"][bq])]
                if kc == 0:
                    wmm += [("VE", prev["ve_ofm"]), ("VE", ve_sq), ("TE", te_prod),
                            ("VE", prev["ve_Ub"]), ("TE", prev["te_state"]), ("TE", prev["te_O"]),
                            ("VE", prev["ve_state"])]
                for nh in range(2):
                    te_y[c2] = em.op("tensor", lambda kc=kc, c2=c2, nh=nh, bq=bq: nc.tensor.matmul(
                        PY[c2][:, 512 * nh:512 * (nh + 1)],
                        ofm2[:, 1536 * c2 + 128 * kc:1536 * c2 + 128 * (kc + 1)],
                        wsob[bq][:, 512 * nh:512 * (nh + 1)],
                        start=(kc == 0), stop=(kc == 11)),
                        wait=wmm if nh == 0 else (), inc="TE" if nh == 1 else None)
        prev["te_y"] = [em.v("TE"), em.v("TE")]
        ve_yf = 0
        for c2 in range(2):
            ve_yt = em.op("vector", lambda c2=c2: nc.vector.tensor_copy(ytmp[:, 0:1024], PY[c2][:, :]),
                          wait=[("TE", te_y[c2]), ("VE", prev["ve_ytmp"])], inc="VE")
            for i in range(8):
                te = em.op("tensor", lambda i=i: nc.tensor.transpose(
                    PT[:, 128 * i:128 * (i + 1)], ytmp[:, 128 * i:128 * (i + 1)], ident_b[:, :]),
                    wait=[("VE", ve_yt), ("VE", ve_yf)] if i == 0 else (),
                    inc="TE" if i == 7 else None)
            ve_yf = em.op("vector", lambda c2=c2: nc.vector.tensor_copy(
                yfm2[:, 1024 * c2:1024 * (c2 + 1)], PT[:, 0:1024]), wait=[("TE", te)], inc="VE")
            prev["ve_ytmp"] = ve_yf
        te_om = [None, None]
        for kc in range(16):
            bq = kc % 2
            prev["dw_wout"][bq] = em.op("gpsimd", lambda kc=kc, bq=bq: nc.gpsimd.dma_start(
                out=wout[bq][:, :], in_=st["dW_out"][128 * kc:128 * (kc + 1), :]),
                wait=[("TE", te_om[bq] if te_om[bq] is not None else prev["te_out"][bq])],
                inc="DW", n=16)
            for c2 in range(2):
                if kc < 8:
                    lhs = lambda kc=kc, c2=c2: yfm2[:, 1024 * c2 + 128 * kc:1024 * c2 + 128 * (kc + 1)]
                else:
                    lhs = lambda kc=kc, c2=c2: zseg[:, 256 * (kc - 8) + 128 * c2:
                                                    256 * (kc - 8) + 128 * c2 + 128]
                wmm = [("DW", prev["dw_wout"][bq])]
                if kc == 0:
                    wmm.append(("VE", ve_yf))
                if kc == 8 and c2 == 0:
                    wmm.append(("DS", prev["ds_zs"]))
                for nh in range(2):
                    te_om[c2] = em.op("tensor", lambda kc=kc, c2=c2, nh=nh, bq=bq, lhs=lhs: nc.tensor.matmul(
                        PY[c2][:, 512 * nh:512 * (nh + 1)], lhs(),
                        wout[bq][:, 512 * nh:512 * (nh + 1)],
                        start=(kc == 0), stop=(kc == 15)),
                        wait=wmm if nh == 0 else (), inc="TE" if nh == 1 else None)
        prev["te_out"] = [em.v("TE"), em.v("TE")]
        for c2 in range(2):
            ve_oadd = em.op("vector", lambda c2=c2, s=s: nc.vector.tensor_tensor(
                xacc[:, 1024 * (2 * s + c2):1024 * (2 * s + c2 + 1)], PY[c2][:, :],
                xacc[:, 1024 * (2 * s + c2):1024 * (2 * s + c2 + 1)], ALU.add),
                wait=[("TE", te_om[c2])], inc="VE")
        prev["ve_out"] = ve_oadd
    if "a2d" in st["taps"]:
        fin = st["em"].snap()
        pieces = [gam, ngam, bex, betf, colst, bC4, rq, q_tm, k_tm, v_tm, g_tm,
                  kf_s, qf_s, qbf_s, qb_tm, A4, AT4, X4, MT4, Ub, Wnegb, bVb, obf,
                  o4tm, ofm2, yfm2, Sa[0], Sb[0], S0bA, sgb]
        col = 0
        for i, tns in enumerate(pieces):
            w = tns.shape[1]
            em.op("gpsimd", lambda tns=tns, col=col, w=w: nc.gpsimd.dma_start(
                out=st["dbg"]["a2d"][0:tns.shape[0], col:col + w], in_=tns[:, :]),
                wait=[("VE", fin["VE"]), ("SE", fin["SE"]), ("TE", fin["TE"])] if i == 0 else (),
                inc="DW", n=16)
            col += w
        st["a2d_cols"] = col
    a2.close()
    a.close()



def kernel(**inputs):
    ins = prep_inputs(inputs)
    nc = build_nc(debug_taps=DEBUG_TAPS)
    in_maps = [dict(ins, x=np.ascontiguousarray(inputs["L0_attn_x"][b])) for b in range(8)]
    res = run_bass_kernel_spmd(nc, in_maps, core_ids=list(range(8)))
    return np.stack([res.results[b]["y"] for b in range(8)], axis=0)


def prep_inputs(inputs):
    P = {}
    P["W_in"] = np.ascontiguousarray(inputs["L0_attn_W_in"])
    P["convw"] = np.ascontiguousarray(np.concatenate(
        [inputs["L0_attn_conv_x_w"].reshape(D, 4), inputs["L0_attn_conv_z_w"].reshape(D, 4)], axis=1))
    P["convb"] = np.ascontiguousarray(np.stack(
        [inputs["L0_attn_conv_x_b"], inputs["L0_attn_conv_z_b"]], axis=1))
    P["Wqkvg"] = np.ascontiguousarray(np.concatenate(
        [inputs["L0_attn_W_q"], inputs["L0_attn_W_k"], inputs["L0_attn_W_v"],
         inputs["L0_attn_W_g"], inputs["L0_attn_W_gk"], inputs["L0_attn_W_b"]], axis=1))
    P["W_so"] = np.ascontiguousarray(inputs["L0_attn_W_so"])
    P["W_out"] = np.ascontiguousarray(inputs["L0_attn_W_out"])
    P["Alog"] = np.ascontiguousarray(inputs["L0_attn_A_log"].reshape(4, 1))
    P["dtbb"] = np.ascontiguousarray(np.concatenate(
        [inputs["L0_attn_dt_bias"], inputs["L0_attn_b_b"]]).reshape(8, 1))
    P["gnw"] = np.ascontiguousarray(inputs["L0_attn_g_norm_w"].reshape(1, DVD))
    P["rw"] = np.ascontiguousarray(inputs["L0_ffn_router_w"])
    P["corr"] = np.ascontiguousarray(inputs["L0_ffn_correction_bias"].reshape(1, NE))
    P["up_w"] = np.ascontiguousarray(inputs["L0_ffn_up_w"])
    P["down_w"] = np.ascontiguousarray(inputs["L0_ffn_down_w"])
    P["sh_up"] = np.ascontiguousarray(inputs["L0_ffn_shared_up_w"])
    P["sh_down"] = np.ascontiguousarray(inputs["L0_ffn_shared_down_w"])
    return P


# version 02
# speedup vs baseline: 1.3282x; 1.3282x over previous
import numpy as np
import concourse.bass as bass
import concourse.mybir as mybir
from concourse.bass_utils import run_bass_kernel_spmd

F32 = mybir.dt.float32
BF16 = mybir.dt.bfloat16
AF = mybir.ActivationFunctionType
ALU = mybir.AluOpType
AXX = mybir.AxisListType.X

T = 2048
D = 1024
NT = 16
KD = 8
H, DKD, DVD = 4, 192, 384
CH = 128
NSEG, SEG = 8, 256
NE = 64
NU = 68
NBUF = 6
GACC = 4
EPS_LN = 1e-5
EPS_L2 = 1e-6
EPS_GN = 1e-5

ENGS = ("tensor", "vector", "scalar", "gpsimd", "sync")


class EM:

    def __init__(self, nc):
        self.nc = nc
        self.q = {e: [] for e in ENGS}
        self.sem = {}
        self.cnt = {}
        self._cm = []

    def new_sem(self, name):
        cm = self.nc.semaphore(name)
        h = cm.__enter__()
        self._cm.append(cm)
        self.sem[name] = h
        self.cnt[name] = 0

    def op(self, eng, fn, wait=(), inc=None, n=1):
        val = None
        if inc is not None:
            self.cnt[inc] += n
            val = self.cnt[inc]
        wait = tuple((s, v) for (s, v) in wait if v)
        self.q[eng].append((fn, wait, inc, n))
        return val

    def v(self, name):
        return self.cnt[name]

    def snap(self):
        return dict(self.cnt)

    def close_sems(self):
        for cm in reversed(self._cm):
            cm.__exit__(None, None, None)

    def emit(self, block):
        for eng in ENGS:
            deco = getattr(block, eng)

            def make(eng):
                items = self.q[eng]
                sems = self.sem

                def body(handle):
                    for fn, waits, inc, n in items:
                        for (s, val) in waits:
                            handle.wait_ge(sems[s], val)
                        if fn is not None:
                            ins = fn()
                            if inc is not None:
                                ins.then_inc(sems[inc], n)
                return body

            deco(make(eng))


class Scope:
    def __init__(self, nc, side="left"):
        self.nc = nc
        self.side = side
        self._cms = []

    def sb(self, name, shape, dtype):
        cm = self.nc.sbuf_tensor(name, shape, dtype, side=self.side)
        t = cm.__enter__()
        self._cms.append(cm)
        return t

    def ps(self, name, shape, dtype=F32):
        cm = self.nc.psum_tensor(name, shape, dtype)
        t = cm.__enter__()
        self._cms.append(cm)
        return t

    def close(self):
        for cm in reversed(self._cms):
            cm.__exit__(None, None, None)
        self._cms = []


DEBUG_TAPS = ()
RUN_ATTN = True
RUN_MOE = True
MOE_NG = None
A2_ON = True
NSEG_RUN = NSEG
A1_LVL = 3


def build_nc(debug_taps=(), run_attn=None):
    run_attn = RUN_ATTN if run_attn is None else run_attn
    nc = bass.Bass("TRN2", target_bir_lowering=False)
    em = EM(nc)
    for s in ("DW", "DS", "TE", "VE", "SE", "GC", "DO"):
        em.new_sem(s)
    st = {"nc": nc, "em": em, "taps": set(debug_taps), "run_attn": run_attn}

    def din(name, shape):
        return nc.dram_tensor(name, shape, F32, kind="ExternalInput").ap()
    st["dx"] = din("x", [T, D])
    st["dW_in"] = din("W_in", [D, 2048])
    st["dconvw"] = din("convw", [D, 8])
    st["dconvb"] = din("convb", [D, 2])
    st["dWqkvg"] = din("Wqkvg", [D, 4616])
    st["dW_so"] = din("W_so", [1536, D])
    st["dW_out"] = din("W_out", [2048, D])
    st["dAlog"] = din("Alog", [4, 1])
    st["ddtbb"] = din("dtbb", [8, 1])
    st["dgnw"] = din("gnw", [1, DVD])
    st["drw"] = din("rw", [NE, D])
    st["dcorr"] = din("corr", [1, NE])
    st["dup"] = din("up_w", [NE, D, 512])
    st["ddn"] = din("down_w", [NE, 512, D])
    st["dshu"] = din("sh_up", [D, 2048])
    st["dshd"] = din("sh_down", [2048, D])
    st["dy"] = nc.dram_tensor("y", [T, D], F32, kind="ExternalOutput").ap()
    st["dbg"] = {}
    for name, shp in (("xn", [128, NT * 1024]), ("combT", [64, T]),
                      ("ffn", [128, NT * 1024]), ("xs", [128, NT * 1024]),
                      ("zs", [128, NT * 1024]), ("rs0", [128, 512]),
                      ("a2d", [128, 28672]), ("invd", [128, 2176])):
        if name in st["taps"]:
            st["dbg"][name] = nc.dram_tensor("dbg_" + name, shp, F32, kind="ExternalOutput").ap()

    g = Scope(nc)
    st["g"] = g
    st["xacc"] = g.sb("xacc", [128, NT * 1024], F32)
    st["ident_b"] = g.sb("ident_b", [128, 128], BF16)
    st["onesb"] = g.sb("onesb", [1, 128], BF16)
    st["epss"] = g.sb("epss", [128, 4], F32)
    xnfs = Scope(nc)
    st["xnf_scope"] = xnfs
    st["xnf"] = xnfs.sb("xnf", [128, NT * 1024], BF16)
    pm = Scope(nc)
    st["pm"] = pm
    st["combT"] = pm.sb("combT", [64, T], BF16)
    st["id64"] = pm.sb("id64", [64, 64], BF16)
    st["ones8"] = pm.sb("ones8", [128, 8], F32)
    st["corr_rep"] = pm.sb("corr_rep", [128, 64], F32)
    st["onesf"] = pm.sb("onesf", [1, 128], F32)
    st["zscr"] = nc.dram_tensor("zscr", [KD, 128, T], BF16, kind="Internal").ap()
    st["gscr"] = nc.dram_tensor("gscr", [4, 256], F32, kind="Internal").ap()

    build_consts(st)
    build_p0(st)
    if RUN_MOE:
        build_moe(st)
    if run_attn and RUN_MOE:
        build_attn(st)
    else:
        st["fin"] = em.snap()

    fin = st["fin"]
    em.op("sync", lambda: nc.sync.dma_start(
        out=st["dy"].rearrange("(t p) j -> p t j", p=128),
        in_=st["xacc"][:, :].rearrange("p (t j) -> p t j", j=1024)),
        wait=[("VE", fin["VE"]), ("SE", fin["SE"]), ("TE", fin["TE"])], inc="DO", n=16)
    em.op("sync", None, wait=[("DO", em.v("DO"))])

    with nc.Block() as block:
        em.emit(block)
    em.close_sems()
    return nc


def build_consts(st):
    nc, em, G = st["nc"], st["em"], st
    ds0 = em.op("sync", lambda: nc.sync.dma_start(out=G["corr_rep"][:1, :], in_=G["dcorr"][:, :]),
                inc="DS", n=16)
    st["ds_corr"] = ds0

    def asel(t, cmp, fill, cm, pat, sl=None):
        ap = t[:, :] if sl is None else t[:, sl[0]:sl[1]]
        return lambda: nc.gpsimd.affine_select(out=ap, in_=ap, compare_op=cmp,
                                               fill=fill, base=0, pattern=pat, channel_multiplier=cm)
    em.op("gpsimd", lambda: nc.gpsimd.memset(G["ident_b"][:, :], 0.0))
    em.op("gpsimd", asel(G["ident_b"], ALU.not_equal, 1.0, 1, [[-1, 128]]))
    em.op("gpsimd", lambda: nc.gpsimd.memset(G["id64"][:, :], 0.0))
    em.op("gpsimd", lambda: nc.gpsimd.affine_select(out=G["id64"][:, :], in_=G["id64"][:, :],
          compare_op=ALU.not_equal, fill=1.0, base=0, pattern=[[-1, 64]], channel_multiplier=1))
    em.op("gpsimd", lambda: nc.gpsimd.memset(G["onesb"][:, :], 1.0))
    em.op("gpsimd", lambda: nc.gpsimd.memset(G["onesf"][:, :], 1.0))
    em.op("gpsimd", lambda: nc.gpsimd.memset(G["ones8"][:, :], 1.0))
    em.op("gpsimd", lambda: nc.gpsimd.memset(G["epss"][:, 0:1], EPS_LN))
    em.op("gpsimd", lambda: nc.gpsimd.memset(G["epss"][:, 1:2], EPS_L2))
    gc = em.op("gpsimd", lambda: nc.gpsimd.memset(G["epss"][:, 2:3], -40.0), inc="GC")
    st["gc_consts"] = gc


def build_p0(st):
    nc, em, G = st["nc"], st["em"], st
    xacc = st["xacc"]
    xnf = st["xnf"]
    p0 = Scope(nc)
    ident_f = p0.sb("ident_f_p0", [128, 128], F32)
    xnf32 = p0.sb("xnf32", [128, NT * 1024], F32)
    xn_t = p0.sb("xn_t", [128, 1024], F32)
    stat = p0.sb("stat", [128, 8], F32)
    rwf = p0.sb("rwf", [64, 1024], F32)
    rwT = p0.sb("rwT", [128, 512], F32)
    rs = p0.sb("rs", [128, 512], F32)
    rA = p0.sb("rA", [128, 1], F32)
    rB = p0.sb("rB", [128, 1], F32)
    cbf = p0.sb("cbf", [128, 64], BF16)
    pt0 = p0.ps("pt0", [128, 512], F32)
    pt1 = p0.ps("pt1", [128, 512], F32)
    pr0 = p0.ps("pr0", [128, 512], F32)
    pr1 = p0.ps("pr1", [128, 512], F32)
    pcb = p0.ps("pcb", [128, 128], BF16)

    em.op("gpsimd", lambda: nc.gpsimd.memset(ident_f[:, :], 0.0))
    gcf = em.op("gpsimd", lambda: nc.gpsimd.affine_select(
        out=ident_f[:, :], in_=ident_f[:, :], compare_op=ALU.not_equal,
        fill=1.0, base=0, pattern=[[-1, 128]], channel_multiplier=1), inc="GC")

    ld_x = em.op("sync", lambda: nc.sync.dma_start(
        out=xacc[:, :].rearrange("p (t j) -> p t j", j=1024),
        in_=st["dx"].rearrange("(t p) j -> p t j", p=128)), inc="DS", n=16)
    ld_rw = em.op("sync", lambda: nc.sync.dma_start(out=rwf[:, :], in_=st["drw"][:, :]), inc="DS", n=16)

    for c in range(KD):
        te = em.op("tensor", lambda c=c: nc.tensor.transpose(
            pt0[:, 64 * c:64 * (c + 1)], rwf[:, 128 * c:128 * (c + 1)], ident_f[:64, :64]),
            wait=([("DS", ld_rw), ("GC", gcf)] if c == 0 else ()), inc="TE" if c == 7 else None)
    ve_rwT = em.op("vector", lambda: nc.vector.tensor_copy(rwT[:, :], pt0[:, :]),
                   wait=[("TE", te)], inc="VE")
    te_cr = em.op("tensor", lambda: nc.tensor.matmul(
        pt1[:, 0:64], st["onesf"][:1, :], st["corr_rep"][:1, :], start=True, stop=True),
        wait=[("DS", st["ds_corr"]), ("GC", st["gc_consts"])], inc="TE")
    ve_corr = em.op("vector", lambda: nc.vector.tensor_copy(st["corr_rep"][:, :], pt1[:, 0:64]),
                    wait=[("TE", te_cr)], inc="VE")
    st["ve_corr"] = ve_corr

    S0, SF, SW, SG, SGW, SGM, SEM_, SSEL, SWS, SX = 0, 64, 128, 192, 200, 208, 216, 280, 344, 408
    te_tr2_prev = 0
    se_prev = 0
    se_sig_bank = [0, 0]
    v6_prev = 0
    tec_prev = 0
    vec_prev = 0
    ve_comb = [0] * NT
    for t in range(NT):
        xt = xacc[:, 1024 * t:1024 * (t + 1)]
        em.op("vector", lambda xt=xt: nc.vector.reduce_sum(stat[:, 0:1], xt, axis=AXX),
              wait=[("DS", ld_x), ("SE", se_prev)] if t == 0 else [("SE", se_prev)])
        em.op("vector", lambda xt=xt: nc.vector.tensor_tensor(xn_t[:, :], xt, xt, ALU.mult),
              wait=[("TE", te_tr2_prev)])
        v1 = em.op("vector", lambda: nc.vector.reduce_sum(stat[:, 2:3], xn_t[:, :], axis=AXX), inc="VE")
        em.op("scalar", lambda: nc.scalar.activation(stat[:, 1:2], stat[:, 0:1], AF.Copy, scale=-1.0 / D),
              wait=[("VE", v1)])
        s1 = em.op("scalar", lambda: nc.scalar.activation(stat[:, 3:4], stat[:, 0:1], AF.Square, scale=1.0 / D),
                   inc="SE")
        v2 = em.op("vector", lambda: nc.vector.scalar_tensor_tensor(
            stat[:, 4:5], stat[:, 2:3], 1.0 / D, stat[:, 3:4], ALU.mult, ALU.subtract),
            wait=[("SE", s1)], inc="VE")
        em.op("scalar", lambda: nc.scalar.activation(stat[:, 5:6], stat[:, 4:5], AF.Ln,
                                                     bias=st["epss"][:, 0:1]),
              wait=[("VE", v2)] + ([("GC", st["gc_consts"])] if t == 0 else []))
        s2 = em.op("scalar", lambda: nc.scalar.activation(stat[:, 6:7], stat[:, 5:6], AF.Exp, scale=-0.5),
                   inc="SE")
        se_prev = s2
        v3 = em.op("vector", lambda xt=xt: nc.vector.tensor_scalar(
            xn_t[:, :], xt, stat[:, 1:2], stat[:, 6:7], ALU.add, ALU.mult),
            wait=[("SE", s2)], inc="VE")
        pt = pt0 if t % 2 == 0 else pt1
        wtr = [("VE", v3)]
        if t == 0:
            wtr.append(("VE", ve_rwT))
        if t == 1:
            wtr.append(("VE", st["ve_corr"]))
        for c in range(4):
            te1 = em.op("tensor", lambda c=c, pt=pt: nc.tensor.transpose(
                pt[:, 128 * c:128 * (c + 1)], xn_t[:, 128 * c:128 * (c + 1)], ident_f[:, :]),
                wait=wtr if c == 0 else (), inc="TE" if c == 3 else None)
        for c in range(4):
            v4 = em.op("vector", lambda c=c, t=t, pt=pt: nc.vector.tensor_copy(
                xnf32[:, 2048 * c + 128 * t:2048 * c + 128 * (t + 1)], pt[:, 128 * c:128 * (c + 1)]),
                wait=[("TE", te1)] if c == 0 else (), inc="VE" if c == 3 else None)
        for c in range(4, 8):
            te2 = em.op("tensor", lambda c=c, pt=pt: nc.tensor.transpose(
                pt[:, 128 * (c - 4):128 * (c - 3)], xn_t[:, 128 * c:128 * (c + 1)], ident_f[:, :]),
                wait=[("VE", v4)] if c == 4 else (), inc="TE" if c == 7 else None)
        te_tr2_prev = te2
        for c in range(4, 8):
            v5 = em.op("vector", lambda c=c, t=t, pt=pt: nc.vector.tensor_copy(
                xnf32[:, 2048 * c + 128 * t:2048 * c + 128 * (t + 1)], pt[:, 128 * (c - 4):128 * (c - 3)]),
                wait=[("TE", te2)] if c == 4 else (), inc="VE" if c == 7 else None)
        pr = pr0 if t % 2 == 0 else pr1
        for c in range(KD):
            ter = em.op("tensor", lambda c=c, t=t, pr=pr: nc.tensor.matmul(
                pr[:, 0:64],
                xnf32[:, 2048 * c + 128 * t:2048 * c + 128 * (t + 1)],
                rwT[:, 64 * c:64 * (c + 1)],
                start=(c == 0), stop=(c == 7)),
                wait=[("VE", v5), ("VE", ve_rwT), ("SE", se_sig_bank[t % 2])] if c == 0 else (),
                inc="TE" if c == 7 else None)
        se_sc = em.op("scalar", lambda pr=pr: nc.scalar.activation(
            rs[:, S0:S0 + 64], pr[:, 0:64], AF.Sigmoid),
            wait=[("TE", ter), ("VE", v6_prev)], inc="SE")
        se_sig_bank[t % 2] = se_sc
        def vop(fn, wait=(), inc=None):
            em.op("vector", lambda: nc.vector.drain())
            return em.op("vector", fn, wait=wait, inc=inc)

        def eq_of(src_ap, width, dst_off):
            vop(lambda: nc.vector.tensor_scalar(
                rs[:, dst_off:dst_off + width], src_ap, rA[:, 0:1], None, ALU.subtract))
            vop(lambda: nc.vector.tensor_scalar(
                rs[:, dst_off:dst_off + width], rs[:, dst_off:dst_off + width],
                1e12, 1.0, ALU.mult, ALU.add))
            vop(lambda: nc.vector.tensor_relu(
                rs[:, dst_off:dst_off + width], rs[:, dst_off:dst_off + width]))

        vop(lambda: nc.vector.tensor_tensor(rs[:, SF:SF + 64], rs[:, S0:S0 + 64], st["corr_rep"][:, :], ALU.add),
            wait=[("SE", se_sc), ("VE", st["ve_corr"])])
        for gi in range(8):
            sfc_g = rs[:, SF + 8 * gi:SF + 8 * (gi + 1)]
            vop(lambda sfc_g=sfc_g: nc.vector.reduce_max(rA[:, 0:1], sfc_g, axis=AXX))
            eq_of(sfc_g, 8, SX + 2)
            vop(lambda sfc_g=sfc_g: nc.vector.scalar_tensor_tensor(
                rs[:, SX + 16:SX + 24], rs[:, SX + 2:SX + 10], -1e30, sfc_g, ALU.mult, ALU.add))
            vop(lambda: nc.vector.reduce_max(rB[:, 0:1], rs[:, SX + 16:SX + 24], axis=AXX))
            vop(lambda gi=gi: nc.vector.tensor_tensor(
                rs[:, SG + gi:SG + gi + 1], rA[:, 0:1], rB[:, 0:1], ALU.add))
        vop(lambda: nc.vector.tensor_copy(rs[:, SGW:SGW + 8], rs[:, SG:SG + 8]))
        vop(lambda: nc.vector.memset(rs[:, SGM:SGM + 8], 0.0))
        for _ in range(4):
            vop(lambda: nc.vector.reduce_max(rA[:, 0:1], rs[:, SGW:SGW + 8], axis=AXX))
            eq_of(rs[:, SGW:SGW + 8], 8, SX + 2)
            vop(lambda: nc.vector.tensor_tensor(
                rs[:, SGM:SGM + 8], rs[:, SGM:SGM + 8], rs[:, SX + 2:SX + 10], ALU.add))
            vop(lambda: nc.vector.scalar_tensor_tensor(
                rs[:, SGW:SGW + 8], rs[:, SX + 2:SX + 10], -1e30, rs[:, SGW:SGW + 8], ALU.mult, ALU.add))
        for gi in range(8):
            vop(lambda gi=gi: nc.vector.tensor_scalar(
                rs[:, SEM_ + 8 * gi:SEM_ + 8 * (gi + 1)], st["ones8"][:, :],
                rs[:, SGM + gi:SGM + gi + 1], None, ALU.mult))
        vop(lambda: nc.vector.tensor_tensor(rs[:, SW:SW + 64], rs[:, SF:SF + 64], rs[:, SEM_:SEM_ + 64], ALU.mult))
        vop(lambda: nc.vector.memset(rs[:, SSEL:SSEL + 64], 0.0))
        for _ in range(8):
            vop(lambda: nc.vector.reduce_max(rA[:, 0:1], rs[:, SW:SW + 64], axis=AXX))
            eq_of(rs[:, SW:SW + 64], 64, SX + 2)
            vop(lambda: nc.vector.tensor_tensor(
                rs[:, SSEL:SSEL + 64], rs[:, SSEL:SSEL + 64], rs[:, SX + 2:SX + 66], ALU.add))
            vop(lambda: nc.vector.scalar_tensor_tensor(
                rs[:, SW:SW + 64], rs[:, SX + 2:SX + 66], -1e30, rs[:, SW:SW + 64], ALU.mult, ALU.add))
        vop(lambda: nc.vector.tensor_tensor(rs[:, SWS:SWS + 64], rs[:, SSEL:SSEL + 64], rs[:, S0:S0 + 64], ALU.mult))
        vop(lambda: nc.vector.reduce_sum(rA[:, 0:1], rs[:, SWS:SWS + 64], axis=AXX))
        vop(lambda: nc.vector.tensor_scalar(rA[:, 0:1], rA[:, 0:1], 1e-20, None, ALU.add))
        vop(lambda: nc.vector.reciprocal(rB[:, 0:1], rA[:, 0:1]))
        v6 = vop(lambda: nc.vector.tensor_scalar(
            cbf[:, :], rs[:, SWS:SWS + 64], rB[:, 0:1], 2.5, ALU.mult, ALU.mult),
            wait=[("TE", tec_prev)], inc="VE")
        v6_prev = v6
        tec = em.op("tensor", lambda: nc.tensor.transpose(pcb[:64, :], cbf[:, :], st["ident_b"][:, :]),
                    wait=[("VE", v6), ("VE", vec_prev)], inc="TE")
        tec_prev = tec
        vec = em.op("vector", lambda t=t: nc.vector.tensor_copy(
            st["combT"][:, 128 * t:128 * (t + 1)], pcb[:64, :]), wait=[("TE", tec)], inc="VE")
        vec_prev = vec
        ve_comb[t] = vec
        if t == 0 and "rs0" in st["taps"]:
            em.op("gpsimd", lambda: nc.gpsimd.dma_start(out=st["dbg"]["rs0"], in_=rs[:, :]),
                  wait=[("VE", v6)], inc="DW", n=16)

    ve_xnf = em.op("vector", lambda: nc.vector.tensor_copy(xnf[:, :], xnf32[:, :]), inc="VE")
    st["ve_xnf"] = ve_xnf
    st["ve_comb"] = ve_comb
    st["te_p0_end"] = em.v("TE")
    st["ve_p0_end"] = em.v("VE")

    if "xn" in st["taps"]:
        em.op("gpsimd", lambda: nc.gpsimd.dma_start(out=st["dbg"]["xn"], in_=xnf[:, :]),
              wait=[("VE", ve_xnf)], inc="DW", n=16)
    if "combT" in st["taps"]:
        em.op("gpsimd", lambda: nc.gpsimd.dma_start(out=st["dbg"]["combT"], in_=st["combT"][:, :]),
              wait=[("VE", ve_comb[-1])], inc="DW", n=16)
    p0.close()


def build_moe(st):
    nc, em, G = st["nc"], st["em"], st
    xacc, xnf, combT, id64 = st["xacc"], st["xnf"], st["combT"], st["id64"]
    m = Scope(nc)
    wu = [m.sb("wu%d" % i, [128, 4096], BF16) for i in range(NBUF)]
    wd = [m.sb("wd%d" % i, [128, 4096], BF16) for i in range(NBUF)]
    tb = m.sb("tb", [128, 1024], BF16)
    sqb = m.sb("sqb", [128, 1024], BF16)
    h2cb = [m.sb("h2cb%d" % i, [128, 1024], BF16) for i in range(2)]
    crep_sb = m.sb("crep_sb", [128, 256], BF16)
    ph = m.ps("ph", [128, 1024], F32)
    pd = [m.ps("pd%d" % i, [128, 1024], F32) for i in range(2)]
    pcrep = [m.ps("pcrep%d" % i, [128, 512], F32) for i in range(2)]

    def up_src(u):
        if u < 4:
            return G["dshu"][:, 512 * u:512 * (u + 1)].rearrange("(c p) w -> p c w", p=128)
        return G["dup"][u - 4].rearrange("(c p) w -> p c w", p=128)

    def dn_src(u):
        if u < 4:
            return G["dshd"][512 * u:512 * (u + 1), :].rearrange("(c p) w -> p c w", p=128)
        return G["ddn"][u - 4].rearrange("(c p) w -> p c w", p=128)

    dw_done = [None] * (NU + NBUF)
    te_dn_done = [0] * (NU + NBUF)

    def emit_load(u):
        b = u % NBUF
        w = ([("TE", st["te_p0_end"]), ("VE", st["ve_p0_end"])] if u < NBUF
             else [("TE", te_dn_done[u - NBUF])])
        em.op("gpsimd", lambda u=u, b=b: nc.gpsimd.dma_start(
            out=wu[b][:, :].rearrange("p (c w) -> p c w", w=512), in_=up_src(u)),
            wait=w, inc="DW", n=16)
        dw_done[u] = em.op("gpsimd", lambda u=u, b=b: nc.gpsimd.dma_start(
            out=wd[b][:, :].rearrange("p (c w) -> p c w", w=1024), in_=dn_src(u)),
            inc="DW", n=16)

    for u in range(NBUF):
        emit_load(u)

    n_groups = NU // GACC if MOE_NG is None else MOE_NG
    ve_evacadd = 0
    se_relu_prev = 0
    ve_h2c_prev = 0
    ve_crep_prev = [0, 0]
    loads_emitted = NBUF
    pending_down = [None]

    def emit_down(u, r, b, first, last, dn_wait):
        nonlocal ve_evacadd
        wdn = [dn_wait]
        if first:
            wdn.append(("VE", ve_evacadd))
        if u == 0 and r == 0:
            wdn.append(("VE", st["ve_p0_end"]))
        te_dn = None
        for t2 in range(2):
            for ich in range(4):
                for nh in range(2):
                    te_dn = em.op("tensor", lambda t2=t2, ich=ich, nh=nh, u=u, b=b, first=first, last=last: nc.tensor.matmul(
                        pd[t2][:, 512 * nh:512 * (nh + 1)],
                        h2cb[u % 2][:, 256 * ich + 128 * t2:256 * ich + 128 * (t2 + 1)],
                        wd[b][:, 1024 * ich + 512 * nh:1024 * ich + 512 * (nh + 1)],
                        start=(first and ich == 0), stop=(last and ich == 3)),
                        wait=wdn if (t2 == 0 and ich == 0 and nh == 0) else (),
                        inc="TE" if (t2 == 1 and ich == 3 and nh == 1) else None)
        te_dn_done[u] = te_dn
        if last:
            for t2 in range(2):
                tile = 2 * r + t2
                ve_evacadd = em.op("vector", lambda t2=t2, tile=tile: nc.vector.tensor_tensor(
                    xacc[:, 1024 * tile:1024 * (tile + 1)],
                    pd[t2][:, :], xacc[:, 1024 * tile:1024 * (tile + 1)], ALU.add),
                    wait=[("TE", te_dn)], inc="VE")

    for grp in range(n_groups):
        for r in range(8):
            for gi in range(GACC):
                u = grp * GACC + gi
                e = u - 4
                b = u % NBUF
                first = gi == 0
                last = gi == GACC - 1
                te_crep = None
                ve_crep = None
                if e >= 0:
                    wcr = [("VE", st["ve_comb"][2 * r + 1]), ("VE", ve_crep_prev[u % 2])]
                    if u in (4, 5) and r == 0:
                        wcr.append(("VE", st["ve_p0_end"]))
                    te_crep = em.op("tensor", lambda e=e, r=r, u=u: nc.tensor.matmul(
                        pcrep[u % 2][:, 0:256], id64[:, e:e + 1].broadcast_to([64, 128]),
                        combT[:, 256 * r:256 * (r + 1)], start=True, stop=True),
                        wait=wcr, inc="TE")
                    ve_crep = em.op("vector", lambda u=u: nc.vector.tensor_copy(
                        crep_sb[:, :], pcrep[u % 2][:, 0:256]), wait=[("TE", te_crep)], inc="VE")
                    ve_crep_prev[u % 2] = ve_crep
                wup = [("DW", dw_done[u]), ("SE", se_relu_prev)]
                if u == 0 and r == 0:
                    wup.append(("VE", st["ve_xnf"]))
                te_up = None
                for mi in range(4):
                    for kc in range(KD):
                        te_up = em.op("tensor", lambda mi=mi, kc=kc, b=b, r=r: nc.tensor.matmul(
                            ph[:, 256 * mi:256 * (mi + 1)],
                            wu[b][:, 512 * kc + 128 * mi:512 * kc + 128 * (mi + 1)],
                            xnf[:, 2048 * kc + 256 * r:2048 * kc + 256 * (r + 1)],
                            start=(kc == 0), stop=(kc == KD - 1)),
                            wait=wup if (mi == 0 and kc == 0) else (),
                            inc="TE" if (mi == 3 and kc == KD - 1) else None)
                se_relu = em.op("scalar", lambda: nc.scalar.activation(
                    tb[:, :], ph[:, :], AF.Relu), wait=[("TE", te_up)], inc="SE")
                se_relu_prev = se_relu
                wsq = [("TE", te_dn_done[u - 2] if u >= 2 else 0), ("VE", ve_h2c_prev)]
                if e >= 0:
                    se_sq = em.op("scalar", lambda: nc.scalar.activation(
                        sqb[:, :], tb[:, :], AF.Square), wait=wsq, inc="SE")
                    em.op("vector", lambda: nc.vector.drain())
                    for mi in range(4):
                        ve_h2c = em.op("vector", lambda u=u, mi=mi: nc.vector.tensor_tensor(
                            h2cb[u % 2][:, 256 * mi:256 * (mi + 1)],
                            sqb[:, 256 * mi:256 * (mi + 1)], crep_sb[:, :], ALU.mult),
                            wait=([("SE", se_sq), ("VE", ve_crep),
                                   ("TE", te_dn_done[u - 2] if u >= 2 else 0)] if mi == 0 else ()),
                            inc="VE" if mi == 3 else None)
                    dn_wait = ("VE", ve_h2c)
                    ve_h2c_prev = ve_h2c
                else:
                    se_sq = em.op("scalar", lambda u=u: nc.scalar.activation(
                        h2cb[u % 2][:, :], tb[:, :], AF.Square), wait=wsq, inc="SE")
                    dn_wait = ("SE", se_sq)
                if pending_down[0] is not None:
                    emit_down(*pending_down[0])
                pending_down[0] = (u, r, b, first, last, dn_wait)
        for _ in range(GACC):
            if loads_emitted < NU:
                emit_load(loads_emitted)
                loads_emitted += 1
    if pending_down[0] is not None:
        emit_down(*pending_down[0])
        pending_down[0] = None

    st["moe_fin"] = em.snap()
    if "ffn" in st["taps"]:
        em.op("gpsimd", lambda: nc.gpsimd.dma_start(out=st["dbg"]["ffn"], in_=xacc[:, :]),
              wait=[("VE", em.v("VE"))], inc="DW", n=16)
    m.close()
    st["pm"].close()


def head_kslices(h):
    return [(256 * h, 128, 0), (256 * h + 128, 64, 128)]


def build_attn(st):
    nc, em, G = st["nc"], st["em"], st
    xacc, xnf = st["xacc"], st["xnf"]
    moe_fin = st["moe_fin"]
    if A1_LVL < 0:
        st["xnf_scope"].close()
        st["fin"] = em.snap()
        return

    a = Scope(nc, side="right")
    xs = a.sb("xs", [128, NT * 1024], BF16)
    zseg = a.sb("zseg", [128, 2048], BF16)
    identf = a.sb("identf2", [128, 128], F32)
    lms01 = a.sb("lms01", [128, 128], F32)
    umi01 = a.sb("umi01", [128, 128], F32)
    bdm4 = a.sb("bdm4", [128, 512], BF16)
    i4b = a.sb("i4b", [128, 512], BF16)
    grw = a.sb("grw", [128, DVD], BF16)
    convw_sb = a.sb("convw_sb", [128, 64], F32)
    convb_sb = a.sb("convb_sb", [128, 16], F32)
    nA = a.sb("nA_sb", [4, 1], F32)
    biasv = a.sb("biasv_sb", [4, 1], F32)
    bbv = a.sb("bbv_sb", [4, 1], F32)
    gnw = a.sb("gnw_sb", [1, DVD], BF16)

    def gp(fn, wait=(), inc=None):
        return em.op("gpsimd", fn, wait=wait, inc=inc)

    def asel(ap, cmp, fill, cm, pat):
        return lambda: nc.gpsimd.affine_select(out=ap, in_=ap, compare_op=cmp,
                                               fill=fill, base=0, pattern=pat, channel_multiplier=cm)
    gp(lambda: nc.gpsimd.memset(identf[:, :], 0.0),
       wait=[("TE", moe_fin["TE"]), ("VE", moe_fin["VE"]), ("SE", moe_fin["SE"])])
    gp(asel(identf[:, :], ALU.not_equal, 1.0, 1, [[-1, 128]]))
    gp(lambda: nc.gpsimd.memset(lms01[:, :], 1.0))
    gp(asel(lms01[:, :], ALU.is_gt, 0.0, 1, [[-1, 128]]))
    gp(lambda: nc.gpsimd.memset(umi01[:, :], 1.0))
    gp(asel(umi01[:, :], ALU.is_ge, 0.0, -1, [[1, 128]]))
    for cp in range(4):
        for blk in range(4):
            sl_ = bdm4[:, 128 * cp + 32 * blk:128 * cp + 32 * (blk + 1)]
            gp(lambda sl_=sl_: nc.gpsimd.memset(sl_, 1.0))
            gp(lambda sl_=sl_, blk=blk: nc.gpsimd.affine_select(
                out=sl_, in_=sl_, compare_op=ALU.is_ge, fill=0.0,
                base=-32 * blk, pattern=[[0, 32]], channel_multiplier=1))
            gp(lambda sl_=sl_, blk=blk: nc.gpsimd.affine_select(
                out=sl_, in_=sl_, compare_op=ALU.is_ge, fill=0.0,
                base=32 * blk + 31, pattern=[[0, 32]], channel_multiplier=-1))
    for i in range(4):
        gp(lambda i=i: nc.gpsimd.memset(i4b[:, 128 * i:128 * (i + 1)], 0.0))
        gp(asel(i4b[:, 0:512][:, 128 * i:128 * (i + 1)], ALU.not_equal, 1.0, 1, [[-1, 128]]))
    gc_att = em.op("gpsimd", lambda: nc.gpsimd.memset(nA[:, :], 0.0), inc="GC")
    st["gc_att"] = gc_att

    wfence = [("TE", moe_fin["TE"]), ("VE", moe_fin["VE"]), ("SE", moe_fin["SE"])]
    ds = em.op("sync", lambda: nc.sync.dma_start(
        out=convw_sb[:, :].rearrange("p (c w) -> p c w", w=8),
        in_=st["dconvw"].rearrange("(c p) w -> p c w", p=128)), wait=wfence, inc="DS", n=16)
    ds = em.op("sync", lambda: nc.sync.dma_start(
        out=convb_sb[:, :].rearrange("p (c w) -> p c w", w=2),
        in_=st["dconvb"].rearrange("(c p) w -> p c w", p=128)), inc="DS", n=16)
    st["ds_conv"] = ds
    ds = em.op("sync", lambda: nc.sync.dma_start(out=nA[:, :], in_=st["dAlog"][:, :]),
               wait=[("GC", gc_att)], inc="DS", n=16)
    ds = em.op("sync", lambda: nc.sync.dma_start(out=biasv[:, :], in_=st["ddtbb"][0:4, :]), inc="DS", n=16)
    ds_sync_small = em.op("sync", lambda: nc.sync.dma_start(out=bbv[:, :], in_=st["ddtbb"][4:8, :]), inc="DS", n=16)
    ds_small = em.op("gpsimd", lambda: nc.gpsimd.dma_start(out=gnw[:1, :], in_=st["dgnw"][:, :]),
                     inc="DW", n=16)
    em.op("scalar", lambda: nc.scalar.activation(nA[:, :], nA[:, :], AF.Exp),
          wait=[("DS", ds_sync_small)])
    se_nA = em.op("scalar", lambda: nc.scalar.mul(nA[:, :], nA[:, :], -1.0), inc="SE")
    st["se_nA"] = se_nA

    build_a1(st, a, locals())
    if A2_ON:
        build_a2(st, a, locals())
    else:
        a.close()
    st["fin"] = em.snap()
    if "attn" in st["taps"]:
        em.op("gpsimd", lambda: nc.gpsimd.dma_start(out=st["dbg"]["attn"], in_=xacc[:, :]),
              wait=[("VE", em.v("VE"))], inc="DW", n=16)


def build_a1(st, a, env):
    nc, em = st["nc"], st["em"]
    xnf = st["xnf"]
    xs = env["xs"]
    convw_sb, convb_sb = env["convw_sb"], env["convb_sb"]
    moe_fin = st["moe_fin"]
    a1 = Scope(nc, side="right")
    zch = [a1.sb("zch%d" % i, [128, 2048], BF16) for i in range(2)]
    winb = [a1.sb("winb%d" % i, [128, 1024], BF16) for i in range(2)]
    xzst = [a1.sb("xzst%d" % i, [128, 2048], F32) for i in range(2)]
    cvacc = a1.sb("cvacc", [128, 2048], F32)
    pxz = [a1.ps("pxz%d" % i, [128, 2048], F32) for i in range(2)]

    dw_win = [0] * NT
    te_mm = [0] * NT
    se_evac = [0] * NT
    ve_taps = [0] * NT
    se_silu = [0] * NT
    ds_conv = st["ds_conv"]

    for m in range(NT if A1_LVL >= 1 else 0):
        b = m % 2
        w = [("TE", moe_fin["TE"]), ("VE", moe_fin["VE"])] if m < 2 else [("TE", te_mm[m - 2])]
        dw_win[m] = em.op("gpsimd", lambda m=m, b=b: nc.gpsimd.dma_start(
            out=winb[b][:, :].rearrange("p (c w) -> p c w", w=128),
            in_=st["dW_in"][:, 128 * m:128 * (m + 1)].rearrange("(c p) w -> p c w", p=128)),
            wait=w, inc="DW", n=16)
        wmm = [("DW", dw_win[m]), ("SE", se_evac[m - 2] if m >= 2 else 0)]
        if m < 2:
            wmm += [("VE", moe_fin["VE"]), ("SE", moe_fin["SE"])]
        for n in range(4):
            for kc in range(KD):
                te = em.op("tensor", lambda m=m, b=b, n=n, kc=kc: nc.tensor.matmul(
                    pxz[b][:, 512 * n:512 * (n + 1)],
                    winb[b][:, 128 * kc:128 * (kc + 1)],
                    xnf[:, 2048 * kc + 512 * n:2048 * kc + 512 * (n + 1)],
                    start=(kc == 0), stop=(kc == KD - 1)),
                    wait=wmm if (n == 0 and kc == 0) else (),
                    inc="TE" if (n == 3 and kc == KD - 1) else None)
        te_mm[m] = te
        se_evac[m] = em.op("scalar", lambda b=b: nc.scalar.activation(
            xzst[b][:, :], pxz[b][:, :], AF.Copy),
            wait=[("TE", te), ("VE", ve_taps[m - 2] if m >= 2 else 0)], inc="SE")
        if A1_LVL < 2:
            ve_taps[m] = em.op("vector", lambda b=b: nc.vector.tensor_copy(
                cvacc[:, 0:8], xzst[b][:, 0:8]), wait=[("SE", se_evac[m])], inc="VE")
            se_silu[m] = se_evac[m]
            continue
        cw = m % 8
        wofs = 32 * (m // 8)
        col = lambda j, m=m: convw_sb[:, 8 * (m % 8) + (0 if m < 8 else 4) + j:
                                      8 * (m % 8) + (0 if m < 8 else 4) + j + 1]
        em.op("vector", lambda b=b, col=col: nc.vector.tensor_scalar(
            cvacc[:, :], xzst[b][:, :], col(1), None, ALU.mult),
            wait=[("SE", se_evac[m]), ("DS", ds_conv), ("SE", env["se_nA"])])
        em.op("vector", lambda: nc.vector.drain())
        em.op("vector", lambda b=b, col=col: nc.vector.scalar_tensor_tensor(
            cvacc[:, 1:2048], xzst[b][:, 0:2047], col(0), cvacc[:, 1:2048], ALU.mult, ALU.add))
        em.op("vector", lambda: nc.vector.drain())
        em.op("vector", lambda b=b, col=col: nc.vector.scalar_tensor_tensor(
            cvacc[:, 0:2047], xzst[b][:, 1:2048], col(2), cvacc[:, 0:2047], ALU.mult, ALU.add))
        em.op("vector", lambda: nc.vector.drain())
        ve_taps[m] = em.op("vector", lambda b=b, col=col: nc.vector.scalar_tensor_tensor(
            cvacc[:, 0:2046], xzst[b][:, 2:2048], col(3), cvacc[:, 0:2046], ALU.mult, ALU.add),
            inc="VE")
        bcol = convb_sb[:, 2 * (m % 8) + (0 if m < 8 else 1):2 * (m % 8) + (0 if m < 8 else 1) + 1]
        if m < 8:
            se_silu[m] = em.op("scalar", lambda m=m, bcol=bcol: nc.scalar.activation(
                xs[:, 2048 * m:2048 * (m + 1)], cvacc[:, :], AF.Silu, bias=bcol),
                wait=[("VE", ve_taps[m])], inc="SE")
        else:
            zb = m % 2
            se_silu[m] = em.op("scalar", lambda zb=zb, bcol=bcol: nc.scalar.activation(
                zch[zb][:, :], cvacc[:, :], AF.Silu, bias=bcol),
                wait=[("VE", ve_taps[m]), ("DS", st.get("ds_zch_%d" % (m - 2), 0))], inc="SE")
            if A1_LVL >= 3:
                st["ds_zch_%d" % m] = em.op("sync", lambda m=m, zb=zb: nc.sync.dma_start(
                    out=st["zscr"][m - 8], in_=zch[zb][:, :]),
                    wait=[("SE", se_silu[m])], inc="DS", n=16)
    st["a1_se_last"] = em.v("SE")
    st["a1_te_last"] = em.v("TE")
    st["a1_ve_last"] = em.v("VE")
    st["a1_ds_last"] = em.v("DS")

    if "xs" in st["taps"]:
        em.op("gpsimd", lambda: nc.gpsimd.dma_start(out=st["dbg"]["xs"], in_=xs[:, :]),
              wait=[("SE", st["a1_se_last"])], inc="DW", n=16)
    if "zs" in st["taps"]:
        em.op("gpsimd", lambda: nc.gpsimd.dma_start(
            out=st["dbg"]["zs"].rearrange("p (c w) -> c p w", w=2048), in_=st["zscr"][:, :, :]),
              wait=[("DS", st["a1_ds_last"])], inc="DW", n=16)
    a1.close()
    st["xnf_scope"].close()


def build_a2(st, a, env):
    nc, em = st["nc"], st["em"]
    xacc = st["xacc"]
    xs, zseg = env["xs"], env["zseg"]
    identf, i4b = env["identf"], env["i4b"]
    lms01, umi01, bdm4 = env["lms01"], env["umi01"], env["bdm4"]
    grw, nA, biasv, gnw = env["grw"], env["nA"], env["biasv"], env["gnw"]
    bbv = env["bbv"]
    ident_b = st["ident_b"]

    a2 = Scope(nc, side="right")
    wqs = [a2.sb("wqs%d" % i, [128, 2048], BF16) for i in range(2)]
    idbg = None
    q_tm = a2.sb("q_tm", [128, 1536], BF16)
    k_tm = a2.sb("k_tm", [128, 1536], BF16)
    v_tm = a2.sb("v_tm", [128, 3072], BF16)
    g_tm = a2.sb("g_tm", [128, 3072], BF16)

    betf = a2.sb("betf", [4, 256], F32)
    gam = a2.sb("gam", [4, 256], F32)
    ngam = a2.sb("ngam", [4, 256], F32)
    bex = a2.sb("bex", [4, 256], F32)
    spt = a2.sb("spt", [4, 256], F32)
    bb4c = a2.sb("bb4c", [4, 128], F32)
    dec4c = a2.sb("dec4c", [4, 128], F32)
    onesfA = a2.sb("onesfA", [1, 512], F32)
    grow = a2.sb("grow", [1, 512], F32)
    colst = a2.sb("colst", [128, 24], F32)
    bC4 = a2.sb("bC4", [128, 4], F32)
    rq = a2.sb("rq", [128, 16], F32)
    rtmp = a2.sb("rtmp", [128, 448], F32)
    kf_s = a2.sb("kf_s", [128, 1024], BF16)
    qf_s = a2.sb("qf_s", [128, 1024], BF16)
    qbf_s = a2.sb("qbf_s", [128, 1024], BF16)
    qb_tm = a2.sb("qb_tm", [128, 768], BF16)
    A4 = a2.sb("A4", [128, 512], BF16)
    AT4 = a2.sb("AT4", [128, 512], BF16)
    X4 = a2.sb("X4", [128, 512], BF16)
    dtmp = a2.sb("dtmp", [128, 128], F32)
    Dsf = a2.sb("Dsf", [128, 128], F32)
    MT4 = a2.sb("MT4", [128, 512], BF16)
    ktl = a2.sb("ktl", [128, 192], BF16)
    kdec = a2.sb("kdec", [128, 192], BF16)
    bVb = a2.sb("bVb", [128, 384], BF16)
    Ub = a2.sb("Ub", [128, 384], BF16)
    Wnegb = a2.sb("Wnegb", [128, 256], BF16)
    Sa = [a2.sb("Sa%d" % h, [128, 384], F32) for h in range(4)]
    Sb = [a2.sb("Sb%d" % h, [64, 384], F32) for h in range(4)]
    S0bA = a2.sb("S0bA", [128, 4 * 384], BF16)
    S0bB = a2.sb("S0bB", [64, 4 * 384], BF16)
    sgb = a2.sb("sgb", [128, 384], BF16)
    sgg = sgb
    obf = a2.sb("obf", [128, 384], F32)
    ro = a2.sb("ro", [128, 4], F32)
    o4tm = a2.sb("o4tm", [128, 1536], BF16)
    ofm2 = a2.sb("ofm2", [128, 3072], BF16)
    ytmp = o4tm
    yfm2 = a2.sb("yfm2", [128, 2048], BF16)
    wsob = [a2.sb("wsob%d" % i, [128, 1024], BF16) for i in range(2)]
    wout = [a2.sb("wout%d" % i, [128, 1024], BF16) for i in range(2)]
    P2a = a2.ps("P2a", [128, 1024], F32)
    P2b = a2.ps("P2b", [128, 1024], F32)
    P1a = a2.ps("P1a", [128, 512], F32)
    P1b = a2.ps("P1b", [128, 512], F32)
    P1c = a2.ps("P1c", [128, 512], F32)
    PT = a2.ps("PTb", [128, 1024], BF16)

    te = em.op("tensor", lambda: nc.tensor.matmul(
        P1a[:, 0:DVD], st["onesb"][:1, :], gnw[:1, :], start=True, stop=True),
        wait=[("DW", env["ds_small"]), ("TE", st["a1_te_last"]), ("SE", st["a1_se_last"]),
              ("VE", st["a1_ve_last"])], inc="TE")
    ve_grw = em.op("vector", lambda: nc.vector.tensor_copy(grw[:, :], P1a[:, 0:DVD]),
                   wait=[("TE", te)], inc="VE")
    gz = None
    a1fence = [("SE", st["a1_se_last"]), ("VE", st["a1_ve_last"]), ("TE", st["a1_te_last"])]
    for h in range(4):
        em.op("gpsimd", lambda h=h: nc.gpsimd.memset(Sa[h][:, :], 0.0),
              wait=a1fence if h == 0 else ())
        em.op("gpsimd", lambda h=h: nc.gpsimd.memset(Sb[h][:, :], 0.0))
    em.op("gpsimd", lambda: nc.gpsimd.memset(S0bA[:, :], 0.0))
    em.op("gpsimd", lambda: nc.gpsimd.memset(S0bB[:, :], 0.0))
    gz0 = em.op("gpsimd", lambda: nc.gpsimd.memset(onesfA[:, :], 1.0), inc="GC")
    gz = gz0
    st["gz_ds"] = 0

    def V(fn, wait=(), inc=None, drain=True):
        if drain:
            em.op("vector", lambda: nc.vector.drain())
        return em.op("vector", fn, wait=wait, inc=inc)

    SLICES = [(0, 512, q_tm, 0), (512, 256, q_tm, 512),
              (768, 512, k_tm, 0), (1280, 256, k_tm, 512),
              (1536, 512, v_tm, 0), (2048, 512, v_tm, 512), (2560, 512, v_tm, 1024),
              (3072, 512, g_tm, 0), (3584, 512, g_tm, 512), (4096, 512, g_tm, 1024)]
    PJ = [P1a, P1b, P1c, P2a]

    ve_sched = {"proj_evac": 0}
    st_fin = None
    prev = {
        "te_qk_last": 0,
        "ve_state": 0,
        "ve_out": 0,
        "te_out": [0, 0],
        "ve_ytmp": 0,
        "dw_wq": [0, 0],
        "te_proj": [0, 0],
        "ve_pj": [0, 0, 0, 0],
        "te_pj": [0, 0, 0, 0],
        "se_gkb": 0,
        "ve_gam": 0,
        "ds_stk": 0,
        "ve_colst": 0,
        "se_dec": 0,
        "ve_kf": 0, "ve_qf": 0, "ve_qbf": 0,
        "ve_A": 0, "ve_MT": 0, "ve_AT": 0,
        "ve_X": 0,
        "ve_Ub": 0, "ve_o4": 0,
        "te_state": 0, "te_O": 0, "te_U": 0,
        "ve_ofm": 0,
        "dw_wso": [0, 0], "dw_wout": [0, 0],
        "te_y": [0, 0],
    }

    for s in range(NSEG_RUN):
        tau = [2 * s, 2 * s + 1]
        ds_zs = em.op("sync", lambda s=s: nc.sync.dma_start(
            out=zseg[:, :].rearrange("p (c w) -> p c w", w=256),
            in_=st["zscr"][:, :, 256 * s:256 * (s + 1)].rearrange("c p w -> p c w")),
            wait=[("TE", prev["te_out"][0]), ("TE", prev["te_out"][1]),
                  ("DS", st["a1_ds_last"] if s == 0 else 0)], inc="DS", n=16)
        prev["ds_zs"] = ds_zs
        for j, (c0, wd, dst, dbase) in enumerate(SLICES):
            dwh = [None, None]
            for half in range(2):
                bq = half
                wld = [("TE", prev["te_proj"][bq])]
                if s == 0 and j == 0:
                    wld += [("TE", st["a1_te_last"]), ("VE", st["a1_ve_last"])]
                dwh[half] = em.op("gpsimd", lambda c0=c0, wd=wd, bq=bq, half=half: nc.gpsimd.dma_start(
                    out=wqs[bq][:, 0:4 * wd].rearrange("p (c w) -> p c w", w=wd),
                    in_=st["dWqkvg"][512 * half:512 * (half + 1), c0:c0 + wd]
                        .rearrange("(c p) w -> p c w", p=128)),
                    wait=wld, inc="DW", n=16)
                prev["dw_wq"][bq] = dwh[half]
            for c2 in range(2):
                pj = PJ[(2 * j + c2) % 4]
                pidx = (2 * j + c2) % 4
                wmm = [("DW", dwh[0]), ("VE", prev["ve_pj"][pidx])]
                if s == 0 and j == 0:
                    wmm.append(("VE", st["a1_ve_last"]))
                    wmm.append(("SE", st["a1_se_last"]))
                if s > 0 and j < 4:
                    wmm.append(("VE", prev["ve_out"]))
                for kc in range(KD):
                    w2 = wmm if kc == 0 else ([("DW", dwh[1])] if kc == 4 else ())
                    te = em.op("tensor", lambda c2=c2, kc=kc, wd=wd, pj=pj, s=s: nc.tensor.matmul(
                        pj[:, 0:wd],
                        xs[:, 2048 * kc + 128 * (2 * s + c2):2048 * kc + 128 * (2 * s + c2) + 128],
                        wqs[kc // 4][:, wd * (kc % 4):wd * (kc % 4) + wd],
                        start=(kc == 0), stop=(kc == KD - 1)),
                        wait=w2, inc="TE" if kc == KD - 1 else None)
                prev["te_pj"][pidx] = te
                prev["ve_pj"][pidx] = em.op("vector", lambda dst=dst, dbase=dbase, c2=c2, wd=wd, pj=pj:
                    nc.vector.tensor_copy(
                        dst[:, (dst.shape[1] // 2) * c2 + dbase:(dst.shape[1] // 2) * c2 + dbase + wd],
                        pj[:, 0:wd]), wait=[("TE", te)], inc="VE")
            prev["te_proj"] = [prev["te_pj"][(2 * j + 1) % 4], prev["te_pj"][(2 * j + 1) % 4]]
        dwg = em.op("gpsimd", lambda: nc.gpsimd.dma_start(
            out=wqs[0][:, 0:KD * 8].rearrange("p (c w) -> p c w", w=8),
            in_=st["dWqkvg"][:, 4608:4616].rearrange("(c p) w -> p c w", p=128)),
            wait=[("TE", prev["te_proj"][0])], inc="DW", n=16)
        wmm = [("DW", dwg), ("SE", prev["se_gkb"]), ("VE", prev["ve_out"])]
        for kc in range(KD):
            em.op("tensor", lambda kc=kc, s=s: nc.tensor.matmul(
                P2b[0:4, 0:256], wqs[0][:, 8 * kc:8 * kc + 4],
                xs[:, 2048 * kc + 256 * s:2048 * kc + 256 * s + 256],
                start=(kc == 0), stop=(kc == KD - 1)),
                wait=wmm if kc == 0 else ())
            te = em.op("tensor", lambda kc=kc, s=s: nc.tensor.matmul(
                P2b[0:4, 512:768], wqs[0][:, 8 * kc + 4:8 * kc + 8],
                xs[:, 2048 * kc + 256 * s:2048 * kc + 256 * s + 256],
                start=(kc == 0), stop=(kc == KD - 1)),
                inc="TE" if kc == KD - 1 else None)
        prev["te_proj"] = [te, te]
        em.op("scalar", lambda: nc.scalar.activation(spt[:, :], P2b[0:4, 0:256], AF.Exp,
                                                     bias=biasv[:, 0:1]),
              wait=[("TE", te), ("SE", env["se_nA"]), ("VE", prev["ve_gam"])])
        em.op("scalar", lambda: nc.scalar.activation(spt[:, :], spt[:, :], AF.Ln, bias=1.0))
        se1 = em.op("scalar", lambda: nc.scalar.activation(betf[:, :], P2b[0:4, 512:768], AF.Sigmoid,
                                                           bias=bbv[:, 0:1]), inc="SE")
        prev["se_gkb"] = se1
        gkf = spt
        V(lambda: nc.vector.tensor_scalar(gkf[:, :], spt[:, :], nA[0:4, 0:1], None, ALU.mult),
          wait=[("SE", se1)])
        for c2 in range(2):
            V(lambda c2=c2: nc.vector.tensor_tensor_scan(
                gam[:, 128 * c2:128 * (c2 + 1)], gkf[:, 128 * c2:128 * (c2 + 1)],
                gkf[:, 128 * c2:128 * (c2 + 1)], 0.0, ALU.add, ALU.bypass))
        ve_gam = V(lambda: nc.vector.tensor_scalar(ngam[:, :], gam[:, :], -1.0, None, ALU.mult),
                   inc="VE")
        se_bex = em.op("scalar", lambda: nc.scalar.activation(bex[:, :], gam[:, :], AF.Exp),
                       wait=[("VE", ve_gam)], inc="SE")
        prev["ve_gam"] = ve_gam
        ds_gw = em.op("sync", lambda: nc.sync.dma_start(out=st["gscr"][:, :], in_=gam[:, :]),
                      wait=[("VE", ve_gam), ("DS", prev.get("ds_stk", 0))], inc="DS", n=16)

        for c2 in range(2):
            tt_ = tau[c2]
            ofmbase = 1536 * c2
            wstk = [("DS", ds_gw), ("GC", gz), ("TE", prev.get("te_gm", 0))]
            ds_stk = em.op("sync", lambda c2=c2: nc.sync.dma_start(
                out=grow[0:1, :].rearrange("p (h t) -> p h t", t=128),
                in_=st["gscr"][:, 128 * c2:128 * (c2 + 1)]),
                wait=wstk, inc="DS", n=16)
            prev["ds_stk"] = ds_stk
            ve_bb = V(lambda c2=c2: nc.vector.tensor_tensor(
                bb4c[:, :], betf[:, 128 * c2:128 * (c2 + 1)], bex[:, 128 * c2:128 * (c2 + 1)],
                ALU.mult), wait=[("SE", se_bex)], inc="VE")
            se_dec = em.op("scalar", lambda c2=c2: nc.scalar.activation(
                dec4c[:, :], gam[:, 128 * c2:128 * (c2 + 1)], AF.Exp, scale=-1.0,
                bias=gam[:, 128 * c2 + 127:128 * c2 + 128]), wait=[("SE", se_bex)], inc="SE")
            wtr = [("VE", ve_bb), ("SE", se_dec), ("VE", prev["ve_state"]), ("TE", prev["te_state"])]
            srcs = [betf[:, 128 * c2:128 * (c2 + 1)], bex[:, 128 * c2:128 * (c2 + 1)],
                    bb4c[:, :], dec4c[:, :], gam[:, 128 * c2:128 * (c2 + 1)],
                    ngam[:, 128 * c2:128 * (c2 + 1)]]
            for i, src in enumerate(srcs):
                te = em.op("tensor", lambda src=src, i=i: nc.tensor.transpose(
                    P1c[:, 4 * i:4 * (i + 1)], src, identf[:4, :4]),
                    wait=wtr if i == 0 else (), inc="TE" if i == 5 else None)
            ve_colst = V(lambda: nc.vector.tensor_copy(colst[:, :], P1c[:, 0:24]),
                         wait=[("TE", te)], inc="VE", drain=False)
            V(lambda: nc.vector.tensor_tensor(bC4[:, :], colst[:, 4:8], colst[:, 12:16], ALU.mult))
            prev["ve_colst"] = ve_colst
            qsl = q_tm[:, 768 * c2:768 * (c2 + 1)]
            ksl = k_tm[:, 768 * c2:768 * (c2 + 1)]
            wsq0 = [("VE", prev["ve_pj"][0]), ("VE", prev["ve_pj"][1]),
                    ("VE", prev["ve_pj"][2]), ("VE", prev["ve_pj"][3])]
            for h in range(4):
                V(lambda h=h, qsl=qsl: nc.vector.tensor_tensor(
                    rtmp[:, 0:192], qsl[:, 192 * h:192 * (h + 1)], qsl[:, 192 * h:192 * (h + 1)],
                    ALU.mult), wait=wsq0 if h == 0 else (), drain=True)
                V(lambda h=h: nc.vector.reduce_sum(rq[:, h:h + 1], rtmp[:, 0:192], axis=AXX),
                  drain=True)
            for h in range(4):
                V(lambda h=h, ksl=ksl: nc.vector.tensor_tensor(
                    rtmp[:, 0:192], ksl[:, 192 * h:192 * (h + 1)], ksl[:, 192 * h:192 * (h + 1)],
                    ALU.mult), drain=True)
                ve_n = V(lambda h=h: nc.vector.reduce_sum(rq[:, 4 + h:5 + h], rtmp[:, 0:192], axis=AXX),
                         drain=True, inc="VE" if h == 3 else None)
            em.op("scalar", lambda: nc.scalar.activation(rq[:, 8:16], rq[:, 0:8], AF.Ln,
                                                         bias=st["epss"][:, 1:2]),
                  wait=[("VE", ve_n)])
            se_rq = em.op("scalar", lambda: nc.scalar.activation(rq[:, 0:8], rq[:, 8:16], AF.Exp,
                                                                 scale=-0.5), inc="SE")
            for h in range(4):
                V(lambda h=h, qsl=qsl: nc.vector.tensor_scalar(
                    qsl[:, 192 * h:192 * (h + 1)], qsl[:, 192 * h:192 * (h + 1)],
                    rq[:, h:h + 1], None, ALU.mult), wait=[("SE", se_rq)] if h == 0 else (),
                    drain=(h == 0))
            for h in range(4):
                ve_nk = V(lambda h=h, ksl=ksl: nc.vector.tensor_scalar(
                    ksl[:, 192 * h:192 * (h + 1)], ksl[:, 192 * h:192 * (h + 1)],
                    rq[:, 4 + h:5 + h], None, ALU.mult), drain=False,
                    inc="VE" if h == 3 else None)
            for h in range(4):
                ve_qb = V(lambda h=h, qsl=qsl: nc.vector.tensor_scalar(
                    qb_tm[:, 192 * h:192 * (h + 1)], qsl[:, 192 * h:192 * (h + 1)],
                    colst[:, 4 + h:5 + h], None, ALU.mult), drain=(h == 0),
                    inc="VE" if h == 3 else None)
            for name, src, dst in (("ve_kf", ksl, kf_s), ("ve_qf", qsl, qf_s), ("ve_qbf", qb_tm, qbf_s)):
                wtp = [("VE", ve_nk if name != "ve_qbf" else ve_qb), ("VE", ve_grw)]
                if name == "ve_kf":
                    wtp.append(("VE", prev["ve_ofm"]))
                else:
                    wtp.append(("VE", prev[{"ve_qf": "ve_kf", "ve_qbf": "ve_qf"}[name]]))
                nt_ = 0
                for h4 in range(4):
                    for (col, nr, loc) in head_kslices(h4):
                        te = em.op("tensor", lambda src=src, h4=h4, col=col, nr=nr, loc=loc: nc.tensor.transpose(
                            PT[0:nr, col:col + 128], src[:, 192 * h4 + loc:192 * h4 + loc + nr],
                            ident_b[:, :]),
                            wait=wtp if nt_ == 0 else (), inc="TE" if nt_ == 7 else None)
                        nt_ += 1
                prev[name] = em.op("vector", lambda dst=dst: nc.vector.tensor_copy(
                    dst[:, :], PT[:, 0:1024]), wait=[("TE", te)], inc="VE")
            for h in range(4):
                ks_ = head_kslices(h)
                wk2 = [("VE", prev["ve_kf"]), ("VE", prev["ve_A"])]
                te_k2 = None
                for (cc, nr, loc) in ks_:
                    te_k2 = em.op("tensor", lambda cc=cc, nr=nr, first=(te_k2 is None): nc.tensor.matmul(
                        P1a[:, 0:128], kf_s[0:nr, cc:cc + 128], kf_s[0:nr, cc:cc + 128],
                        start=first, stop=not first),
                        wait=wk2 if te_k2 is None else (), inc="TE")
                wqk = [("VE", prev["ve_qf"]), ("VE", prev["ve_MT"])]
                te_qk = None
                for (cc, nr, loc) in ks_:
                    te_qk = em.op("tensor", lambda cc=cc, nr=nr, first=(te_qk is None): nc.tensor.matmul(
                        P1b[:, 0:128], kf_s[0:nr, cc:cc + 128], qf_s[0:nr, cc:cc + 128],
                        start=first, stop=not first),
                        wait=wqk if te_qk is None else (), inc="TE")
                prev["te_qk_last"] = te_qk
                wgm = [("DS", prev["ds_stk"]), ("VE", prev["ve_colst"])]
                te_gm = em.op("tensor", lambda h=h: nc.tensor.matmul(
                    P1c[:, 128:256], onesfA[:1, 0:128], grow[0:1, 128 * h:128 * (h + 1)],
                    start=True, stop=True), wait=wgm, inc="TE")
                prev["te_gm"] = te_gm
                V(lambda h=h: nc.vector.tensor_scalar(
                    dtmp[:, :], P1c[:, 128:256], -1.0, colst[:, 16 + h:17 + h],
                    ALU.mult, ALU.add), wait=[("TE", te_gm)], drain=False)
                ve_d = V(lambda: nc.vector.scalar_tensor_tensor(
                    dtmp[:, :], dtmp[:, :], 40.0, lms01[:, :], ALU.add, ALU.mult),
                    drain=True, inc="VE")
                se_ds = em.op("scalar", lambda: nc.scalar.activation(
                    Dsf[:, :], dtmp[:, :], AF.Exp, bias=st["epss"][:, 2:3]),
                    wait=[("VE", ve_d)], inc="SE")
                ve_A = V(lambda h=h: nc.vector.scalar_tensor_tensor(
                    A4[:, 128 * h:128 * (h + 1)], P1a[:, 0:128], colst[:, h:h + 1], Dsf[:, :],
                    ALU.mult, ALU.mult), wait=[("SE", se_ds), ("TE", te_k2)], inc="VE", drain=False)
                prev["ve_A"] = ve_A
                if "invd" in st["taps"] and s == 0 and c2 == 1 and h == 3:
                    idbg = v_tm
                    V(lambda: nc.vector.tensor_copy(idbg[:, 0:128], P1c[:, 128:256]), drain=True)
                    V(lambda: nc.vector.tensor_copy(idbg[:, 128:256], Dsf[:, :]), drain=False)
                    V(lambda: nc.vector.tensor_copy(idbg[:, 256:384], P1a[:, 0:128]), drain=False)
                    V(lambda h=h: nc.vector.tensor_copy(idbg[:, 384:512], A4[:, 128 * h:128 * (h + 1)]),
                      drain=False)
                    ve_idbg = V(lambda: nc.vector.tensor_copy(idbg[:, 512:536], colst[:, :]),
                                drain=False, inc="VE")
                    em.op("gpsimd", lambda: nc.gpsimd.dma_start(
                        out=st["dbg"]["invd"], in_=idbg[:, 0:2176]),
                        wait=[("VE", ve_idbg)], inc="DW", n=16)
                    prev["ve_MT"] = ve_idbg
                V(lambda h=h: nc.vector.tensor_scalar(
                    dtmp[:, :], P1c[:, 128:256], 1.0, colst[:, 20 + h:21 + h],
                    ALU.mult, ALU.add), wait=[("SE", se_ds)], drain=False)
                ve_d2 = V(lambda: nc.vector.scalar_tensor_tensor(
                    dtmp[:, :], dtmp[:, :], 40.0, umi01[:, :], ALU.add, ALU.mult),
                    drain=True, inc="VE")
                se_di = em.op("scalar", lambda: nc.scalar.activation(
                    Dsf[:, :], dtmp[:, :], AF.Exp, bias=st["epss"][:, 2:3]),
                    wait=[("VE", ve_d2)], inc="SE")
                ve_MT = V(lambda h=h: nc.vector.tensor_tensor(
                    MT4[:, 128 * h:128 * (h + 1)], P1b[:, 0:128], Dsf[:, :], ALU.mult),
                    wait=[("SE", se_di), ("TE", te_qk)], inc="VE", drain=False)
                prev["ve_MT"] = ve_MT
                te_at = em.op("tensor", lambda h=h: nc.tensor.transpose(
                    PT[:, 0:128], A4[:, 128 * h:128 * (h + 1)], ident_b[:, :]),
                    wait=[("VE", ve_A), ("VE", prev["ve_qbf"] if h == 0 else prev["ve_AT"])],
                    inc="TE")
                prev["ve_AT"] = em.op("vector", lambda h=h: nc.vector.tensor_copy(
                    AT4[:, 128 * h:128 * (h + 1)], PT[:, 0:128]), wait=[("TE", te_at)], inc="VE")
            Db4 = kf_s[:, 0:512]
            DbT4 = kf_s[:, 512:1024]
            F4 = qb_tm[:, 0:512]
            Hb = qf_s[:, 0:512]
            Gb = qf_s[:, 512:1024]
            XC = o4tm[:, 0:512]
            Tdb = kf_s[:, 0:512]
            V(lambda: nc.vector.tensor_tensor(Db4, A4[:, :], bdm4[:, :], ALU.mult),
              wait=[("VE", prev["ve_A"]), ("TE", prev["te_qk_last"]), ("VE", prev["ve_MT"])],
              drain=False)
            V(lambda: nc.vector.tensor_tensor(DbT4, AT4[:, :], bdm4[:, :], ALU.mult),
              wait=[("VE", prev["ve_AT"])], drain=False)
            V(lambda: nc.vector.tensor_tensor(F4, A4[:, :], Db4, ALU.subtract), drain=True)
            ve_X = V(lambda: nc.vector.tensor_tensor(X4[:, :], i4b[:, 0:512], DbT4, ALU.subtract),
                     drain=True, inc="VE")
            ve_sq = ve_X
            for j in range(4):
                wpow = [("VE", ve_sq)]
                if j == 0:
                    wpow.append(("VE", prev["ve_pj"][3]))
                te_pow = None
                for h in range(4):
                    te_pow = em.op("tensor", lambda h=h: nc.tensor.matmul(
                        P2a[:, 128 * h:128 * (h + 1)], DbT4[:, 128 * h:128 * (h + 1)],
                        Db4[:, 128 * h:128 * (h + 1)], start=True, stop=True),
                        wait=wpow if h == 0 else (), inc="TE" if h == 3 else None)
                if j < 3:
                    for h in range(4):
                        te_pow = em.op("tensor", lambda h=h: nc.tensor.matmul(
                            P2a[:, 512 + 128 * h:512 + 128 * (h + 1)], Db4[:, 128 * h:128 * (h + 1)],
                            DbT4[:, 128 * h:128 * (h + 1)], start=True, stop=True),
                            inc="TE" if h == 3 else None)
                if j < 3:
                    V(lambda: nc.vector.tensor_copy(Db4, P2a[:, 0:512]),
                      wait=[("TE", te_pow)], drain=False)
                    ve_sq = V(lambda: nc.vector.tensor_copy(DbT4, P2a[:, 512:1024]),
                              drain=False, inc="VE")
                else:
                    ve_sq = V(lambda: nc.vector.tensor_copy(Db4, P2a[:, 0:512]),
                              wait=[("TE", te_pow)], drain=False, inc="VE")
                te_prod = None
                for h in range(4):
                    te_prod = em.op("tensor", lambda h=h: nc.tensor.matmul(
                        P2b[:, 128 * h:128 * (h + 1)], Db4[:, 128 * h:128 * (h + 1)],
                        X4[:, 128 * h:128 * (h + 1)], start=True, stop=True),
                        wait=[("VE", ve_sq), ("VE", ve_X)] if h == 0 else (),
                        inc="TE" if h == 3 else None)
                ve_X = V(lambda: nc.vector.tensor_tensor(X4[:, :], X4[:, :], P2b[:, 0:512], ALU.add),
                         wait=[("TE", te_prod)], drain=False, inc="VE")
            te_H = None
            for h in range(4):
                te_H = em.op("tensor", lambda h=h: nc.tensor.matmul(
                    P2b[:, 128 * h:128 * (h + 1)], F4[:, 128 * h:128 * (h + 1)],
                    X4[:, 128 * h:128 * (h + 1)], start=True, stop=True),
                    wait=[("VE", ve_X)] if h == 0 else (), inc="TE" if h == 3 else None)
            te_G = None
            for h in range(4):
                te_G = em.op("tensor", lambda h=h: nc.tensor.matmul(
                    P2a[:, 128 * h:128 * (h + 1)], X4[:, 128 * h:128 * (h + 1)],
                    F4[:, 128 * h:128 * (h + 1)], start=True, stop=True),
                    inc="TE" if h == 3 else None)
            V(lambda: nc.vector.tensor_copy(Hb, P2b[:, 0:512]), wait=[("TE", te_H), ("TE", te_G)],
              drain=False)
            ve_HG = V(lambda: nc.vector.tensor_copy(Gb, P2a[:, 0:512]), drain=False, inc="VE")
            te_h2 = None
            for h in range(4):
                te_h2 = em.op("tensor", lambda h=h: nc.tensor.matmul(
                    P2b[:, 512 + 128 * h:512 + 128 * (h + 1)], Gb[:, 128 * h:128 * (h + 1)],
                    Hb[:, 128 * h:128 * (h + 1)], start=True, stop=True),
                    wait=[("VE", ve_HG)] if h == 0 else (), inc="TE" if h == 3 else None)
            te_g2 = None
            for h in range(4):
                te_g2 = em.op("tensor", lambda h=h: nc.tensor.matmul(
                    P2a[:, 512 + 128 * h:512 + 128 * (h + 1)], Hb[:, 128 * h:128 * (h + 1)],
                    Gb[:, 128 * h:128 * (h + 1)], start=True, stop=True),
                    inc="TE" if h == 3 else None)
            ve_g2 = V(lambda: nc.vector.tensor_copy(F4, P2a[:, 512:1024]),
                      wait=[("TE", te_g2)], drain=False, inc="VE")
            te_h3 = None
            for h in range(4):
                te_h3 = em.op("tensor", lambda h=h: nc.tensor.matmul(
                    P2a[:, 128 * h:128 * (h + 1)], F4[:, 128 * h:128 * (h + 1)],
                    Hb[:, 128 * h:128 * (h + 1)], start=True, stop=True),
                    wait=[("VE", ve_g2)] if h == 0 else (), inc="TE" if h == 3 else None)
            V(lambda: nc.vector.tensor_tensor(XC, i4b[:, 0:512], Hb, ALU.subtract),
              wait=[("TE", te_h3)], drain=True)
            V(lambda: nc.vector.tensor_tensor(XC, XC, P2b[:, 512:1024], ALU.add), drain=True)
            ve_xc = V(lambda: nc.vector.tensor_tensor(XC, XC, P2a[:, 0:512], ALU.subtract),
                      drain=True, inc="VE")
            te_tdt = None
            for h in range(4):
                te_tdt = em.op("tensor", lambda h=h: nc.tensor.transpose(
                    PT[:, 128 * h:128 * (h + 1)], X4[:, 128 * h:128 * (h + 1)], ident_b[:, :]),
                    wait=[("VE", ve_xc)] if h == 0 else (), inc="TE" if h == 3 else None)
            ve_td = V(lambda: nc.vector.tensor_copy(Tdb, PT[:, 0:512]),
                      wait=[("TE", te_tdt)], drain=False, inc="VE")
            te_fin = None
            for h in range(4):
                te_fin = em.op("tensor", lambda h=h: nc.tensor.matmul(
                    P2b[:, 128 * h:128 * (h + 1)], Tdb[:, 128 * h:128 * (h + 1)],
                    XC[:, 128 * h:128 * (h + 1)], start=True, stop=True),
                    wait=[("VE", ve_td)] if h == 0 else (), inc="TE" if h == 3 else None)
            ve_X = V(lambda: nc.vector.tensor_copy(X4[:, :], P2b[:, 0:512]),
                     wait=[("TE", te_fin)], drain=False, inc="VE")
            for h in range(4):
                ks_ = head_kslices(h)
                vsl = v_tm[:, 1536 * c2 + 384 * h:1536 * c2 + 384 * (h + 1)]
                gsl = g_tm[:, 1536 * c2 + 384 * h:1536 * c2 + 384 * (h + 1)]
                ve_bv = V(lambda vsl=vsl, h=h: nc.vector.tensor_scalar(
                    bVb[:, :], vsl, colst[:, h:h + 1], None, ALU.mult),
                    wait=[("VE", ve_X), ("TE", prev["te_U"])], inc="VE", drain=True)
                V(lambda h=h, ksl=ksl: nc.vector.tensor_scalar(
                    ktl[:, :], ksl[:, 192 * h:192 * (h + 1)], colst[:, 8 + h:9 + h], None, ALU.mult),
                  drain=False)
                ve_kd = V(lambda h=h, ksl=ksl: nc.vector.tensor_scalar(
                    kdec[:, :], ksl[:, 192 * h:192 * (h + 1)], colst[:, 12 + h:13 + h], None, ALU.mult),
                    drain=False, inc="VE")
                te_u1 = em.op("tensor", lambda h=h: nc.tensor.matmul(
                    P1a[:, 0:384], X4[:, 128 * h:128 * (h + 1)], bVb[:, :], start=True, stop=False),
                    wait=[("VE", ve_bv), ("VE", prev["ve_Ub"])], inc="TE")
                em.op("tensor", lambda h=h: nc.tensor.matmul(
                    P1c[:, 0:128], ktl[:, 0:128], X4[:, 128 * h:128 * (h + 1)],
                    start=True, stop=True),
                    wait=[("VE", ve_kd), ("TE", prev["te_state"]), ("VE", prev["ve_state"])])
                te_w = em.op("tensor", lambda h=h: nc.tensor.matmul(
                    P1c[0:64, 128:256], ktl[:, 128:192], X4[:, 128 * h:128 * (h + 1)],
                    start=True, stop=True), inc="TE")
                ve_wn = V(lambda: nc.vector.tensor_scalar(
                    Wnegb[:, :], P1c[:, 0:256], -1.0, None, ALU.mult),
                    wait=[("TE", te_w)], inc="VE", drain=False)
                te_u = em.op("tensor", lambda h=h: nc.tensor.matmul(
                    P1a[:, 0:384], Wnegb[:, 0:128], S0bA[:, 384 * h:384 * (h + 1)],
                    start=False, stop=False),
                    wait=[("VE", ve_wn), ("VE", prev["ve_state"])])
                te_u = em.op("tensor", lambda h=h: nc.tensor.matmul(
                    P1a[:, 0:384], Wnegb[0:64, 128:256], S0bB[:, 384 * h:384 * (h + 1)],
                    start=False, stop=True), inc="TE")
                prev["te_U"] = te_u
                ve_ub = V(lambda: nc.vector.tensor_copy(Ub[:, :], P1a[:, 0:384]),
                          wait=[("TE", te_u)], inc="VE", drain=False)
                prev["ve_Ub"] = ve_ub
                te_o = em.op("tensor", lambda h=h: nc.tensor.matmul(
                    P1b[:, 0:384], MT4[:, 128 * h:128 * (h + 1)], Ub[:, :], start=True, stop=False),
                    wait=[("VE", ve_ub), ("VE", prev["ve_o4"])])
                for i, (cc, nr, loc) in enumerate(ks_):
                    s0 = S0bA if loc == 0 else S0bB
                    te_o = em.op("tensor", lambda cc=cc, nr=nr, h=h, i=i, s0=s0: nc.tensor.matmul(
                        P1b[:, 0:384], qbf_s[0:nr, cc:cc + 128],
                        s0[0:nr, 384 * h:384 * (h + 1)],
                        start=False, stop=(i == 1)), inc="TE" if i == 1 else None)
                prev["te_O"] = te_o
                te_s1 = em.op("tensor", lambda h=h: nc.tensor.matmul(
                    P2b[:, 512:896], kdec[:, 0:128], Ub[:, :], start=True, stop=True),
                    wait=[("VE", ve_ub)])
                te_s2 = em.op("tensor", lambda h=h: nc.tensor.matmul(
                    P1c[0:64, 0:384], kdec[:, 128:192], Ub[:, :], start=True, stop=True),
                    wait=[("VE", ve_wn)], inc="TE")
                prev["te_state"] = te_s2
                V(lambda h=h: nc.vector.scalar_tensor_tensor(
                    Sa[h][:, :], Sa[h][:, :], bC4[:, h:h + 1], P2b[:, 512:896], ALU.mult, ALU.add),
                  wait=[("TE", te_s2)], drain=True)
                V(lambda h=h: nc.vector.scalar_tensor_tensor(
                    Sb[h][:, :], Sb[h][:, :], bC4[0:64, h:h + 1], P1c[0:64, 0:384], ALU.mult, ALU.add),
                  drain=False)
                V(lambda h=h: nc.vector.tensor_copy(S0bA[:, 384 * h:384 * (h + 1)], Sa[h][:, :]),
                  drain=True)
                ve_st = V(lambda h=h: nc.vector.tensor_copy(S0bB[:, 384 * h:384 * (h + 1)], Sb[h][:, :]),
                          drain=False, inc="VE")
                prev["ve_state"] = ve_st
                se_sg = em.op("scalar", lambda gsl=gsl: nc.scalar.activation(sgb[:, :], gsl, AF.Silu),
                              wait=[("VE", prev["ve_pj"][3])], inc="SE")
                V(lambda: nc.vector.tensor_tensor(sgg[:, :], sgb[:, :], grw[:, :], ALU.mult),
                  wait=[("SE", se_sg), ("VE", ve_grw)], drain=False)
                V(lambda: nc.vector.tensor_copy(obf[:, :], P1b[:, 0:384]),
                  wait=[("TE", te_o)], drain=False)
                V(lambda: nc.vector.tensor_tensor(rtmp[:, 0:384], obf[:, :], obf[:, :], ALU.mult),
                  drain=True)
                ve_ro = V(lambda: nc.vector.reduce_sum(ro[:, 0:1], rtmp[:, 0:384], axis=AXX),
                          drain=True, inc="VE")
                em.op("scalar", lambda: nc.scalar.activation(ro[:, 1:2], ro[:, 0:1], AF.Ln,
                                                             scale=1.0 / DVD, bias=st["epss"][:, 0:1]),
                      wait=[("VE", ve_ro)])
                se_ro = em.op("scalar", lambda: nc.scalar.activation(ro[:, 2:3], ro[:, 1:2], AF.Exp,
                                                                     scale=-0.5), inc="SE")
                ve_o4 = V(lambda h=h: nc.vector.scalar_tensor_tensor(
                    o4tm[:, 384 * h:384 * (h + 1)], obf[:, :], ro[:, 2:3], sgg[:, :],
                    ALU.mult, ALU.mult), wait=[("SE", se_ro)], inc="VE", drain=True)
                prev["ve_o4"] = ve_o4
            wtp = [("VE", prev["ve_o4"]), ("VE", prev["ve_AT"])]
            for i in range(8):
                te = em.op("tensor", lambda i=i: nc.tensor.transpose(
                    PT[:, 128 * i:128 * (i + 1)], o4tm[:, 128 * i:128 * (i + 1)], ident_b[:, :]),
                    wait=wtp if i == 0 else (), inc="TE" if i == 7 else None)
            ve_of = em.op("vector", lambda ofmbase=ofmbase: nc.vector.tensor_copy(
                ofm2[:, ofmbase:ofmbase + 1024], PT[:, 0:1024]), wait=[("TE", te)], inc="VE")
            for i in range(4):
                te = em.op("tensor", lambda i=i: nc.tensor.transpose(
                    PT[:, 128 * i:128 * (i + 1)], o4tm[:, 1024 + 128 * i:1024 + 128 * (i + 1)],
                    ident_b[:, :]), wait=[("VE", ve_of)] if i == 0 else (),
                    inc="TE" if i == 3 else None)
            ve_of = em.op("vector", lambda ofmbase=ofmbase: nc.vector.tensor_copy(
                ofm2[:, ofmbase + 1024:ofmbase + 1536], PT[:, 0:512]), wait=[("TE", te)], inc="VE")
            prev["ve_ofm"] = ve_of
        PY = [P2a, P2b]
        te_y = [None, None]
        for kc in range(12):
            bq = kc % 2
            prev["dw_wso"][bq] = em.op("gpsimd", lambda kc=kc, bq=bq: nc.gpsimd.dma_start(
                out=wsob[bq][:, :], in_=st["dW_so"][128 * kc:128 * (kc + 1), :]),
                wait=[("TE", te_y[bq] if te_y[bq] is not None else prev["te_y"][bq])], inc="DW", n=16)
            for c2 in range(2):
                wmm = [("DW", prev["dw_wso"][bq])]
                if kc == 0:
                    wmm += [("VE", prev["ve_ofm"]), ("VE", ve_sq), ("TE", te_prod),
                            ("VE", prev["ve_Ub"]), ("TE", prev["te_state"]), ("TE", prev["te_O"]),
                            ("VE", prev["ve_state"])]
                for nh in range(2):
                    te_y[c2] = em.op("tensor", lambda kc=kc, c2=c2, nh=nh, bq=bq: nc.tensor.matmul(
                        PY[c2][:, 512 * nh:512 * (nh + 1)],
                        ofm2[:, 1536 * c2 + 128 * kc:1536 * c2 + 128 * (kc + 1)],
                        wsob[bq][:, 512 * nh:512 * (nh + 1)],
                        start=(kc == 0), stop=(kc == 11)),
                        wait=wmm if nh == 0 else (), inc="TE" if nh == 1 else None)
        prev["te_y"] = [em.v("TE"), em.v("TE")]
        ve_yf = 0
        for c2 in range(2):
            ve_yt = em.op("vector", lambda c2=c2: nc.vector.tensor_copy(ytmp[:, 0:1024], PY[c2][:, :]),
                          wait=[("TE", te_y[c2]), ("VE", prev["ve_ytmp"])], inc="VE")
            for i in range(8):
                te = em.op("tensor", lambda i=i: nc.tensor.transpose(
                    PT[:, 128 * i:128 * (i + 1)], ytmp[:, 128 * i:128 * (i + 1)], ident_b[:, :]),
                    wait=[("VE", ve_yt), ("VE", ve_yf)] if i == 0 else (),
                    inc="TE" if i == 7 else None)
            ve_yf = em.op("vector", lambda c2=c2: nc.vector.tensor_copy(
                yfm2[:, 1024 * c2:1024 * (c2 + 1)], PT[:, 0:1024]), wait=[("TE", te)], inc="VE")
            prev["ve_ytmp"] = ve_yf
        te_om = [None, None]
        for kc in range(16):
            bq = kc % 2
            prev["dw_wout"][bq] = em.op("gpsimd", lambda kc=kc, bq=bq: nc.gpsimd.dma_start(
                out=wout[bq][:, :], in_=st["dW_out"][128 * kc:128 * (kc + 1), :]),
                wait=[("TE", te_om[bq] if te_om[bq] is not None else prev["te_out"][bq])],
                inc="DW", n=16)
            for c2 in range(2):
                if kc < 8:
                    lhs = lambda kc=kc, c2=c2: yfm2[:, 1024 * c2 + 128 * kc:1024 * c2 + 128 * (kc + 1)]
                else:
                    lhs = lambda kc=kc, c2=c2: zseg[:, 256 * (kc - 8) + 128 * c2:
                                                    256 * (kc - 8) + 128 * c2 + 128]
                wmm = [("DW", prev["dw_wout"][bq])]
                if kc == 0:
                    wmm.append(("VE", ve_yf))
                if kc == 8 and c2 == 0:
                    wmm.append(("DS", prev["ds_zs"]))
                for nh in range(2):
                    te_om[c2] = em.op("tensor", lambda kc=kc, c2=c2, nh=nh, bq=bq, lhs=lhs: nc.tensor.matmul(
                        PY[c2][:, 512 * nh:512 * (nh + 1)], lhs(),
                        wout[bq][:, 512 * nh:512 * (nh + 1)],
                        start=(kc == 0), stop=(kc == 15)),
                        wait=wmm if nh == 0 else (), inc="TE" if nh == 1 else None)
        prev["te_out"] = [em.v("TE"), em.v("TE")]
        for c2 in range(2):
            ve_oadd = em.op("vector", lambda c2=c2, s=s: nc.vector.tensor_tensor(
                xacc[:, 1024 * (2 * s + c2):1024 * (2 * s + c2 + 1)], PY[c2][:, :],
                xacc[:, 1024 * (2 * s + c2):1024 * (2 * s + c2 + 1)], ALU.add),
                wait=[("TE", te_om[c2])], inc="VE")
        prev["ve_out"] = ve_oadd
    if "a2d" in st["taps"]:
        fin = st["em"].snap()
        pieces = [gam, ngam, bex, betf, colst, bC4, rq, q_tm, k_tm, v_tm, g_tm,
                  kf_s, qf_s, qbf_s, qb_tm, A4, AT4, X4, MT4, Ub, Wnegb, bVb, obf,
                  o4tm, ofm2, yfm2, Sa[0], Sb[0], S0bA, sgb]
        col = 0
        for i, tns in enumerate(pieces):
            w = tns.shape[1]
            em.op("gpsimd", lambda tns=tns, col=col, w=w: nc.gpsimd.dma_start(
                out=st["dbg"]["a2d"][0:tns.shape[0], col:col + w], in_=tns[:, :]),
                wait=[("VE", fin["VE"]), ("SE", fin["SE"]), ("TE", fin["TE"])] if i == 0 else (),
                inc="DW", n=16)
            col += w
        st["a2d_cols"] = col
    a2.close()
    a.close()



def kernel(**inputs):
    ins = prep_inputs(inputs)
    nc = build_nc(debug_taps=DEBUG_TAPS)
    in_maps = [dict(ins, x=np.ascontiguousarray(inputs["L0_attn_x"][b])) for b in range(8)]
    res = run_bass_kernel_spmd(nc, in_maps, core_ids=list(range(8)))
    return np.stack([res.results[b]["y"] for b in range(8)], axis=0)


def prep_inputs(inputs):
    P = {}
    P["W_in"] = np.ascontiguousarray(inputs["L0_attn_W_in"])
    P["convw"] = np.ascontiguousarray(np.concatenate(
        [inputs["L0_attn_conv_x_w"].reshape(D, 4), inputs["L0_attn_conv_z_w"].reshape(D, 4)], axis=1))
    P["convb"] = np.ascontiguousarray(np.stack(
        [inputs["L0_attn_conv_x_b"], inputs["L0_attn_conv_z_b"]], axis=1))
    P["Wqkvg"] = np.ascontiguousarray(np.concatenate(
        [inputs["L0_attn_W_q"], inputs["L0_attn_W_k"], inputs["L0_attn_W_v"],
         inputs["L0_attn_W_g"], inputs["L0_attn_W_gk"], inputs["L0_attn_W_b"]], axis=1))
    P["W_so"] = np.ascontiguousarray(inputs["L0_attn_W_so"])
    P["W_out"] = np.ascontiguousarray(inputs["L0_attn_W_out"])
    P["Alog"] = np.ascontiguousarray(inputs["L0_attn_A_log"].reshape(4, 1))
    P["dtbb"] = np.ascontiguousarray(np.concatenate(
        [inputs["L0_attn_dt_bias"], inputs["L0_attn_b_b"]]).reshape(8, 1))
    P["gnw"] = np.ascontiguousarray(inputs["L0_attn_g_norm_w"].reshape(1, DVD))
    P["rw"] = np.ascontiguousarray(inputs["L0_ffn_router_w"])
    P["corr"] = np.ascontiguousarray(inputs["L0_ffn_correction_bias"].reshape(1, NE))
    P["up_w"] = np.ascontiguousarray(inputs["L0_ffn_up_w"])
    P["down_w"] = np.ascontiguousarray(inputs["L0_ffn_down_w"])
    P["sh_up"] = np.ascontiguousarray(inputs["L0_ffn_shared_up_w"])
    P["sh_down"] = np.ascontiguousarray(inputs["L0_ffn_shared_down_w"])
    return P


# version 03
# speedup vs baseline: 1.5494x; 1.1665x over previous
import numpy as np
import concourse.bass as bass
import concourse.mybir as mybir
from concourse.bass_utils import run_bass_kernel_spmd

F32 = mybir.dt.float32
BF16 = mybir.dt.bfloat16
AF = mybir.ActivationFunctionType
ALU = mybir.AluOpType
AXX = mybir.AxisListType.X

T = 2048
D = 1024
NT = 16
KD = 8
H, DKD, DVD = 4, 192, 384
CH = 128
NSEG, SEG = 8, 256
NE = 64
NU = 68
NBUF = 6
GACC = 4
EPS_LN = 1e-5
EPS_L2 = 1e-6
EPS_GN = 1e-5

ENGS = ("tensor", "vector", "scalar", "gpsimd", "sync")


class EM:

    def __init__(self, nc):
        self.nc = nc
        self.q = {e: [] for e in ENGS}
        self.sem = {}
        self.cnt = {}
        self._cm = []

    def new_sem(self, name):
        cm = self.nc.semaphore(name)
        h = cm.__enter__()
        self._cm.append(cm)
        self.sem[name] = h
        self.cnt[name] = 0

    def op(self, eng, fn, wait=(), inc=None, n=1):
        val = None
        if inc is not None:
            self.cnt[inc] += n
            val = self.cnt[inc]
        wait = tuple((s, v) for (s, v) in wait if v)
        self.q[eng].append((fn, wait, inc, n))
        return val

    def v(self, name):
        return self.cnt[name]

    def snap(self):
        return dict(self.cnt)

    def close_sems(self):
        for cm in reversed(self._cm):
            cm.__exit__(None, None, None)

    def emit(self, block):
        for eng in ENGS:
            deco = getattr(block, eng)

            def make(eng):
                items = self.q[eng]
                sems = self.sem

                def body(handle):
                    for fn, waits, inc, n in items:
                        for (s, val) in waits:
                            handle.wait_ge(sems[s], val)
                        if fn is not None:
                            ins = fn()
                            if inc is not None:
                                ins.then_inc(sems[inc], n)
                return body

            deco(make(eng))


class Scope:
    def __init__(self, nc, side="left"):
        self.nc = nc
        self.side = side
        self._cms = []

    def sb(self, name, shape, dtype):
        cm = self.nc.sbuf_tensor(name, shape, dtype, side=self.side)
        t = cm.__enter__()
        self._cms.append(cm)
        return t

    def ps(self, name, shape, dtype=F32):
        cm = self.nc.psum_tensor(name, shape, dtype)
        t = cm.__enter__()
        self._cms.append(cm)
        return t

    def close(self):
        for cm in reversed(self._cms):
            cm.__exit__(None, None, None)
        self._cms = []


DEBUG_TAPS = ()
RUN_ATTN = True
RUN_MOE = True
MOE_NG = None
A2_ON = True
NSEG_RUN = NSEG
A1_LVL = 3


def build_nc(debug_taps=(), run_attn=None):
    run_attn = RUN_ATTN if run_attn is None else run_attn
    nc = bass.Bass("TRN2", target_bir_lowering=False)
    em = EM(nc)
    for s in ("DW", "DS", "TE", "VE", "SE", "GC", "DO"):
        em.new_sem(s)
    st = {"nc": nc, "em": em, "taps": set(debug_taps), "run_attn": run_attn}

    def din(name, shape):
        return nc.dram_tensor(name, shape, F32, kind="ExternalInput").ap()
    st["dx"] = din("x", [T, D])
    st["dW_in"] = din("W_in", [D, 2048])
    st["dconvw"] = din("convw", [D, 8])
    st["dconvb"] = din("convb", [D, 2])
    st["dWqkvg"] = din("Wqkvg", [D, 4616])
    st["dW_so"] = din("W_so", [1536, D])
    st["dW_out"] = din("W_out", [2048, D])
    st["dAlog"] = din("Alog", [4, 1])
    st["ddtbb"] = din("dtbb", [8, 1])
    st["dgnw"] = din("gnw", [1, DVD])
    st["drw"] = din("rw", [NE, D])
    st["dcorr"] = din("corr", [1, NE])
    st["dup"] = din("up_w", [NE, D, 512])
    st["ddn"] = din("down_w", [NE, 512, D])
    st["dshu"] = din("sh_up", [D, 2048])
    st["dshd"] = din("sh_down", [2048, D])
    st["dy"] = nc.dram_tensor("y", [T, D], F32, kind="ExternalOutput").ap()
    st["dbg"] = {}
    for name, shp in (("xn", [128, NT * 1024]), ("combT", [64, T]),
                      ("ffn", [128, NT * 1024]), ("xs", [128, NT * 1024]),
                      ("zs", [128, NT * 1024]), ("rs0", [128, 512]),
                      ("a2d", [128, 28672]), ("invd", [128, 2176])):
        if name in st["taps"]:
            st["dbg"][name] = nc.dram_tensor("dbg_" + name, shp, F32, kind="ExternalOutput").ap()

    g = Scope(nc)
    st["g"] = g
    st["xacc"] = g.sb("xacc", [128, NT * 1024], F32)
    st["ident_b"] = g.sb("ident_b", [128, 128], BF16)
    st["onesb"] = g.sb("onesb", [1, 128], BF16)
    st["epss"] = g.sb("epss", [128, 4], F32)
    xnfs = Scope(nc)
    st["xnf_scope"] = xnfs
    st["xnf"] = xnfs.sb("xnf", [128, NT * 1024], BF16)
    pm = Scope(nc)
    st["pm"] = pm
    st["combT"] = pm.sb("combT", [64, T], BF16)
    st["id64"] = pm.sb("id64", [64, 64], BF16)
    st["ones8"] = pm.sb("ones8", [128, 8], F32)
    st["corr_rep"] = pm.sb("corr_rep", [128, 64], F32)
    st["onesf"] = pm.sb("onesf", [1, 128], F32)
    st["zscr"] = nc.dram_tensor("zscr", [KD, 128, T], BF16, kind="Internal").ap()
    st["gscr"] = nc.dram_tensor("gscr", [4, 256], F32, kind="Internal").ap()
    st["dwq16"] = nc.dram_tensor("dwq16", [D, 4616], BF16, kind="Internal").ap()
    st["dso16"] = nc.dram_tensor("dso16", [1536, D], BF16, kind="Internal").ap()
    st["dout16"] = nc.dram_tensor("dout16", [2048, D], BF16, kind="Internal").ap()

    build_consts(st)
    build_p0(st)
    if RUN_MOE:
        build_moe(st)
    if run_attn and RUN_MOE:
        build_attn(st)
    else:
        st["fin"] = em.snap()

    fin = st["fin"]
    em.op("sync", lambda: nc.sync.dma_start(
        out=st["dy"].rearrange("(t p) j -> p t j", p=128),
        in_=st["xacc"][:, :].rearrange("p (t j) -> p t j", j=1024)),
        wait=[("VE", fin["VE"]), ("SE", fin["SE"]), ("TE", fin["TE"])], inc="DO", n=16)
    em.op("sync", None, wait=[("DO", em.v("DO"))])

    with nc.Block() as block:
        em.emit(block)
    em.close_sems()
    return nc


def build_consts(st):
    nc, em, G = st["nc"], st["em"], st
    ds0 = em.op("sync", lambda: nc.sync.dma_start(out=G["corr_rep"][:1, :], in_=G["dcorr"][:, :]),
                inc="DS", n=16)
    st["ds_corr"] = ds0

    def asel(t, cmp, fill, cm, pat, sl=None):
        ap = t[:, :] if sl is None else t[:, sl[0]:sl[1]]
        return lambda: nc.gpsimd.affine_select(out=ap, in_=ap, compare_op=cmp,
                                               fill=fill, base=0, pattern=pat, channel_multiplier=cm)
    em.op("gpsimd", lambda: nc.gpsimd.memset(G["ident_b"][:, :], 0.0))
    em.op("gpsimd", asel(G["ident_b"], ALU.not_equal, 1.0, 1, [[-1, 128]]))
    em.op("gpsimd", lambda: nc.gpsimd.memset(G["id64"][:, :], 0.0))
    em.op("gpsimd", lambda: nc.gpsimd.affine_select(out=G["id64"][:, :], in_=G["id64"][:, :],
          compare_op=ALU.not_equal, fill=1.0, base=0, pattern=[[-1, 64]], channel_multiplier=1))
    em.op("gpsimd", lambda: nc.gpsimd.memset(G["onesb"][:, :], 1.0))
    em.op("gpsimd", lambda: nc.gpsimd.memset(G["onesf"][:, :], 1.0))
    em.op("gpsimd", lambda: nc.gpsimd.memset(G["ones8"][:, :], 1.0))
    em.op("gpsimd", lambda: nc.gpsimd.memset(G["epss"][:, 0:1], EPS_LN))
    em.op("gpsimd", lambda: nc.gpsimd.memset(G["epss"][:, 1:2], EPS_L2))
    gc = em.op("gpsimd", lambda: nc.gpsimd.memset(G["epss"][:, 2:3], -40.0), inc="GC")
    st["gc_consts"] = gc


def build_p0(st):
    nc, em, G = st["nc"], st["em"], st
    xacc = st["xacc"]
    xnf = st["xnf"]
    p0 = Scope(nc)
    ident_f = p0.sb("ident_f_p0", [128, 128], F32)
    xnf32 = p0.sb("xnf32", [128, NT * 1024], F32)
    xn_t = p0.sb("xn_t", [128, 1024], F32)
    stat = p0.sb("stat", [128, 8], F32)
    rwf = p0.sb("rwf", [64, 1024], F32)
    rwT = p0.sb("rwT", [128, 512], F32)
    rs = p0.sb("rs", [128, 512], F32)
    rA = p0.sb("rA", [128, 1], F32)
    rB = p0.sb("rB", [128, 1], F32)
    cbf = p0.sb("cbf", [128, 64], BF16)
    pt0 = p0.ps("pt0", [128, 512], F32)
    pt1 = p0.ps("pt1", [128, 512], F32)
    pr0 = p0.ps("pr0", [128, 512], F32)
    pr1 = p0.ps("pr1", [128, 512], F32)
    pcb = p0.ps("pcb", [128, 128], BF16)

    em.op("gpsimd", lambda: nc.gpsimd.memset(ident_f[:, :], 0.0))
    gcf = em.op("gpsimd", lambda: nc.gpsimd.affine_select(
        out=ident_f[:, :], in_=ident_f[:, :], compare_op=ALU.not_equal,
        fill=1.0, base=0, pattern=[[-1, 128]], channel_multiplier=1), inc="GC")

    ld_x = em.op("sync", lambda: nc.sync.dma_start(
        out=xacc[:, :].rearrange("p (t j) -> p t j", j=1024),
        in_=st["dx"].rearrange("(t p) j -> p t j", p=128)), inc="DS", n=16)
    ld_rw = em.op("sync", lambda: nc.sync.dma_start(out=rwf[:, :], in_=st["drw"][:, :]), inc="DS", n=16)

    for c in range(KD):
        te = em.op("tensor", lambda c=c: nc.tensor.transpose(
            pt0[:, 64 * c:64 * (c + 1)], rwf[:, 128 * c:128 * (c + 1)], ident_f[:64, :64]),
            wait=([("DS", ld_rw), ("GC", gcf)] if c == 0 else ()), inc="TE" if c == 7 else None)
    ve_rwT = em.op("vector", lambda: nc.vector.tensor_copy(rwT[:, :], pt0[:, :]),
                   wait=[("TE", te)], inc="VE")
    te_cr = em.op("tensor", lambda: nc.tensor.matmul(
        pt1[:, 0:64], st["onesf"][:1, :], st["corr_rep"][:1, :], start=True, stop=True),
        wait=[("DS", st["ds_corr"]), ("GC", st["gc_consts"])], inc="TE")
    ve_corr = em.op("vector", lambda: nc.vector.tensor_copy(st["corr_rep"][:, :], pt1[:, 0:64]),
                    wait=[("TE", te_cr)], inc="VE")
    st["ve_corr"] = ve_corr

    S0, SF, SW, SG, SGW, SGM, SEM_, SSEL, SWS, SX = 0, 64, 128, 192, 200, 208, 216, 280, 344, 408
    te_tr2_prev = 0
    se_prev = 0
    se_sig_bank = [0, 0]
    v6_prev = 0
    tec_prev = 0
    vec_prev = 0
    ve_comb = [0] * NT
    for t in range(NT):
        xt = xacc[:, 1024 * t:1024 * (t + 1)]
        em.op("vector", lambda xt=xt: nc.vector.reduce_sum(stat[:, 0:1], xt, axis=AXX),
              wait=[("DS", ld_x), ("SE", se_prev)] if t == 0 else [("SE", se_prev)])
        em.op("vector", lambda xt=xt: nc.vector.tensor_tensor(xn_t[:, :], xt, xt, ALU.mult),
              wait=[("TE", te_tr2_prev)])
        v1 = em.op("vector", lambda: nc.vector.reduce_sum(stat[:, 2:3], xn_t[:, :], axis=AXX), inc="VE")
        em.op("scalar", lambda: nc.scalar.activation(stat[:, 1:2], stat[:, 0:1], AF.Copy, scale=-1.0 / D),
              wait=[("VE", v1)])
        s1 = em.op("scalar", lambda: nc.scalar.activation(stat[:, 3:4], stat[:, 0:1], AF.Square, scale=1.0 / D),
                   inc="SE")
        v2 = em.op("vector", lambda: nc.vector.scalar_tensor_tensor(
            stat[:, 4:5], stat[:, 2:3], 1.0 / D, stat[:, 3:4], ALU.mult, ALU.subtract),
            wait=[("SE", s1)], inc="VE")
        em.op("scalar", lambda: nc.scalar.activation(stat[:, 5:6], stat[:, 4:5], AF.Ln,
                                                     bias=st["epss"][:, 0:1]),
              wait=[("VE", v2)] + ([("GC", st["gc_consts"])] if t == 0 else []))
        s2 = em.op("scalar", lambda: nc.scalar.activation(stat[:, 6:7], stat[:, 5:6], AF.Exp, scale=-0.5),
                   inc="SE")
        se_prev = s2
        v3 = em.op("vector", lambda xt=xt: nc.vector.tensor_scalar(
            xn_t[:, :], xt, stat[:, 1:2], stat[:, 6:7], ALU.add, ALU.mult),
            wait=[("SE", s2)], inc="VE")
        pt = pt0 if t % 2 == 0 else pt1
        wtr = [("VE", v3)]
        if t == 0:
            wtr.append(("VE", ve_rwT))
        if t == 1:
            wtr.append(("VE", st["ve_corr"]))
        for c in range(4):
            te1 = em.op("tensor", lambda c=c, pt=pt: nc.tensor.transpose(
                pt[:, 128 * c:128 * (c + 1)], xn_t[:, 128 * c:128 * (c + 1)], ident_f[:, :]),
                wait=wtr if c == 0 else (), inc="TE" if c == 3 else None)
        for c in range(4):
            v4 = em.op("vector", lambda c=c, t=t, pt=pt: nc.vector.tensor_copy(
                xnf32[:, 2048 * c + 128 * t:2048 * c + 128 * (t + 1)], pt[:, 128 * c:128 * (c + 1)]),
                wait=[("TE", te1)] if c == 0 else (), inc="VE" if c == 3 else None)
        for c in range(4, 8):
            te2 = em.op("tensor", lambda c=c, pt=pt: nc.tensor.transpose(
                pt[:, 128 * (c - 4):128 * (c - 3)], xn_t[:, 128 * c:128 * (c + 1)], ident_f[:, :]),
                wait=[("VE", v4)] if c == 4 else (), inc="TE" if c == 7 else None)
        te_tr2_prev = te2
        for c in range(4, 8):
            v5 = em.op("vector", lambda c=c, t=t, pt=pt: nc.vector.tensor_copy(
                xnf32[:, 2048 * c + 128 * t:2048 * c + 128 * (t + 1)], pt[:, 128 * (c - 4):128 * (c - 3)]),
                wait=[("TE", te2)] if c == 4 else (), inc="VE" if c == 7 else None)
        pr = pr0 if t % 2 == 0 else pr1
        for c in range(KD):
            ter = em.op("tensor", lambda c=c, t=t, pr=pr: nc.tensor.matmul(
                pr[:, 0:64],
                xnf32[:, 2048 * c + 128 * t:2048 * c + 128 * (t + 1)],
                rwT[:, 64 * c:64 * (c + 1)],
                start=(c == 0), stop=(c == 7)),
                wait=[("VE", v5), ("VE", ve_rwT), ("SE", se_sig_bank[t % 2])] if c == 0 else (),
                inc="TE" if c == 7 else None)
        se_sc = em.op("scalar", lambda pr=pr: nc.scalar.activation(
            rs[:, S0:S0 + 64], pr[:, 0:64], AF.Sigmoid),
            wait=[("TE", ter), ("VE", v6_prev)], inc="SE")
        se_sig_bank[t % 2] = se_sc
        def vop(fn, wait=(), inc=None):
            em.op("vector", lambda: nc.vector.drain())
            return em.op("vector", fn, wait=wait, inc=inc)

        def eq_of(src_ap, width, dst_off):
            vop(lambda: nc.vector.tensor_scalar(
                rs[:, dst_off:dst_off + width], src_ap, rA[:, 0:1], None, ALU.subtract))
            vop(lambda: nc.vector.tensor_scalar(
                rs[:, dst_off:dst_off + width], rs[:, dst_off:dst_off + width],
                1e12, 1.0, ALU.mult, ALU.add))
            vop(lambda: nc.vector.tensor_relu(
                rs[:, dst_off:dst_off + width], rs[:, dst_off:dst_off + width]))

        vop(lambda: nc.vector.tensor_tensor(rs[:, SF:SF + 64], rs[:, S0:S0 + 64], st["corr_rep"][:, :], ALU.add),
            wait=[("SE", se_sc), ("VE", st["ve_corr"])])
        for gi in range(8):
            sfc_g = rs[:, SF + 8 * gi:SF + 8 * (gi + 1)]
            vop(lambda sfc_g=sfc_g: nc.vector.reduce_max(rA[:, 0:1], sfc_g, axis=AXX))
            eq_of(sfc_g, 8, SX + 2)
            vop(lambda sfc_g=sfc_g: nc.vector.scalar_tensor_tensor(
                rs[:, SX + 16:SX + 24], rs[:, SX + 2:SX + 10], -1e30, sfc_g, ALU.mult, ALU.add))
            vop(lambda: nc.vector.reduce_max(rB[:, 0:1], rs[:, SX + 16:SX + 24], axis=AXX))
            vop(lambda gi=gi: nc.vector.tensor_tensor(
                rs[:, SG + gi:SG + gi + 1], rA[:, 0:1], rB[:, 0:1], ALU.add))
        vop(lambda: nc.vector.tensor_copy(rs[:, SGW:SGW + 8], rs[:, SG:SG + 8]))
        vop(lambda: nc.vector.memset(rs[:, SGM:SGM + 8], 0.0))
        for _ in range(4):
            vop(lambda: nc.vector.reduce_max(rA[:, 0:1], rs[:, SGW:SGW + 8], axis=AXX))
            eq_of(rs[:, SGW:SGW + 8], 8, SX + 2)
            vop(lambda: nc.vector.tensor_tensor(
                rs[:, SGM:SGM + 8], rs[:, SGM:SGM + 8], rs[:, SX + 2:SX + 10], ALU.add))
            vop(lambda: nc.vector.scalar_tensor_tensor(
                rs[:, SGW:SGW + 8], rs[:, SX + 2:SX + 10], -1e30, rs[:, SGW:SGW + 8], ALU.mult, ALU.add))
        for gi in range(8):
            vop(lambda gi=gi: nc.vector.tensor_scalar(
                rs[:, SEM_ + 8 * gi:SEM_ + 8 * (gi + 1)], st["ones8"][:, :],
                rs[:, SGM + gi:SGM + gi + 1], None, ALU.mult))
        vop(lambda: nc.vector.tensor_tensor(rs[:, SW:SW + 64], rs[:, SF:SF + 64], rs[:, SEM_:SEM_ + 64], ALU.mult))
        vop(lambda: nc.vector.memset(rs[:, SSEL:SSEL + 64], 0.0))
        for _ in range(8):
            vop(lambda: nc.vector.reduce_max(rA[:, 0:1], rs[:, SW:SW + 64], axis=AXX))
            eq_of(rs[:, SW:SW + 64], 64, SX + 2)
            vop(lambda: nc.vector.tensor_tensor(
                rs[:, SSEL:SSEL + 64], rs[:, SSEL:SSEL + 64], rs[:, SX + 2:SX + 66], ALU.add))
            vop(lambda: nc.vector.scalar_tensor_tensor(
                rs[:, SW:SW + 64], rs[:, SX + 2:SX + 66], -1e30, rs[:, SW:SW + 64], ALU.mult, ALU.add))
        vop(lambda: nc.vector.tensor_tensor(rs[:, SWS:SWS + 64], rs[:, SSEL:SSEL + 64], rs[:, S0:S0 + 64], ALU.mult))
        vop(lambda: nc.vector.reduce_sum(rA[:, 0:1], rs[:, SWS:SWS + 64], axis=AXX))
        vop(lambda: nc.vector.tensor_scalar(rA[:, 0:1], rA[:, 0:1], 1e-20, None, ALU.add))
        vop(lambda: nc.vector.reciprocal(rB[:, 0:1], rA[:, 0:1]))
        v6 = vop(lambda: nc.vector.tensor_scalar(
            cbf[:, :], rs[:, SWS:SWS + 64], rB[:, 0:1], 2.5, ALU.mult, ALU.mult),
            wait=[("TE", tec_prev)], inc="VE")
        v6_prev = v6
        tec = em.op("tensor", lambda: nc.tensor.transpose(pcb[:64, :], cbf[:, :], st["ident_b"][:, :]),
                    wait=[("VE", v6), ("VE", vec_prev)], inc="TE")
        tec_prev = tec
        vec = em.op("vector", lambda t=t: nc.vector.tensor_copy(
            st["combT"][:, 128 * t:128 * (t + 1)], pcb[:64, :]), wait=[("TE", tec)], inc="VE")
        vec_prev = vec
        ve_comb[t] = vec
        if t == 0 and "rs0" in st["taps"]:
            em.op("gpsimd", lambda: nc.gpsimd.dma_start(out=st["dbg"]["rs0"], in_=rs[:, :]),
                  wait=[("VE", v6)], inc="DW", n=16)

    ve_xnf = em.op("vector", lambda: nc.vector.tensor_copy(xnf[:, :], xnf32[:, :]), inc="VE")
    st["ve_xnf"] = ve_xnf
    st["ve_comb"] = ve_comb
    st["te_p0_end"] = em.v("TE")
    st["ve_p0_end"] = em.v("VE")

    if "xn" in st["taps"]:
        em.op("gpsimd", lambda: nc.gpsimd.dma_start(out=st["dbg"]["xn"], in_=xnf[:, :]),
              wait=[("VE", ve_xnf)], inc="DW", n=16)
    if "combT" in st["taps"]:
        em.op("gpsimd", lambda: nc.gpsimd.dma_start(out=st["dbg"]["combT"], in_=st["combT"][:, :]),
              wait=[("VE", ve_comb[-1])], inc="DW", n=16)
    p0.close()


CAST_PIECES = (
    [lambda st, i=i: (lambda: st["nc"].gpsimd.dma_start(
        out=st["dwq16"][128 * i:128 * (i + 1), :],
        in_=st["dWqkvg"][128 * i:128 * (i + 1), :])) for i in range(8)]
    + [lambda st, i=i: (lambda: st["nc"].gpsimd.dma_start(
        out=st["dso16"][768 * i:768 * (i + 1), :],
        in_=st["dW_so"][768 * i:768 * (i + 1), :])) for i in range(2)]
    + [lambda st, i=i: (lambda: st["nc"].gpsimd.dma_start(
        out=st["dout16"][1024 * i:1024 * (i + 1), :],
        in_=st["dW_out"][1024 * i:1024 * (i + 1), :])) for i in range(2)]
)


def build_moe(st):
    nc, em, G = st["nc"], st["em"], st
    xacc, xnf, combT, id64 = st["xacc"], st["xnf"], st["combT"], st["id64"]
    m = Scope(nc)
    wu = [m.sb("wu%d" % i, [128, 4096], BF16) for i in range(NBUF)]
    wd = [m.sb("wd%d" % i, [128, 4096], BF16) for i in range(NBUF)]
    tb = m.sb("tb", [128, 1024], BF16)
    sqb = m.sb("sqb", [128, 1024], BF16)
    h2cb = [m.sb("h2cb%d" % i, [128, 1024], BF16) for i in range(2)]
    crep_sb = m.sb("crep_sb", [128, 256], BF16)
    ph = m.ps("ph", [128, 1024], F32)
    pd = [m.ps("pd%d" % i, [128, 1024], F32) for i in range(2)]
    pcrep = [m.ps("pcrep%d" % i, [128, 512], F32) for i in range(2)]

    def up_src(u):
        if u < 4:
            return G["dshu"][:, 512 * u:512 * (u + 1)].rearrange("(c p) w -> p c w", p=128)
        return G["dup"][u - 4].rearrange("(c p) w -> p c w", p=128)

    def dn_src(u):
        if u < 4:
            return G["dshd"][512 * u:512 * (u + 1), :].rearrange("(c p) w -> p c w", p=128)
        return G["ddn"][u - 4].rearrange("(c p) w -> p c w", p=128)

    dw_done = [None] * (NU + NBUF)
    te_dn_done = [0] * (NU + NBUF)

    def emit_load(u):
        b = u % NBUF
        w = ([("TE", st["te_p0_end"]), ("VE", st["ve_p0_end"])] if u < NBUF
             else [("TE", te_dn_done[u - NBUF])])
        em.op("gpsimd", lambda u=u, b=b: nc.gpsimd.dma_start(
            out=wu[b][:, :].rearrange("p (c w) -> p c w", w=512), in_=up_src(u)),
            wait=w, inc="DW", n=16)
        dw_done[u] = em.op("gpsimd", lambda u=u, b=b: nc.gpsimd.dma_start(
            out=wd[b][:, :].rearrange("p (c w) -> p c w", w=1024), in_=dn_src(u)),
            inc="DW", n=16)

    for u in range(NBUF):
        emit_load(u)

    n_groups = NU // GACC if MOE_NG is None else MOE_NG
    ve_evacadd = 0
    se_relu_prev = 0
    ve_h2c_prev = 0
    ve_crep_prev = [0, 0]
    loads_emitted = NBUF
    pending_down = [None]

    def emit_down(u, r, b, first, last, dn_wait):
        nonlocal ve_evacadd
        wdn = [dn_wait]
        if first:
            wdn.append(("VE", ve_evacadd))
        if u == 0 and r == 0:
            wdn.append(("VE", st["ve_p0_end"]))
        te_dn = None
        for t2 in range(2):
            for ich in range(4):
                for nh in range(2):
                    te_dn = em.op("tensor", lambda t2=t2, ich=ich, nh=nh, u=u, b=b, first=first, last=last: nc.tensor.matmul(
                        pd[t2][:, 512 * nh:512 * (nh + 1)],
                        h2cb[u % 2][:, 256 * ich + 128 * t2:256 * ich + 128 * (t2 + 1)],
                        wd[b][:, 1024 * ich + 512 * nh:1024 * ich + 512 * (nh + 1)],
                        start=(first and ich == 0), stop=(last and ich == 3)),
                        wait=wdn if (t2 == 0 and ich == 0 and nh == 0) else (),
                        inc="TE" if (t2 == 1 and ich == 3 and nh == 1) else None)
        te_dn_done[u] = te_dn
        if last:
            for t2 in range(2):
                tile = 2 * r + t2
                ve_evacadd = em.op("vector", lambda t2=t2, tile=tile: nc.vector.tensor_tensor(
                    xacc[:, 1024 * tile:1024 * (tile + 1)],
                    pd[t2][:, :], xacc[:, 1024 * tile:1024 * (tile + 1)], ALU.add),
                    wait=[("TE", te_dn)], inc="VE")

    for grp in range(n_groups):
        for r in range(8):
            for gi in range(GACC):
                u = grp * GACC + gi
                e = u - 4
                b = u % NBUF
                first = gi == 0
                last = gi == GACC - 1
                te_crep = None
                ve_crep = None
                if e >= 0:
                    wcr = [("VE", st["ve_comb"][2 * r + 1]), ("VE", ve_crep_prev[u % 2])]
                    if u in (4, 5) and r == 0:
                        wcr.append(("VE", st["ve_p0_end"]))
                    te_crep = em.op("tensor", lambda e=e, r=r, u=u: nc.tensor.matmul(
                        pcrep[u % 2][:, 0:256], id64[:, e:e + 1].broadcast_to([64, 128]),
                        combT[:, 256 * r:256 * (r + 1)], start=True, stop=True),
                        wait=wcr, inc="TE")
                    ve_crep = em.op("vector", lambda u=u: nc.vector.tensor_copy(
                        crep_sb[:, :], pcrep[u % 2][:, 0:256]), wait=[("TE", te_crep)], inc="VE")
                    ve_crep_prev[u % 2] = ve_crep
                wup = [("DW", dw_done[u]), ("SE", se_relu_prev)]
                if u == 0 and r == 0:
                    wup.append(("VE", st["ve_xnf"]))
                te_up = None
                for mi in range(4):
                    for kc in range(KD):
                        te_up = em.op("tensor", lambda mi=mi, kc=kc, b=b, r=r: nc.tensor.matmul(
                            ph[:, 256 * mi:256 * (mi + 1)],
                            wu[b][:, 512 * kc + 128 * mi:512 * kc + 128 * (mi + 1)],
                            xnf[:, 2048 * kc + 256 * r:2048 * kc + 256 * (r + 1)],
                            start=(kc == 0), stop=(kc == KD - 1)),
                            wait=wup if (mi == 0 and kc == 0) else (),
                            inc="TE" if (mi == 3 and kc == KD - 1) else None)
                se_relu = em.op("scalar", lambda: nc.scalar.activation(
                    tb[:, :], ph[:, :], AF.Relu), wait=[("TE", te_up)], inc="SE")
                se_relu_prev = se_relu
                wsq = [("TE", te_dn_done[u - 2] if u >= 2 else 0), ("VE", ve_h2c_prev)]
                if e >= 0:
                    se_sq = em.op("scalar", lambda: nc.scalar.activation(
                        sqb[:, :], tb[:, :], AF.Square), wait=wsq, inc="SE")
                    em.op("vector", lambda: nc.vector.drain())
                    for mi in range(4):
                        ve_h2c = em.op("vector", lambda u=u, mi=mi: nc.vector.tensor_tensor(
                            h2cb[u % 2][:, 256 * mi:256 * (mi + 1)],
                            sqb[:, 256 * mi:256 * (mi + 1)], crep_sb[:, :], ALU.mult),
                            wait=([("SE", se_sq), ("VE", ve_crep),
                                   ("TE", te_dn_done[u - 2] if u >= 2 else 0)] if mi == 0 else ()),
                            inc="VE" if mi == 3 else None)
                    dn_wait = ("VE", ve_h2c)
                    ve_h2c_prev = ve_h2c
                else:
                    se_sq = em.op("scalar", lambda u=u: nc.scalar.activation(
                        h2cb[u % 2][:, :], tb[:, :], AF.Square), wait=wsq, inc="SE")
                    dn_wait = ("SE", se_sq)
                if pending_down[0] is not None:
                    emit_down(*pending_down[0])
                pending_down[0] = (u, r, b, first, last, dn_wait)
        for _ in range(GACC):
            if loads_emitted < NU:
                emit_load(loads_emitted)
                loads_emitted += 1
        if grp < len(CAST_PIECES):
            st["dw_cast"] = em.op("gpsimd", CAST_PIECES[grp](st), inc="DW", n=16)
            if grp == 7:
                st["dw_cast_wq"] = st["dw_cast"]
    for i in range(min(n_groups, len(CAST_PIECES)), len(CAST_PIECES)):
        st["dw_cast"] = em.op("gpsimd", CAST_PIECES[i](st), inc="DW", n=16)
        if i == 7:
            st["dw_cast_wq"] = st["dw_cast"]
    if pending_down[0] is not None:
        emit_down(*pending_down[0])
        pending_down[0] = None

    st["moe_fin"] = em.snap()
    if "ffn" in st["taps"]:
        em.op("gpsimd", lambda: nc.gpsimd.dma_start(out=st["dbg"]["ffn"], in_=xacc[:, :]),
              wait=[("VE", em.v("VE"))], inc="DW", n=16)
    m.close()
    st["pm"].close()


def head_kslices(h):
    return [(256 * h, 128, 0), (256 * h + 128, 64, 128)]


def build_attn(st):
    nc, em, G = st["nc"], st["em"], st
    xacc, xnf = st["xacc"], st["xnf"]
    moe_fin = st["moe_fin"]
    if A1_LVL < 0:
        st["xnf_scope"].close()
        st["fin"] = em.snap()
        return

    a = Scope(nc, side="right")
    xs = a.sb("xs", [128, NT * 1024], BF16)
    zseg = a.sb("zseg", [128, 2048], BF16)
    identf = a.sb("identf2", [128, 128], F32)
    lms01 = a.sb("lms01", [128, 128], F32)
    umi01 = a.sb("umi01", [128, 128], F32)
    bdm4 = a.sb("bdm4", [128, 512], BF16)
    i4b = a.sb("i4b", [128, 512], BF16)
    grw = a.sb("grw", [128, DVD], BF16)
    convw_sb = a.sb("convw_sb", [128, 64], F32)
    convb_sb = a.sb("convb_sb", [128, 16], F32)
    nA = a.sb("nA_sb", [4, 1], F32)
    biasv = a.sb("biasv_sb", [4, 1], F32)
    bbv = a.sb("bbv_sb", [4, 1], F32)
    gnw = a.sb("gnw_sb", [1, DVD], BF16)

    def gp(fn, wait=(), inc=None):
        return em.op("gpsimd", fn, wait=wait, inc=inc)

    def asel(ap, cmp, fill, cm, pat):
        return lambda: nc.gpsimd.affine_select(out=ap, in_=ap, compare_op=cmp,
                                               fill=fill, base=0, pattern=pat, channel_multiplier=cm)
    gp(lambda: nc.gpsimd.memset(identf[:, :], 0.0),
       wait=[("TE", moe_fin["TE"]), ("VE", moe_fin["VE"]), ("SE", moe_fin["SE"])])
    gp(asel(identf[:, :], ALU.not_equal, 1.0, 1, [[-1, 128]]))
    gp(lambda: nc.gpsimd.memset(lms01[:, :], 1.0))
    gp(asel(lms01[:, :], ALU.is_gt, 0.0, 1, [[-1, 128]]))
    gp(lambda: nc.gpsimd.memset(umi01[:, :], 1.0))
    gp(asel(umi01[:, :], ALU.is_ge, 0.0, -1, [[1, 128]]))
    for cp in range(4):
        for blk in range(4):
            sl_ = bdm4[:, 128 * cp + 32 * blk:128 * cp + 32 * (blk + 1)]
            gp(lambda sl_=sl_: nc.gpsimd.memset(sl_, 1.0))
            gp(lambda sl_=sl_, blk=blk: nc.gpsimd.affine_select(
                out=sl_, in_=sl_, compare_op=ALU.is_ge, fill=0.0,
                base=-32 * blk, pattern=[[0, 32]], channel_multiplier=1))
            gp(lambda sl_=sl_, blk=blk: nc.gpsimd.affine_select(
                out=sl_, in_=sl_, compare_op=ALU.is_ge, fill=0.0,
                base=32 * blk + 31, pattern=[[0, 32]], channel_multiplier=-1))
    for i in range(4):
        gp(lambda i=i: nc.gpsimd.memset(i4b[:, 128 * i:128 * (i + 1)], 0.0))
        gp(asel(i4b[:, 0:512][:, 128 * i:128 * (i + 1)], ALU.not_equal, 1.0, 1, [[-1, 128]]))
    gc_att = em.op("gpsimd", lambda: nc.gpsimd.memset(nA[:, :], 0.0), inc="GC")
    st["gc_att"] = gc_att

    wfence = [("TE", moe_fin["TE"]), ("VE", moe_fin["VE"]), ("SE", moe_fin["SE"])]
    ds = em.op("sync", lambda: nc.sync.dma_start(
        out=convw_sb[:, :].rearrange("p (c w) -> p c w", w=8),
        in_=st["dconvw"].rearrange("(c p) w -> p c w", p=128)), wait=wfence, inc="DS", n=16)
    ds = em.op("sync", lambda: nc.sync.dma_start(
        out=convb_sb[:, :].rearrange("p (c w) -> p c w", w=2),
        in_=st["dconvb"].rearrange("(c p) w -> p c w", p=128)), inc="DS", n=16)
    st["ds_conv"] = ds
    ds = em.op("sync", lambda: nc.sync.dma_start(out=nA[:, :], in_=st["dAlog"][:, :]),
               wait=[("GC", gc_att)], inc="DS", n=16)
    ds = em.op("sync", lambda: nc.sync.dma_start(out=biasv[:, :], in_=st["ddtbb"][0:4, :]), inc="DS", n=16)
    ds_sync_small = em.op("sync", lambda: nc.sync.dma_start(out=bbv[:, :], in_=st["ddtbb"][4:8, :]), inc="DS", n=16)
    ds_small = em.op("gpsimd", lambda: nc.gpsimd.dma_start(out=gnw[:1, :], in_=st["dgnw"][:, :]),
                     inc="DW", n=16)
    em.op("scalar", lambda: nc.scalar.activation(nA[:, :], nA[:, :], AF.Exp),
          wait=[("DS", ds_sync_small)])
    se_nA = em.op("scalar", lambda: nc.scalar.mul(nA[:, :], nA[:, :], -1.0), inc="SE")
    st["se_nA"] = se_nA

    build_a1(st, a, locals())
    if A2_ON:
        build_a2(st, a, locals())
    else:
        a.close()
    st["fin"] = em.snap()
    if "attn" in st["taps"]:
        em.op("gpsimd", lambda: nc.gpsimd.dma_start(out=st["dbg"]["attn"], in_=xacc[:, :]),
              wait=[("VE", em.v("VE"))], inc="DW", n=16)


def build_a1(st, a, env):
    nc, em = st["nc"], st["em"]
    xnf = st["xnf"]
    xs = env["xs"]
    convw_sb, convb_sb = env["convw_sb"], env["convb_sb"]
    moe_fin = st["moe_fin"]
    a1 = Scope(nc, side="right")
    zch = [a1.sb("zch%d" % i, [128, 2048], BF16) for i in range(2)]
    winb = [a1.sb("winb%d" % i, [128, 1024], BF16) for i in range(2)]
    xzst = [a1.sb("xzst%d" % i, [128, 2048], F32) for i in range(2)]
    cvacc = a1.sb("cvacc", [128, 2048], F32)
    pxz = [a1.ps("pxz%d" % i, [128, 2048], F32) for i in range(2)]

    dw_win = [0] * NT
    te_mm = [0] * NT
    se_evac = [0] * NT
    ve_taps = [0] * NT
    se_silu = [0] * NT
    ds_conv = st["ds_conv"]

    for m in range(NT if A1_LVL >= 1 else 0):
        b = m % 2
        w = [("TE", moe_fin["TE"]), ("VE", moe_fin["VE"])] if m < 2 else [("TE", te_mm[m - 2])]
        dw_win[m] = em.op("gpsimd", lambda m=m, b=b: nc.gpsimd.dma_start(
            out=winb[b][:, :].rearrange("p (c w) -> p c w", w=128),
            in_=st["dW_in"][:, 128 * m:128 * (m + 1)].rearrange("(c p) w -> p c w", p=128)),
            wait=w, inc="DW", n=16)
        wmm = [("DW", dw_win[m]), ("SE", se_evac[m - 2] if m >= 2 else 0)]
        if m < 2:
            wmm += [("VE", moe_fin["VE"]), ("SE", moe_fin["SE"])]
        for n in range(4):
            for kc in range(KD):
                te = em.op("tensor", lambda m=m, b=b, n=n, kc=kc: nc.tensor.matmul(
                    pxz[b][:, 512 * n:512 * (n + 1)],
                    winb[b][:, 128 * kc:128 * (kc + 1)],
                    xnf[:, 2048 * kc + 512 * n:2048 * kc + 512 * (n + 1)],
                    start=(kc == 0), stop=(kc == KD - 1)),
                    wait=wmm if (n == 0 and kc == 0) else (),
                    inc="TE" if (n == 3 and kc == KD - 1) else None)
        te_mm[m] = te
        se_evac[m] = em.op("scalar", lambda b=b: nc.scalar.activation(
            xzst[b][:, :], pxz[b][:, :], AF.Copy),
            wait=[("TE", te), ("VE", ve_taps[m - 2] if m >= 2 else 0)], inc="SE")
        if A1_LVL < 2:
            ve_taps[m] = em.op("vector", lambda b=b: nc.vector.tensor_copy(
                cvacc[:, 0:8], xzst[b][:, 0:8]), wait=[("SE", se_evac[m])], inc="VE")
            se_silu[m] = se_evac[m]
            continue
        cw = m % 8
        wofs = 32 * (m // 8)
        col = lambda j, m=m: convw_sb[:, 8 * (m % 8) + (0 if m < 8 else 4) + j:
                                      8 * (m % 8) + (0 if m < 8 else 4) + j + 1]
        em.op("vector", lambda b=b, col=col: nc.vector.tensor_scalar(
            cvacc[:, :], xzst[b][:, :], col(1), None, ALU.mult),
            wait=[("SE", se_evac[m]), ("DS", ds_conv), ("SE", env["se_nA"])])
        em.op("vector", lambda: nc.vector.drain())
        em.op("vector", lambda b=b, col=col: nc.vector.scalar_tensor_tensor(
            cvacc[:, 1:2048], xzst[b][:, 0:2047], col(0), cvacc[:, 1:2048], ALU.mult, ALU.add))
        em.op("vector", lambda: nc.vector.drain())
        em.op("vector", lambda b=b, col=col: nc.vector.scalar_tensor_tensor(
            cvacc[:, 0:2047], xzst[b][:, 1:2048], col(2), cvacc[:, 0:2047], ALU.mult, ALU.add))
        em.op("vector", lambda: nc.vector.drain())
        ve_taps[m] = em.op("vector", lambda b=b, col=col: nc.vector.scalar_tensor_tensor(
            cvacc[:, 0:2046], xzst[b][:, 2:2048], col(3), cvacc[:, 0:2046], ALU.mult, ALU.add),
            inc="VE")
        bcol = convb_sb[:, 2 * (m % 8) + (0 if m < 8 else 1):2 * (m % 8) + (0 if m < 8 else 1) + 1]
        if m < 8:
            se_silu[m] = em.op("scalar", lambda m=m, bcol=bcol: nc.scalar.activation(
                xs[:, 2048 * m:2048 * (m + 1)], cvacc[:, :], AF.Silu, bias=bcol),
                wait=[("VE", ve_taps[m])], inc="SE")
        else:
            zb = m % 2
            se_silu[m] = em.op("scalar", lambda zb=zb, bcol=bcol: nc.scalar.activation(
                zch[zb][:, :], cvacc[:, :], AF.Silu, bias=bcol),
                wait=[("VE", ve_taps[m]), ("DS", st.get("ds_zch_%d" % (m - 2), 0))], inc="SE")
            if A1_LVL >= 3:
                st["ds_zch_%d" % m] = em.op("sync", lambda m=m, zb=zb: nc.sync.dma_start(
                    out=st["zscr"][m - 8], in_=zch[zb][:, :]),
                    wait=[("SE", se_silu[m])], inc="DS", n=16)
    st["a1_se_last"] = em.v("SE")
    st["a1_te_last"] = em.v("TE")
    st["a1_ve_last"] = em.v("VE")
    st["a1_ds_last"] = em.v("DS")

    if "xs" in st["taps"]:
        em.op("gpsimd", lambda: nc.gpsimd.dma_start(out=st["dbg"]["xs"], in_=xs[:, :]),
              wait=[("SE", st["a1_se_last"])], inc="DW", n=16)
    if "zs" in st["taps"]:
        em.op("gpsimd", lambda: nc.gpsimd.dma_start(
            out=st["dbg"]["zs"].rearrange("p (c w) -> c p w", w=2048), in_=st["zscr"][:, :, :]),
              wait=[("DS", st["a1_ds_last"])], inc="DW", n=16)
    a1.close()
    st["xnf_scope"].close()


def build_a2(st, a, env):
    nc, em = st["nc"], st["em"]
    xacc = st["xacc"]
    xs, zseg = env["xs"], env["zseg"]
    identf, i4b = env["identf"], env["i4b"]
    lms01, umi01, bdm4 = env["lms01"], env["umi01"], env["bdm4"]
    grw, nA, biasv, gnw = env["grw"], env["nA"], env["biasv"], env["gnw"]
    bbv = env["bbv"]
    ident_b = st["ident_b"]

    a2 = Scope(nc, side="right")
    wqs = [a2.sb("wqs%d" % i, [128, 2048], BF16) for i in range(2)]
    idbg = None
    q_tm = a2.sb("q_tm", [128, 1536], BF16)
    k_tm = a2.sb("k_tm", [128, 1536], BF16)
    v_tm = a2.sb("v_tm", [128, 3072], BF16)
    g_tm = a2.sb("g_tm", [128, 3072], BF16)

    betf = a2.sb("betf", [4, 256], F32)
    gam = a2.sb("gam", [4, 256], F32)
    ngam = a2.sb("ngam", [4, 256], F32)
    bex = a2.sb("bex", [4, 256], F32)
    spt = a2.sb("spt", [4, 256], F32)
    bb4c = a2.sb("bb4c", [4, 128], F32)
    dec4c = a2.sb("dec4c", [4, 128], F32)
    onesfA = a2.sb("onesfA", [1, 512], F32)
    grow = a2.sb("grow", [1, 512], F32)
    colst = a2.sb("colst", [128, 24], F32)
    bC4 = a2.sb("bC4", [128, 4], F32)
    rq = a2.sb("rq", [128, 16], F32)
    rtmp = a2.sb("rtmp", [128, 448], F32)
    kf_s = a2.sb("kf_s", [128, 1024], BF16)
    qf_s = a2.sb("qf_s", [128, 1024], BF16)
    qbf_s = a2.sb("qbf_s", [128, 1024], BF16)
    qb_tm = a2.sb("qb_tm", [128, 768], BF16)
    A4 = a2.sb("A4", [128, 512], BF16)
    AT4 = a2.sb("AT4", [128, 512], BF16)
    X4 = a2.sb("X4", [128, 512], BF16)
    dtmp = a2.sb("dtmp", [128, 128], F32)
    Dsf = a2.sb("Dsf", [128, 128], F32)
    MT4 = a2.sb("MT4", [128, 512], BF16)
    ktl = a2.sb("ktl", [128, 192], BF16)
    kdec = a2.sb("kdec", [128, 192], BF16)
    bVb = a2.sb("bVb", [128, 384], BF16)
    Ub = a2.sb("Ub", [128, 384], BF16)
    Wnegb = a2.sb("Wnegb", [128, 256], BF16)
    Sa = [a2.sb("Sa%d" % h, [128, 384], F32) for h in range(4)]
    Sb = [a2.sb("Sb%d" % h, [64, 384], F32) for h in range(4)]
    S0bA = a2.sb("S0bA", [128, 4 * 384], BF16)
    S0bB = a2.sb("S0bB", [64, 4 * 384], BF16)
    sgb = a2.sb("sgb", [128, 384], BF16)
    sgg = sgb
    obf = a2.sb("obf", [128, 384], F32)
    ro = a2.sb("ro", [128, 4], F32)
    o4tm = a2.sb("o4tm", [128, 1536], BF16)
    ofm2 = a2.sb("ofm2", [128, 3072], BF16)
    ytmp = o4tm
    yfm2 = a2.sb("yfm2", [128, 2048], BF16)
    wsob = [a2.sb("wsob%d" % i, [128, 1024], BF16) for i in range(4)]
    P2a = a2.ps("P2a", [128, 1024], F32)
    P2b = a2.ps("P2b", [128, 1024], F32)
    P1a = a2.ps("P1a", [128, 512], F32)
    P1b = a2.ps("P1b", [128, 512], F32)
    P1c = a2.ps("P1c", [128, 512], F32)
    PT = a2.ps("PTb", [128, 1024], BF16)

    te = em.op("tensor", lambda: nc.tensor.matmul(
        P1a[:, 0:DVD], st["onesb"][:1, :], gnw[:1, :], start=True, stop=True),
        wait=[("DW", env["ds_small"]), ("TE", st["a1_te_last"]), ("SE", st["a1_se_last"]),
              ("VE", st["a1_ve_last"])], inc="TE")
    ve_grw = em.op("vector", lambda: nc.vector.tensor_copy(grw[:, :], P1a[:, 0:DVD]),
                   wait=[("TE", te)], inc="VE")
    gz = None
    a1fence = [("SE", st["a1_se_last"]), ("VE", st["a1_ve_last"]), ("TE", st["a1_te_last"])]
    for h in range(4):
        em.op("gpsimd", lambda h=h: nc.gpsimd.memset(Sa[h][:, :], 0.0),
              wait=a1fence if h == 0 else ())
        em.op("gpsimd", lambda h=h: nc.gpsimd.memset(Sb[h][:, :], 0.0))
    em.op("gpsimd", lambda: nc.gpsimd.memset(S0bA[:, :], 0.0))
    em.op("gpsimd", lambda: nc.gpsimd.memset(S0bB[:, :], 0.0))
    gz0 = em.op("gpsimd", lambda: nc.gpsimd.memset(onesfA[:, :], 1.0), inc="GC")
    gz = gz0
    st["gz_ds"] = 0

    def V(fn, wait=(), inc=None, drain=True):
        if drain:
            em.op("vector", lambda: nc.vector.drain())
        return em.op("vector", fn, wait=wait, inc=inc)

    SLICES = [(0, 512, q_tm, 0), (512, 256, q_tm, 512),
              (768, 512, k_tm, 0), (1280, 256, k_tm, 512),
              (1536, 512, v_tm, 0), (2048, 512, v_tm, 512), (2560, 512, v_tm, 1024),
              (3072, 512, g_tm, 0), (3584, 512, g_tm, 512), (4096, 512, g_tm, 1024)]
    PJ = [P1a, P1b, P1c, P2a]

    ve_sched = {"proj_evac": 0}
    st_fin = None
    prev = {
        "te_qk_last": 0,
        "ve_state": 0,
        "ve_out": 0,
        "te_out": [0, 0],
        "ve_ytmp": 0,
        "dw_wq": [0, 0],
        "te_proj": [0, 0],
        "ve_pj": [0, 0, 0, 0],
        "te_pj": [0, 0, 0, 0],
        "se_gkb": 0,
        "ve_gam": 0,
        "ds_stk": 0,
        "ve_colst": 0,
        "se_dec": 0,
        "ve_kf": 0, "ve_qf": 0, "ve_qbf": 0,
        "ve_A": 0, "ve_MT": 0, "ve_AT": 0,
        "ve_X": 0,
        "ve_Ub": 0, "ve_o4": 0,
        "te_state": 0, "te_O": 0, "te_U": 0,
        "ve_ofm": 0,
        "te_wblk": [0, 0, 0, 0],
        "te_y": [0, 0],
    }

    for s in range(NSEG_RUN):
        tau = [2 * s, 2 * s + 1]
        ds_zs = em.op("sync", lambda s=s: nc.sync.dma_start(
            out=zseg[:, :].rearrange("p (c w) -> p c w", w=256),
            in_=st["zscr"][:, :, 256 * s:256 * (s + 1)].rearrange("c p w -> p c w")),
            wait=[("TE", prev["te_out"][0]), ("TE", prev["te_out"][1]),
                  ("DS", st["a1_ds_last"] if s == 0 else 0)], inc="DS", n=16)
        prev["ds_zs"] = ds_zs
        for j, (c0, wd, dst, dbase) in enumerate(SLICES):
            dwh = [None, None]
            for half in range(2):
                bq = half
                wld = [("TE", prev["te_proj"][bq])]
                if s == 0 and j == 0:
                    wld += [("TE", st["a1_te_last"]), ("VE", st["a1_ve_last"])]
                if s == 0:
                    wld.append(("DW", st["dw_cast"]))
                dwh[half] = em.op("sync", lambda c0=c0, wd=wd, bq=bq, half=half: nc.sync.dma_start(
                    out=wqs[bq][:, 0:4 * wd].rearrange("p (c w) -> p c w", w=wd),
                    in_=st["dwq16"][512 * half:512 * (half + 1), c0:c0 + wd]
                        .rearrange("(c p) w -> p c w", p=128)),
                    wait=wld, inc="DS", n=16)
                prev["dw_wq"][bq] = dwh[half]
            for c2 in range(2):
                pj = PJ[(2 * j + c2) % 4]
                pidx = (2 * j + c2) % 4
                wmm = [("DS", dwh[0]), ("VE", prev["ve_pj"][pidx])]
                if s == 0 and j == 0:
                    wmm.append(("VE", st["a1_ve_last"]))
                    wmm.append(("SE", st["a1_se_last"]))
                if s > 0 and j < 4:
                    wmm.append(("VE", prev["ve_out"]))
                for kc in range(KD):
                    w2 = wmm if kc == 0 else ([("DS", dwh[1])] if kc == 4 else ())
                    te = em.op("tensor", lambda c2=c2, kc=kc, wd=wd, pj=pj, s=s: nc.tensor.matmul(
                        pj[:, 0:wd],
                        xs[:, 2048 * kc + 128 * (2 * s + c2):2048 * kc + 128 * (2 * s + c2) + 128],
                        wqs[kc // 4][:, wd * (kc % 4):wd * (kc % 4) + wd],
                        start=(kc == 0), stop=(kc == KD - 1)),
                        wait=w2, inc="TE" if kc == KD - 1 else None)
                prev["te_pj"][pidx] = te
                prev["ve_pj"][pidx] = em.op("vector", lambda dst=dst, dbase=dbase, c2=c2, wd=wd, pj=pj:
                    nc.vector.tensor_copy(
                        dst[:, (dst.shape[1] // 2) * c2 + dbase:(dst.shape[1] // 2) * c2 + dbase + wd],
                        pj[:, 0:wd]), wait=[("TE", te)], inc="VE")
            prev["te_proj"] = [prev["te_pj"][(2 * j + 1) % 4], prev["te_pj"][(2 * j + 1) % 4]]
        dwg = em.op("sync", lambda: nc.sync.dma_start(
            out=wqs[0][:, 0:KD * 8].rearrange("p (c w) -> p c w", w=8),
            in_=st["dwq16"][:, 4608:4616].rearrange("(c p) w -> p c w", p=128)),
            wait=([("DW", st["dw_cast"])] if s == 0 else []) + [("TE", prev["te_proj"][0])],
            inc="DS", n=16)
        wmm = [("DS", dwg), ("SE", prev["se_gkb"]), ("VE", prev["ve_out"])]
        for kc in range(KD):
            em.op("tensor", lambda kc=kc, s=s: nc.tensor.matmul(
                P2b[0:4, 0:256], wqs[0][:, 8 * kc:8 * kc + 4],
                xs[:, 2048 * kc + 256 * s:2048 * kc + 256 * s + 256],
                start=(kc == 0), stop=(kc == KD - 1)),
                wait=wmm if kc == 0 else ())
            te = em.op("tensor", lambda kc=kc, s=s: nc.tensor.matmul(
                P2b[0:4, 512:768], wqs[0][:, 8 * kc + 4:8 * kc + 8],
                xs[:, 2048 * kc + 256 * s:2048 * kc + 256 * s + 256],
                start=(kc == 0), stop=(kc == KD - 1)),
                inc="TE" if kc == KD - 1 else None)
        prev["te_proj"] = [te, te]
        em.op("scalar", lambda: nc.scalar.activation(spt[:, :], P2b[0:4, 0:256], AF.Exp,
                                                     bias=biasv[:, 0:1]),
              wait=[("TE", te), ("SE", env["se_nA"]), ("VE", prev["ve_gam"])])
        em.op("scalar", lambda: nc.scalar.activation(spt[:, :], spt[:, :], AF.Ln, bias=1.0))
        se1 = em.op("scalar", lambda: nc.scalar.activation(betf[:, :], P2b[0:4, 512:768], AF.Sigmoid,
                                                           bias=bbv[:, 0:1]), inc="SE")
        prev["se_gkb"] = se1
        gkf = spt
        V(lambda: nc.vector.tensor_scalar(gkf[:, :], spt[:, :], nA[0:4, 0:1], None, ALU.mult),
          wait=[("SE", se1)])
        for c2 in range(2):
            V(lambda c2=c2: nc.vector.tensor_tensor_scan(
                gam[:, 128 * c2:128 * (c2 + 1)], gkf[:, 128 * c2:128 * (c2 + 1)],
                gkf[:, 128 * c2:128 * (c2 + 1)], 0.0, ALU.add, ALU.bypass))
        ve_gam = V(lambda: nc.vector.tensor_scalar(ngam[:, :], gam[:, :], -1.0, None, ALU.mult),
                   inc="VE")
        se_bex = em.op("scalar", lambda: nc.scalar.activation(bex[:, :], gam[:, :], AF.Exp),
                       wait=[("VE", ve_gam)], inc="SE")
        prev["ve_gam"] = ve_gam
        ds_gw = em.op("sync", lambda: nc.sync.dma_start(out=st["gscr"][:, :], in_=gam[:, :]),
                      wait=[("VE", ve_gam), ("DS", prev.get("ds_stk", 0))], inc="DS", n=16)

        for c2 in range(2):
            tt_ = tau[c2]
            ofmbase = 1536 * c2
            wstk = [("DS", ds_gw), ("GC", gz), ("TE", prev.get("te_gm", 0))]
            ds_stk = em.op("sync", lambda c2=c2: nc.sync.dma_start(
                out=grow[0:1, :].rearrange("p (h t) -> p h t", t=128),
                in_=st["gscr"][:, 128 * c2:128 * (c2 + 1)]),
                wait=wstk, inc="DS", n=16)
            prev["ds_stk"] = ds_stk
            ve_bb = V(lambda c2=c2: nc.vector.tensor_tensor(
                bb4c[:, :], betf[:, 128 * c2:128 * (c2 + 1)], bex[:, 128 * c2:128 * (c2 + 1)],
                ALU.mult), wait=[("SE", se_bex)], inc="VE")
            se_dec = em.op("scalar", lambda c2=c2: nc.scalar.activation(
                dec4c[:, :], gam[:, 128 * c2:128 * (c2 + 1)], AF.Exp, scale=-1.0,
                bias=gam[:, 128 * c2 + 127:128 * c2 + 128]), wait=[("SE", se_bex)], inc="SE")
            wtr = [("VE", ve_bb), ("SE", se_dec), ("VE", prev["ve_state"]), ("TE", prev["te_state"])]
            srcs = [betf[:, 128 * c2:128 * (c2 + 1)], bex[:, 128 * c2:128 * (c2 + 1)],
                    bb4c[:, :], dec4c[:, :], gam[:, 128 * c2:128 * (c2 + 1)],
                    ngam[:, 128 * c2:128 * (c2 + 1)]]
            for i, src in enumerate(srcs):
                te = em.op("tensor", lambda src=src, i=i: nc.tensor.transpose(
                    P1c[:, 4 * i:4 * (i + 1)], src, identf[:4, :4]),
                    wait=wtr if i == 0 else (), inc="TE" if i == 5 else None)
            ve_colst = V(lambda: nc.vector.tensor_copy(colst[:, :], P1c[:, 0:24]),
                         wait=[("TE", te)], inc="VE", drain=False)
            V(lambda: nc.vector.tensor_tensor(bC4[:, :], colst[:, 4:8], colst[:, 12:16], ALU.mult))
            prev["ve_colst"] = ve_colst
            qsl = q_tm[:, 768 * c2:768 * (c2 + 1)]
            ksl = k_tm[:, 768 * c2:768 * (c2 + 1)]
            wsq0 = [("VE", prev["ve_pj"][0]), ("VE", prev["ve_pj"][1]),
                    ("VE", prev["ve_pj"][2]), ("VE", prev["ve_pj"][3])]
            for h in range(4):
                V(lambda h=h, qsl=qsl: nc.vector.tensor_tensor(
                    rtmp[:, 0:192], qsl[:, 192 * h:192 * (h + 1)], qsl[:, 192 * h:192 * (h + 1)],
                    ALU.mult), wait=wsq0 if h == 0 else (), drain=True)
                V(lambda h=h: nc.vector.reduce_sum(rq[:, h:h + 1], rtmp[:, 0:192], axis=AXX),
                  drain=True)
            for h in range(4):
                V(lambda h=h, ksl=ksl: nc.vector.tensor_tensor(
                    rtmp[:, 0:192], ksl[:, 192 * h:192 * (h + 1)], ksl[:, 192 * h:192 * (h + 1)],
                    ALU.mult), drain=True)
                ve_n = V(lambda h=h: nc.vector.reduce_sum(rq[:, 4 + h:5 + h], rtmp[:, 0:192], axis=AXX),
                         drain=True, inc="VE" if h == 3 else None)
            em.op("scalar", lambda: nc.scalar.activation(rq[:, 8:16], rq[:, 0:8], AF.Ln,
                                                         bias=st["epss"][:, 1:2]),
                  wait=[("VE", ve_n)])
            se_rq = em.op("scalar", lambda: nc.scalar.activation(rq[:, 0:8], rq[:, 8:16], AF.Exp,
                                                                 scale=-0.5), inc="SE")
            for h in range(4):
                V(lambda h=h, qsl=qsl: nc.vector.tensor_scalar(
                    qsl[:, 192 * h:192 * (h + 1)], qsl[:, 192 * h:192 * (h + 1)],
                    rq[:, h:h + 1], None, ALU.mult), wait=[("SE", se_rq)] if h == 0 else (),
                    drain=(h == 0))
            for h in range(4):
                ve_nk = V(lambda h=h, ksl=ksl: nc.vector.tensor_scalar(
                    ksl[:, 192 * h:192 * (h + 1)], ksl[:, 192 * h:192 * (h + 1)],
                    rq[:, 4 + h:5 + h], None, ALU.mult), drain=False,
                    inc="VE" if h == 3 else None)
            for h in range(4):
                ve_qb = V(lambda h=h, qsl=qsl: nc.vector.tensor_scalar(
                    qb_tm[:, 192 * h:192 * (h + 1)], qsl[:, 192 * h:192 * (h + 1)],
                    colst[:, 4 + h:5 + h], None, ALU.mult), drain=(h == 0),
                    inc="VE" if h == 3 else None)
            for name, src, dst in (("ve_kf", ksl, kf_s), ("ve_qf", qsl, qf_s), ("ve_qbf", qb_tm, qbf_s)):
                wtp = [("VE", ve_nk if name != "ve_qbf" else ve_qb), ("VE", ve_grw)]
                if name == "ve_kf":
                    wtp.append(("VE", prev["ve_ofm"]))
                else:
                    wtp.append(("VE", prev[{"ve_qf": "ve_kf", "ve_qbf": "ve_qf"}[name]]))
                nt_ = 0
                for h4 in range(4):
                    for (col, nr, loc) in head_kslices(h4):
                        te = em.op("tensor", lambda src=src, h4=h4, col=col, nr=nr, loc=loc: nc.tensor.transpose(
                            PT[0:nr, col:col + 128], src[:, 192 * h4 + loc:192 * h4 + loc + nr],
                            ident_b[:, :]),
                            wait=wtp if nt_ == 0 else (), inc="TE" if nt_ == 7 else None)
                        nt_ += 1
                prev[name] = em.op("vector", lambda dst=dst: nc.vector.tensor_copy(
                    dst[:, :], PT[:, 0:1024]), wait=[("TE", te)], inc="VE")
            for h in range(4):
                ks_ = head_kslices(h)
                wk2 = [("VE", prev["ve_kf"]), ("VE", prev["ve_A"])]
                te_k2 = None
                for (cc, nr, loc) in ks_:
                    te_k2 = em.op("tensor", lambda cc=cc, nr=nr, first=(te_k2 is None): nc.tensor.matmul(
                        P1a[:, 0:128], kf_s[0:nr, cc:cc + 128], kf_s[0:nr, cc:cc + 128],
                        start=first, stop=not first),
                        wait=wk2 if te_k2 is None else (), inc="TE")
                wqk = [("VE", prev["ve_qf"]), ("VE", prev["ve_MT"])]
                te_qk = None
                for (cc, nr, loc) in ks_:
                    te_qk = em.op("tensor", lambda cc=cc, nr=nr, first=(te_qk is None): nc.tensor.matmul(
                        P1b[:, 0:128], kf_s[0:nr, cc:cc + 128], qf_s[0:nr, cc:cc + 128],
                        start=first, stop=not first),
                        wait=wqk if te_qk is None else (), inc="TE")
                prev["te_qk_last"] = te_qk
                wgm = [("DS", prev["ds_stk"]), ("VE", prev["ve_colst"])]
                te_gm = em.op("tensor", lambda h=h: nc.tensor.matmul(
                    P1c[:, 128:256], onesfA[:1, 0:128], grow[0:1, 128 * h:128 * (h + 1)],
                    start=True, stop=True), wait=wgm, inc="TE")
                prev["te_gm"] = te_gm
                V(lambda h=h: nc.vector.tensor_scalar(
                    dtmp[:, :], P1c[:, 128:256], -1.0, colst[:, 16 + h:17 + h],
                    ALU.mult, ALU.add), wait=[("TE", te_gm)], drain=False)
                ve_d = V(lambda: nc.vector.scalar_tensor_tensor(
                    dtmp[:, :], dtmp[:, :], 40.0, lms01[:, :], ALU.add, ALU.mult),
                    drain=True, inc="VE")
                se_ds = em.op("scalar", lambda: nc.scalar.activation(
                    Dsf[:, :], dtmp[:, :], AF.Exp, bias=st["epss"][:, 2:3]),
                    wait=[("VE", ve_d)], inc="SE")
                ve_A = V(lambda h=h: nc.vector.scalar_tensor_tensor(
                    A4[:, 128 * h:128 * (h + 1)], P1a[:, 0:128], colst[:, h:h + 1], Dsf[:, :],
                    ALU.mult, ALU.mult), wait=[("SE", se_ds), ("TE", te_k2)], inc="VE", drain=False)
                prev["ve_A"] = ve_A
                if "invd" in st["taps"] and s == 0 and c2 == 1 and h == 3:
                    idbg = v_tm
                    V(lambda: nc.vector.tensor_copy(idbg[:, 0:128], P1c[:, 128:256]), drain=True)
                    V(lambda: nc.vector.tensor_copy(idbg[:, 128:256], Dsf[:, :]), drain=False)
                    V(lambda: nc.vector.tensor_copy(idbg[:, 256:384], P1a[:, 0:128]), drain=False)
                    V(lambda h=h: nc.vector.tensor_copy(idbg[:, 384:512], A4[:, 128 * h:128 * (h + 1)]),
                      drain=False)
                    ve_idbg = V(lambda: nc.vector.tensor_copy(idbg[:, 512:536], colst[:, :]),
                                drain=False, inc="VE")
                    em.op("gpsimd", lambda: nc.gpsimd.dma_start(
                        out=st["dbg"]["invd"], in_=idbg[:, 0:2176]),
                        wait=[("VE", ve_idbg)], inc="DW", n=16)
                    prev["ve_MT"] = ve_idbg
                V(lambda h=h: nc.vector.tensor_scalar(
                    dtmp[:, :], P1c[:, 128:256], 1.0, colst[:, 20 + h:21 + h],
                    ALU.mult, ALU.add), wait=[("SE", se_ds)], drain=False)
                ve_d2 = V(lambda: nc.vector.scalar_tensor_tensor(
                    dtmp[:, :], dtmp[:, :], 40.0, umi01[:, :], ALU.add, ALU.mult),
                    drain=True, inc="VE")
                se_di = em.op("scalar", lambda: nc.scalar.activation(
                    Dsf[:, :], dtmp[:, :], AF.Exp, bias=st["epss"][:, 2:3]),
                    wait=[("VE", ve_d2)], inc="SE")
                ve_MT = V(lambda h=h: nc.vector.tensor_tensor(
                    MT4[:, 128 * h:128 * (h + 1)], P1b[:, 0:128], Dsf[:, :], ALU.mult),
                    wait=[("SE", se_di), ("TE", te_qk)], inc="VE", drain=False)
                prev["ve_MT"] = ve_MT
                te_at = em.op("tensor", lambda h=h: nc.tensor.transpose(
                    PT[:, 0:128], A4[:, 128 * h:128 * (h + 1)], ident_b[:, :]),
                    wait=[("VE", ve_A), ("VE", prev["ve_qbf"] if h == 0 else prev["ve_AT"])],
                    inc="TE")
                prev["ve_AT"] = em.op("vector", lambda h=h: nc.vector.tensor_copy(
                    AT4[:, 128 * h:128 * (h + 1)], PT[:, 0:128]), wait=[("TE", te_at)], inc="VE")
            Db4 = kf_s[:, 0:512]
            DbT4 = kf_s[:, 512:1024]
            F4 = qb_tm[:, 0:512]
            Hb = qf_s[:, 0:512]
            Gb = qf_s[:, 512:1024]
            XC = o4tm[:, 0:512]
            Tdb = kf_s[:, 0:512]
            V(lambda: nc.vector.tensor_tensor(Db4, A4[:, :], bdm4[:, :], ALU.mult),
              wait=[("VE", prev["ve_A"]), ("TE", prev["te_qk_last"]), ("VE", prev["ve_MT"])],
              drain=False)
            V(lambda: nc.vector.tensor_tensor(DbT4, AT4[:, :], bdm4[:, :], ALU.mult),
              wait=[("VE", prev["ve_AT"])], drain=False)
            V(lambda: nc.vector.tensor_tensor(F4, A4[:, :], Db4, ALU.subtract), drain=True)
            ve_X = V(lambda: nc.vector.tensor_tensor(X4[:, :], i4b[:, 0:512], DbT4, ALU.subtract),
                     drain=True, inc="VE")
            ve_sq = ve_X
            for j in range(4):
                wpow = [("VE", ve_sq)]
                if j == 0:
                    wpow.append(("VE", prev["ve_pj"][3]))
                te_pow = None
                for h in range(4):
                    te_pow = em.op("tensor", lambda h=h: nc.tensor.matmul(
                        P2a[:, 128 * h:128 * (h + 1)], DbT4[:, 128 * h:128 * (h + 1)],
                        Db4[:, 128 * h:128 * (h + 1)], start=True, stop=True),
                        wait=wpow if h == 0 else (), inc="TE" if h == 3 else None)
                if j < 3:
                    for h in range(4):
                        te_pow = em.op("tensor", lambda h=h: nc.tensor.matmul(
                            P2a[:, 512 + 128 * h:512 + 128 * (h + 1)], Db4[:, 128 * h:128 * (h + 1)],
                            DbT4[:, 128 * h:128 * (h + 1)], start=True, stop=True),
                            inc="TE" if h == 3 else None)
                if j < 3:
                    V(lambda: nc.vector.tensor_copy(Db4, P2a[:, 0:512]),
                      wait=[("TE", te_pow)], drain=False)
                    ve_sq = V(lambda: nc.vector.tensor_copy(DbT4, P2a[:, 512:1024]),
                              drain=False, inc="VE")
                else:
                    ve_sq = V(lambda: nc.vector.tensor_copy(Db4, P2a[:, 0:512]),
                              wait=[("TE", te_pow)], drain=False, inc="VE")
                te_prod = None
                for h in range(4):
                    te_prod = em.op("tensor", lambda h=h: nc.tensor.matmul(
                        P2b[:, 128 * h:128 * (h + 1)], Db4[:, 128 * h:128 * (h + 1)],
                        X4[:, 128 * h:128 * (h + 1)], start=True, stop=True),
                        wait=[("VE", ve_sq), ("VE", ve_X)] if h == 0 else (),
                        inc="TE" if h == 3 else None)
                ve_X = V(lambda: nc.vector.tensor_tensor(X4[:, :], X4[:, :], P2b[:, 0:512], ALU.add),
                         wait=[("TE", te_prod)], drain=False, inc="VE")
            te_H = None
            for h in range(4):
                te_H = em.op("tensor", lambda h=h: nc.tensor.matmul(
                    P2b[:, 128 * h:128 * (h + 1)], F4[:, 128 * h:128 * (h + 1)],
                    X4[:, 128 * h:128 * (h + 1)], start=True, stop=True),
                    wait=[("VE", ve_X)] if h == 0 else (), inc="TE" if h == 3 else None)
            te_G = None
            for h in range(4):
                te_G = em.op("tensor", lambda h=h: nc.tensor.matmul(
                    P2a[:, 128 * h:128 * (h + 1)], X4[:, 128 * h:128 * (h + 1)],
                    F4[:, 128 * h:128 * (h + 1)], start=True, stop=True),
                    inc="TE" if h == 3 else None)
            V(lambda: nc.vector.tensor_copy(Hb, P2b[:, 0:512]), wait=[("TE", te_H), ("TE", te_G)],
              drain=False)
            ve_HG = V(lambda: nc.vector.tensor_copy(Gb, P2a[:, 0:512]), drain=False, inc="VE")
            te_h2 = None
            for h in range(4):
                te_h2 = em.op("tensor", lambda h=h: nc.tensor.matmul(
                    P2b[:, 512 + 128 * h:512 + 128 * (h + 1)], Gb[:, 128 * h:128 * (h + 1)],
                    Hb[:, 128 * h:128 * (h + 1)], start=True, stop=True),
                    wait=[("VE", ve_HG)] if h == 0 else (), inc="TE" if h == 3 else None)
            te_g2 = None
            for h in range(4):
                te_g2 = em.op("tensor", lambda h=h: nc.tensor.matmul(
                    P2a[:, 512 + 128 * h:512 + 128 * (h + 1)], Hb[:, 128 * h:128 * (h + 1)],
                    Gb[:, 128 * h:128 * (h + 1)], start=True, stop=True),
                    inc="TE" if h == 3 else None)
            ve_g2 = V(lambda: nc.vector.tensor_copy(F4, P2a[:, 512:1024]),
                      wait=[("TE", te_g2)], drain=False, inc="VE")
            te_h3 = None
            for h in range(4):
                te_h3 = em.op("tensor", lambda h=h: nc.tensor.matmul(
                    P2a[:, 128 * h:128 * (h + 1)], F4[:, 128 * h:128 * (h + 1)],
                    Hb[:, 128 * h:128 * (h + 1)], start=True, stop=True),
                    wait=[("VE", ve_g2)] if h == 0 else (), inc="TE" if h == 3 else None)
            V(lambda: nc.vector.tensor_tensor(XC, i4b[:, 0:512], Hb, ALU.subtract),
              wait=[("TE", te_h3)], drain=True)
            V(lambda: nc.vector.tensor_tensor(XC, XC, P2b[:, 512:1024], ALU.add), drain=True)
            ve_xc = V(lambda: nc.vector.tensor_tensor(XC, XC, P2a[:, 0:512], ALU.subtract),
                      drain=True, inc="VE")
            te_tdt = None
            for h in range(4):
                te_tdt = em.op("tensor", lambda h=h: nc.tensor.transpose(
                    PT[:, 128 * h:128 * (h + 1)], X4[:, 128 * h:128 * (h + 1)], ident_b[:, :]),
                    wait=[("VE", ve_xc)] if h == 0 else (), inc="TE" if h == 3 else None)
            ve_td = V(lambda: nc.vector.tensor_copy(Tdb, PT[:, 0:512]),
                      wait=[("TE", te_tdt)], drain=False, inc="VE")
            te_fin = None
            for h in range(4):
                te_fin = em.op("tensor", lambda h=h: nc.tensor.matmul(
                    P2b[:, 128 * h:128 * (h + 1)], Tdb[:, 128 * h:128 * (h + 1)],
                    XC[:, 128 * h:128 * (h + 1)], start=True, stop=True),
                    wait=[("VE", ve_td)] if h == 0 else (), inc="TE" if h == 3 else None)
            ve_X = V(lambda: nc.vector.tensor_copy(X4[:, :], P2b[:, 0:512]),
                     wait=[("TE", te_fin)], drain=False, inc="VE")
            for h in range(4):
                ks_ = head_kslices(h)
                vsl = v_tm[:, 1536 * c2 + 384 * h:1536 * c2 + 384 * (h + 1)]
                gsl = g_tm[:, 1536 * c2 + 384 * h:1536 * c2 + 384 * (h + 1)]
                ve_bv = V(lambda vsl=vsl, h=h: nc.vector.tensor_scalar(
                    bVb[:, :], vsl, colst[:, h:h + 1], None, ALU.mult),
                    wait=[("VE", ve_X), ("TE", prev["te_U"])], inc="VE", drain=True)
                V(lambda h=h, ksl=ksl: nc.vector.tensor_scalar(
                    ktl[:, :], ksl[:, 192 * h:192 * (h + 1)], colst[:, 8 + h:9 + h], None, ALU.mult),
                  drain=False)
                ve_kd = V(lambda h=h, ksl=ksl: nc.vector.tensor_scalar(
                    kdec[:, :], ksl[:, 192 * h:192 * (h + 1)], colst[:, 12 + h:13 + h], None, ALU.mult),
                    drain=False, inc="VE")
                te_u1 = em.op("tensor", lambda h=h: nc.tensor.matmul(
                    P1a[:, 0:384], X4[:, 128 * h:128 * (h + 1)], bVb[:, :], start=True, stop=False),
                    wait=[("VE", ve_bv), ("VE", prev["ve_Ub"])], inc="TE")
                em.op("tensor", lambda h=h: nc.tensor.matmul(
                    P1c[:, 0:128], ktl[:, 0:128], X4[:, 128 * h:128 * (h + 1)],
                    start=True, stop=True),
                    wait=[("VE", ve_kd), ("TE", prev["te_state"]), ("VE", prev["ve_state"])])
                te_w = em.op("tensor", lambda h=h: nc.tensor.matmul(
                    P1c[0:64, 128:256], ktl[:, 128:192], X4[:, 128 * h:128 * (h + 1)],
                    start=True, stop=True), inc="TE")
                ve_wn = V(lambda: nc.vector.tensor_scalar(
                    Wnegb[:, :], P1c[:, 0:256], -1.0, None, ALU.mult),
                    wait=[("TE", te_w)], inc="VE", drain=False)
                te_u = em.op("tensor", lambda h=h: nc.tensor.matmul(
                    P1a[:, 0:384], Wnegb[:, 0:128], S0bA[:, 384 * h:384 * (h + 1)],
                    start=False, stop=False),
                    wait=[("VE", ve_wn), ("VE", prev["ve_state"])])
                te_u = em.op("tensor", lambda h=h: nc.tensor.matmul(
                    P1a[:, 0:384], Wnegb[0:64, 128:256], S0bB[:, 384 * h:384 * (h + 1)],
                    start=False, stop=True), inc="TE")
                prev["te_U"] = te_u
                ve_ub = V(lambda: nc.vector.tensor_copy(Ub[:, :], P1a[:, 0:384]),
                          wait=[("TE", te_u)], inc="VE", drain=False)
                prev["ve_Ub"] = ve_ub
                te_o = em.op("tensor", lambda h=h: nc.tensor.matmul(
                    P1b[:, 0:384], MT4[:, 128 * h:128 * (h + 1)], Ub[:, :], start=True, stop=False),
                    wait=[("VE", ve_ub), ("VE", prev["ve_o4"])])
                for i, (cc, nr, loc) in enumerate(ks_):
                    s0 = S0bA if loc == 0 else S0bB
                    te_o = em.op("tensor", lambda cc=cc, nr=nr, h=h, i=i, s0=s0: nc.tensor.matmul(
                        P1b[:, 0:384], qbf_s[0:nr, cc:cc + 128],
                        s0[0:nr, 384 * h:384 * (h + 1)],
                        start=False, stop=(i == 1)), inc="TE" if i == 1 else None)
                prev["te_O"] = te_o
                te_s1 = em.op("tensor", lambda h=h: nc.tensor.matmul(
                    P2b[:, 512:896], kdec[:, 0:128], Ub[:, :], start=True, stop=True),
                    wait=[("VE", ve_ub)])
                te_s2 = em.op("tensor", lambda h=h: nc.tensor.matmul(
                    P1c[0:64, 0:384], kdec[:, 128:192], Ub[:, :], start=True, stop=True),
                    wait=[("VE", ve_wn)], inc="TE")
                prev["te_state"] = te_s2
                V(lambda h=h: nc.vector.scalar_tensor_tensor(
                    Sa[h][:, :], Sa[h][:, :], bC4[:, h:h + 1], P2b[:, 512:896], ALU.mult, ALU.add),
                  wait=[("TE", te_s2)], drain=True)
                V(lambda h=h: nc.vector.scalar_tensor_tensor(
                    Sb[h][:, :], Sb[h][:, :], bC4[0:64, h:h + 1], P1c[0:64, 0:384], ALU.mult, ALU.add),
                  drain=False)
                V(lambda h=h: nc.vector.tensor_copy(S0bA[:, 384 * h:384 * (h + 1)], Sa[h][:, :]),
                  drain=True)
                ve_st = V(lambda h=h: nc.vector.tensor_copy(S0bB[:, 384 * h:384 * (h + 1)], Sb[h][:, :]),
                          drain=False, inc="VE")
                prev["ve_state"] = ve_st
                se_sg = em.op("scalar", lambda gsl=gsl: nc.scalar.activation(sgb[:, :], gsl, AF.Silu),
                              wait=[("VE", prev["ve_pj"][3])], inc="SE")
                V(lambda: nc.vector.tensor_tensor(sgg[:, :], sgb[:, :], grw[:, :], ALU.mult),
                  wait=[("SE", se_sg), ("VE", ve_grw)], drain=False)
                V(lambda: nc.vector.tensor_copy(obf[:, :], P1b[:, 0:384]),
                  wait=[("TE", te_o)], drain=False)
                V(lambda: nc.vector.tensor_tensor(rtmp[:, 0:384], obf[:, :], obf[:, :], ALU.mult),
                  drain=True)
                ve_ro = V(lambda: nc.vector.reduce_sum(ro[:, 0:1], rtmp[:, 0:384], axis=AXX),
                          drain=True, inc="VE")
                em.op("scalar", lambda: nc.scalar.activation(ro[:, 1:2], ro[:, 0:1], AF.Ln,
                                                             scale=1.0 / DVD, bias=st["epss"][:, 0:1]),
                      wait=[("VE", ve_ro)])
                se_ro = em.op("scalar", lambda: nc.scalar.activation(ro[:, 2:3], ro[:, 1:2], AF.Exp,
                                                                     scale=-0.5), inc="SE")
                ve_o4 = V(lambda h=h: nc.vector.scalar_tensor_tensor(
                    o4tm[:, 384 * h:384 * (h + 1)], obf[:, :], ro[:, 2:3], sgg[:, :],
                    ALU.mult, ALU.mult), wait=[("SE", se_ro)], inc="VE", drain=True)
                prev["ve_o4"] = ve_o4
            wtp = [("VE", prev["ve_o4"]), ("VE", prev["ve_AT"])]
            for i in range(8):
                te = em.op("tensor", lambda i=i: nc.tensor.transpose(
                    PT[:, 128 * i:128 * (i + 1)], o4tm[:, 128 * i:128 * (i + 1)], ident_b[:, :]),
                    wait=wtp if i == 0 else (), inc="TE" if i == 7 else None)
            ve_of = em.op("vector", lambda ofmbase=ofmbase: nc.vector.tensor_copy(
                ofm2[:, ofmbase:ofmbase + 1024], PT[:, 0:1024]), wait=[("TE", te)], inc="VE")
            for i in range(4):
                te = em.op("tensor", lambda i=i: nc.tensor.transpose(
                    PT[:, 128 * i:128 * (i + 1)], o4tm[:, 1024 + 128 * i:1024 + 128 * (i + 1)],
                    ident_b[:, :]), wait=[("VE", ve_of)] if i == 0 else (),
                    inc="TE" if i == 3 else None)
            ve_of = em.op("vector", lambda ofmbase=ofmbase: nc.vector.tensor_copy(
                ofm2[:, ofmbase + 1024:ofmbase + 1536], PT[:, 0:512]), wait=[("TE", te)], inc="VE")
            prev["ve_ofm"] = ve_of
        PY = [P2a, P2b]
        te_y = [None, None]
        te_wblk = prev["te_wblk"]

        def wload(src, kc, bq, s=s):
            w = [("TE", te_wblk[bq])]
            if s == 0:
                w.append(("DW", st["dw_cast"]))
            return em.op("sync", lambda src=src, kc=kc, bq=bq: nc.sync.dma_start(
                out=wsob[bq][:, :], in_=src[128 * kc:128 * (kc + 1), :]),
                wait=w, inc="DS", n=16)

        ds_wso = [0] * 12
        for kc in range(4):
            ds_wso[kc] = wload(st["dso16"], kc, kc)
        for kc in range(12):
            bq = kc % 4
            for c2 in range(2):
                wmm = []
                if c2 == 0:
                    wmm.append(("DS", ds_wso[kc]))
                if kc == 0:
                    wmm += [("VE", prev["ve_ofm"]), ("VE", ve_sq), ("TE", te_prod),
                            ("VE", prev["ve_Ub"]), ("TE", prev["te_state"]), ("TE", prev["te_O"]),
                            ("VE", prev["ve_state"])]
                for nh in range(2):
                    te_y[c2] = em.op("tensor", lambda kc=kc, c2=c2, nh=nh, bq=bq: nc.tensor.matmul(
                        PY[c2][:, 512 * nh:512 * (nh + 1)],
                        ofm2[:, 1536 * c2 + 128 * kc:1536 * c2 + 128 * (kc + 1)],
                        wsob[bq][:, 512 * nh:512 * (nh + 1)],
                        start=(kc == 0), stop=(kc == 11)),
                        wait=wmm if (nh == 0 and wmm) else (), inc="TE" if nh == 1 else None)
            te_wblk[bq] = em.v("TE")
            if kc + 4 < 12:
                ds_wso[kc + 4] = wload(st["dso16"], kc + 4, (kc + 4) % 4)
        prev["te_y"] = [em.v("TE"), em.v("TE")]
        ve_yf = 0
        for c2 in range(2):
            ve_yt = em.op("vector", lambda c2=c2: nc.vector.tensor_copy(ytmp[:, 0:1024], PY[c2][:, :]),
                          wait=[("TE", te_y[c2]), ("VE", prev["ve_ytmp"])], inc="VE")
            for i in range(8):
                te = em.op("tensor", lambda i=i: nc.tensor.transpose(
                    PT[:, 128 * i:128 * (i + 1)], ytmp[:, 128 * i:128 * (i + 1)], ident_b[:, :]),
                    wait=[("VE", ve_yt), ("VE", ve_yf)] if i == 0 else (),
                    inc="TE" if i == 7 else None)
            ve_yf = em.op("vector", lambda c2=c2: nc.vector.tensor_copy(
                yfm2[:, 1024 * c2:1024 * (c2 + 1)], PT[:, 0:1024]), wait=[("TE", te)], inc="VE")
            prev["ve_ytmp"] = ve_yf
        te_om = [None, None]
        ds_wout = [0] * 16
        for kc in range(4):
            ds_wout[kc] = wload(st["dout16"], kc, kc)
        for kc in range(16):
            bq = kc % 4
            for c2 in range(2):
                if kc < 8:
                    lhs = lambda kc=kc, c2=c2: yfm2[:, 1024 * c2 + 128 * kc:1024 * c2 + 128 * (kc + 1)]
                else:
                    lhs = lambda kc=kc, c2=c2: zseg[:, 256 * (kc - 8) + 128 * c2:
                                                    256 * (kc - 8) + 128 * c2 + 128]
                wmm = []
                if c2 == 0:
                    wmm.append(("DS", ds_wout[kc]))
                if kc == 0:
                    wmm.append(("VE", ve_yf))
                if kc == 8 and c2 == 0:
                    wmm.append(("DS", prev["ds_zs"]))
                for nh in range(2):
                    te_om[c2] = em.op("tensor", lambda kc=kc, c2=c2, nh=nh, bq=bq, lhs=lhs: nc.tensor.matmul(
                        PY[c2][:, 512 * nh:512 * (nh + 1)], lhs(),
                        wsob[bq][:, 512 * nh:512 * (nh + 1)],
                        start=(kc == 0), stop=(kc == 15)),
                        wait=wmm if (nh == 0 and wmm) else (), inc="TE" if nh == 1 else None)
            te_wblk[bq] = em.v("TE")
            if kc + 4 < 16:
                ds_wout[kc + 4] = wload(st["dout16"], kc + 4, (kc + 4) % 4)
        prev["te_out"] = [em.v("TE"), em.v("TE")]
        for c2 in range(2):
            ve_oadd = em.op("vector", lambda c2=c2, s=s: nc.vector.tensor_tensor(
                xacc[:, 1024 * (2 * s + c2):1024 * (2 * s + c2 + 1)], PY[c2][:, :],
                xacc[:, 1024 * (2 * s + c2):1024 * (2 * s + c2 + 1)], ALU.add),
                wait=[("TE", te_om[c2])], inc="VE")
        prev["ve_out"] = ve_oadd
    if "a2d" in st["taps"]:
        fin = st["em"].snap()
        pieces = [gam, ngam, bex, betf, colst, bC4, rq, q_tm, k_tm, v_tm, g_tm,
                  kf_s, qf_s, qbf_s, qb_tm, A4, AT4, X4, MT4, Ub, Wnegb, bVb, obf,
                  o4tm, ofm2, yfm2, Sa[0], Sb[0], S0bA, sgb]
        col = 0
        for i, tns in enumerate(pieces):
            w = tns.shape[1]
            em.op("gpsimd", lambda tns=tns, col=col, w=w: nc.gpsimd.dma_start(
                out=st["dbg"]["a2d"][0:tns.shape[0], col:col + w], in_=tns[:, :]),
                wait=[("VE", fin["VE"]), ("SE", fin["SE"]), ("TE", fin["TE"])] if i == 0 else (),
                inc="DW", n=16)
            col += w
        st["a2d_cols"] = col
    a2.close()
    a.close()



def kernel(**inputs):
    ins = prep_inputs(inputs)
    nc = build_nc(debug_taps=DEBUG_TAPS)
    in_maps = [dict(ins, x=np.ascontiguousarray(inputs["L0_attn_x"][b])) for b in range(8)]
    res = run_bass_kernel_spmd(nc, in_maps, core_ids=list(range(8)))
    return np.stack([res.results[b]["y"] for b in range(8)], axis=0)


def prep_inputs(inputs):
    P = {}
    P["W_in"] = np.ascontiguousarray(inputs["L0_attn_W_in"])
    P["convw"] = np.ascontiguousarray(np.concatenate(
        [inputs["L0_attn_conv_x_w"].reshape(D, 4), inputs["L0_attn_conv_z_w"].reshape(D, 4)], axis=1))
    P["convb"] = np.ascontiguousarray(np.stack(
        [inputs["L0_attn_conv_x_b"], inputs["L0_attn_conv_z_b"]], axis=1))
    P["Wqkvg"] = np.ascontiguousarray(np.concatenate(
        [inputs["L0_attn_W_q"], inputs["L0_attn_W_k"], inputs["L0_attn_W_v"],
         inputs["L0_attn_W_g"], inputs["L0_attn_W_gk"], inputs["L0_attn_W_b"]], axis=1))
    P["W_so"] = np.ascontiguousarray(inputs["L0_attn_W_so"])
    P["W_out"] = np.ascontiguousarray(inputs["L0_attn_W_out"])
    P["Alog"] = np.ascontiguousarray(inputs["L0_attn_A_log"].reshape(4, 1))
    P["dtbb"] = np.ascontiguousarray(np.concatenate(
        [inputs["L0_attn_dt_bias"], inputs["L0_attn_b_b"]]).reshape(8, 1))
    P["gnw"] = np.ascontiguousarray(inputs["L0_attn_g_norm_w"].reshape(1, DVD))
    P["rw"] = np.ascontiguousarray(inputs["L0_ffn_router_w"])
    P["corr"] = np.ascontiguousarray(inputs["L0_ffn_correction_bias"].reshape(1, NE))
    P["up_w"] = np.ascontiguousarray(inputs["L0_ffn_up_w"])
    P["down_w"] = np.ascontiguousarray(inputs["L0_ffn_down_w"])
    P["sh_up"] = np.ascontiguousarray(inputs["L0_ffn_shared_up_w"])
    P["sh_down"] = np.ascontiguousarray(inputs["L0_ffn_shared_down_w"])
    return P


# version 04
# speedup vs baseline: 1.6080x; 1.0379x over previous
import numpy as np
import concourse.bass as bass
import concourse.mybir as mybir
from concourse.bass_utils import run_bass_kernel_spmd

F32 = mybir.dt.float32
BF16 = mybir.dt.bfloat16
AF = mybir.ActivationFunctionType
ALU = mybir.AluOpType
AXX = mybir.AxisListType.X

T = 2048
D = 1024
NT = 16
KD = 8
H, DKD, DVD = 4, 192, 384
CH = 128
NSEG, SEG = 8, 256
NE = 64
NU = 68
NBUF = 6
GACC = 4
EPS_LN = 1e-5
EPS_L2 = 1e-6
EPS_GN = 1e-5

ENGS = ("tensor", "vector", "scalar", "gpsimd", "sync")


class EM:

    def __init__(self, nc):
        self.nc = nc
        self.q = {e: [] for e in ENGS}
        self.sem = {}
        self.cnt = {}
        self._cm = []

    def new_sem(self, name):
        cm = self.nc.semaphore(name)
        h = cm.__enter__()
        self._cm.append(cm)
        self.sem[name] = h
        self.cnt[name] = 0

    def op(self, eng, fn, wait=(), inc=None, n=1):
        val = None
        if inc is not None:
            self.cnt[inc] += n
            val = self.cnt[inc]
        wait = tuple((s, v) for (s, v) in wait if v)
        self.q[eng].append((fn, wait, inc, n))
        return val

    def v(self, name):
        return self.cnt[name]

    def snap(self):
        return dict(self.cnt)

    def close_sems(self):
        for cm in reversed(self._cm):
            cm.__exit__(None, None, None)

    def emit(self, block):
        for eng in ENGS:
            deco = getattr(block, eng)

            def make(eng):
                items = self.q[eng]
                sems = self.sem

                def body(handle):
                    for fn, waits, inc, n in items:
                        for (s, val) in waits:
                            handle.wait_ge(sems[s], val)
                        if fn is not None:
                            ins = fn()
                            if inc is not None:
                                ins.then_inc(sems[inc], n)
                return body

            deco(make(eng))


class Scope:
    def __init__(self, nc, side="left"):
        self.nc = nc
        self.side = side
        self._cms = []

    def sb(self, name, shape, dtype):
        cm = self.nc.sbuf_tensor(name, shape, dtype, side=self.side)
        t = cm.__enter__()
        self._cms.append(cm)
        return t

    def ps(self, name, shape, dtype=F32):
        cm = self.nc.psum_tensor(name, shape, dtype)
        t = cm.__enter__()
        self._cms.append(cm)
        return t

    def close(self):
        for cm in reversed(self._cms):
            cm.__exit__(None, None, None)
        self._cms = []


DEBUG_TAPS = ()
RUN_ATTN = True
RUN_MOE = True
MOE_NG = None
A2_ON = True
NSEG_RUN = NSEG
A1_LVL = 3


def build_nc(debug_taps=(), run_attn=None):
    run_attn = RUN_ATTN if run_attn is None else run_attn
    nc = bass.Bass("TRN2", target_bir_lowering=False)
    em = EM(nc)
    for s in ("DW", "DS", "TE", "VE", "SE", "GC", "DO"):
        em.new_sem(s)
    st = {"nc": nc, "em": em, "taps": set(debug_taps), "run_attn": run_attn}

    def din(name, shape):
        return nc.dram_tensor(name, shape, F32, kind="ExternalInput").ap()
    st["dx"] = din("x", [T, D])
    st["dW_in"] = din("W_in", [D, 2048])
    st["dconvw"] = din("convw", [D, 8])
    st["dconvb"] = din("convb", [D, 2])
    st["dWqkvg"] = din("Wqkvg", [D, 4616])
    st["dW_so"] = din("W_so", [1536, D])
    st["dW_out"] = din("W_out", [2048, D])
    st["dAlog"] = din("Alog", [4, 1])
    st["ddtbb"] = din("dtbb", [8, 1])
    st["dgnw"] = din("gnw", [1, DVD])
    st["drw"] = din("rw", [NE, D])
    st["dcorr"] = din("corr", [1, NE])
    st["dup"] = din("up_w", [NE, D, 512])
    st["ddn"] = din("down_w", [NE, 512, D])
    st["dshu"] = din("sh_up", [D, 2048])
    st["dshd"] = din("sh_down", [2048, D])
    st["dy"] = nc.dram_tensor("y", [T, D], F32, kind="ExternalOutput").ap()
    st["dbg"] = {}
    for name, shp in (("xn", [128, NT * 1024]), ("combT", [64, T]),
                      ("ffn", [128, NT * 1024]), ("xs", [128, NT * 1024]),
                      ("zs", [128, NT * 1024]), ("rs0", [128, 512]),
                      ("a2d", [128, 28672]), ("invd", [128, 2176])):
        if name in st["taps"]:
            st["dbg"][name] = nc.dram_tensor("dbg_" + name, shp, F32, kind="ExternalOutput").ap()

    g = Scope(nc)
    st["g"] = g
    st["xacc"] = g.sb("xacc", [128, NT * 1024], F32)
    st["ident_b"] = g.sb("ident_b", [128, 128], BF16)
    st["onesb"] = g.sb("onesb", [1, 128], BF16)
    st["epss"] = g.sb("epss", [128, 4], F32)
    xnfs = Scope(nc)
    st["xnf_scope"] = xnfs
    st["xnf"] = xnfs.sb("xnf", [128, NT * 1024], BF16)
    pm = Scope(nc)
    st["pm"] = pm
    st["combT"] = pm.sb("combT", [64, T], BF16)
    st["id64"] = pm.sb("id64", [64, 64], BF16)
    st["ones8"] = pm.sb("ones8", [128, 8], F32)
    st["corr_rep"] = pm.sb("corr_rep", [128, 64], F32)
    st["onesf"] = pm.sb("onesf", [1, 128], F32)
    st["zscr"] = nc.dram_tensor("zscr", [KD, 128, T], BF16, kind="Internal").ap()
    st["gscr"] = nc.dram_tensor("gscr", [4, 256], F32, kind="Internal").ap()
    st["dwq16"] = nc.dram_tensor("dwq16", [D, 4616], BF16, kind="Internal").ap()
    st["dso16"] = nc.dram_tensor("dso16", [1536, D], BF16, kind="Internal").ap()
    st["dout16"] = nc.dram_tensor("dout16", [2048, D], BF16, kind="Internal").ap()

    build_consts(st)
    build_p0(st)
    if RUN_MOE:
        build_moe(st)
    if run_attn and RUN_MOE:
        build_attn(st)
    else:
        st["fin"] = em.snap()

    fin = st["fin"]
    em.op("sync", lambda: nc.sync.dma_start(
        out=st["dy"].rearrange("(t p) j -> p t j", p=128),
        in_=st["xacc"][:, :].rearrange("p (t j) -> p t j", j=1024)),
        wait=[("VE", fin["VE"]), ("SE", fin["SE"]), ("TE", fin["TE"])], inc="DO", n=16)
    em.op("sync", None, wait=[("DO", em.v("DO"))])

    with nc.Block() as block:
        em.emit(block)
    em.close_sems()
    return nc


def build_consts(st):
    nc, em, G = st["nc"], st["em"], st
    ds0 = em.op("sync", lambda: nc.sync.dma_start(out=G["corr_rep"][:1, :], in_=G["dcorr"][:, :]),
                inc="DS", n=16)
    st["ds_corr"] = ds0

    def asel(t, cmp, fill, cm, pat, sl=None):
        ap = t[:, :] if sl is None else t[:, sl[0]:sl[1]]
        return lambda: nc.gpsimd.affine_select(out=ap, in_=ap, compare_op=cmp,
                                               fill=fill, base=0, pattern=pat, channel_multiplier=cm)
    em.op("gpsimd", lambda: nc.gpsimd.memset(G["ident_b"][:, :], 0.0))
    em.op("gpsimd", asel(G["ident_b"], ALU.not_equal, 1.0, 1, [[-1, 128]]))
    em.op("gpsimd", lambda: nc.gpsimd.memset(G["id64"][:, :], 0.0))
    em.op("gpsimd", lambda: nc.gpsimd.affine_select(out=G["id64"][:, :], in_=G["id64"][:, :],
          compare_op=ALU.not_equal, fill=1.0, base=0, pattern=[[-1, 64]], channel_multiplier=1))
    em.op("gpsimd", lambda: nc.gpsimd.memset(G["onesb"][:, :], 1.0))
    em.op("gpsimd", lambda: nc.gpsimd.memset(G["onesf"][:, :], 1.0))
    em.op("gpsimd", lambda: nc.gpsimd.memset(G["ones8"][:, :], 1.0))
    em.op("gpsimd", lambda: nc.gpsimd.memset(G["epss"][:, 0:1], EPS_LN))
    em.op("gpsimd", lambda: nc.gpsimd.memset(G["epss"][:, 1:2], EPS_L2))
    gc = em.op("gpsimd", lambda: nc.gpsimd.memset(G["epss"][:, 2:3], -40.0), inc="GC")
    st["gc_consts"] = gc


def build_p0(st):
    nc, em, G = st["nc"], st["em"], st
    xacc = st["xacc"]
    xnf = st["xnf"]
    p0 = Scope(nc)
    ident_f = p0.sb("ident_f_p0", [128, 128], F32)
    xnf32 = p0.sb("xnf32", [128, NT * 1024], F32)
    xn_t = p0.sb("xn_t", [128, 1024], F32)
    stat = p0.sb("stat", [128, 8], F32)
    rwf = p0.sb("rwf", [64, 1024], F32)
    rwT = p0.sb("rwT", [128, 512], F32)
    rs = p0.sb("rs", [128, 512], F32)
    rA = p0.sb("rA", [128, 1], F32)
    rB = p0.sb("rB", [128, 1], F32)
    cbf = p0.sb("cbf", [128, 64], BF16)
    pt0 = p0.ps("pt0", [128, 512], F32)
    pt1 = p0.ps("pt1", [128, 512], F32)
    pr0 = p0.ps("pr0", [128, 512], F32)
    pr1 = p0.ps("pr1", [128, 512], F32)
    pcb = p0.ps("pcb", [128, 128], BF16)

    em.op("gpsimd", lambda: nc.gpsimd.memset(ident_f[:, :], 0.0))
    gcf = em.op("gpsimd", lambda: nc.gpsimd.affine_select(
        out=ident_f[:, :], in_=ident_f[:, :], compare_op=ALU.not_equal,
        fill=1.0, base=0, pattern=[[-1, 128]], channel_multiplier=1), inc="GC")

    ld_x = em.op("sync", lambda: nc.sync.dma_start(
        out=xacc[:, :].rearrange("p (t j) -> p t j", j=1024),
        in_=st["dx"].rearrange("(t p) j -> p t j", p=128)), inc="DS", n=16)
    ld_rw = em.op("sync", lambda: nc.sync.dma_start(out=rwf[:, :], in_=st["drw"][:, :]), inc="DS", n=16)

    for c in range(KD):
        te = em.op("tensor", lambda c=c: nc.tensor.transpose(
            pt0[:, 64 * c:64 * (c + 1)], rwf[:, 128 * c:128 * (c + 1)], ident_f[:64, :64]),
            wait=([("DS", ld_rw), ("GC", gcf)] if c == 0 else ()), inc="TE" if c == 7 else None)
    ve_rwT = em.op("vector", lambda: nc.vector.tensor_copy(rwT[:, :], pt0[:, :]),
                   wait=[("TE", te)], inc="VE")
    te_cr = em.op("tensor", lambda: nc.tensor.matmul(
        pt1[:, 0:64], st["onesf"][:1, :], st["corr_rep"][:1, :], start=True, stop=True),
        wait=[("DS", st["ds_corr"]), ("GC", st["gc_consts"])], inc="TE")
    ve_corr = em.op("vector", lambda: nc.vector.tensor_copy(st["corr_rep"][:, :], pt1[:, 0:64]),
                    wait=[("TE", te_cr)], inc="VE")
    st["ve_corr"] = ve_corr

    S0, SF, SW, SG, SGW, SGM, SEM_, SSEL, SWS, SX = 0, 64, 128, 192, 200, 208, 216, 280, 344, 408
    te_tr2_prev = 0
    se_prev = 0
    se_sig_bank = [0, 0]
    v6_prev = 0
    tec_prev = 0
    vec_prev = 0
    ve_comb = [0] * NT
    for t in range(NT):
        xt = xacc[:, 1024 * t:1024 * (t + 1)]
        em.op("vector", lambda xt=xt: nc.vector.reduce_sum(stat[:, 0:1], xt, axis=AXX),
              wait=[("DS", ld_x), ("SE", se_prev)] if t == 0 else [("SE", se_prev)])
        em.op("vector", lambda xt=xt: nc.vector.tensor_tensor(xn_t[:, :], xt, xt, ALU.mult),
              wait=[("TE", te_tr2_prev)])
        v1 = em.op("vector", lambda: nc.vector.reduce_sum(stat[:, 2:3], xn_t[:, :], axis=AXX), inc="VE")
        em.op("scalar", lambda: nc.scalar.activation(stat[:, 1:2], stat[:, 0:1], AF.Copy, scale=-1.0 / D),
              wait=[("VE", v1)])
        s1 = em.op("scalar", lambda: nc.scalar.activation(stat[:, 3:4], stat[:, 0:1], AF.Square, scale=1.0 / D),
                   inc="SE")
        v2 = em.op("vector", lambda: nc.vector.scalar_tensor_tensor(
            stat[:, 4:5], stat[:, 2:3], 1.0 / D, stat[:, 3:4], ALU.mult, ALU.subtract),
            wait=[("SE", s1)], inc="VE")
        em.op("scalar", lambda: nc.scalar.activation(stat[:, 5:6], stat[:, 4:5], AF.Ln,
                                                     bias=st["epss"][:, 0:1]),
              wait=[("VE", v2)] + ([("GC", st["gc_consts"])] if t == 0 else []))
        s2 = em.op("scalar", lambda: nc.scalar.activation(stat[:, 6:7], stat[:, 5:6], AF.Exp, scale=-0.5),
                   inc="SE")
        se_prev = s2
        v3 = em.op("vector", lambda xt=xt: nc.vector.tensor_scalar(
            xn_t[:, :], xt, stat[:, 1:2], stat[:, 6:7], ALU.add, ALU.mult),
            wait=[("SE", s2)], inc="VE")
        pt = pt0 if t % 2 == 0 else pt1
        wtr = [("VE", v3)]
        if t == 0:
            wtr.append(("VE", ve_rwT))
        if t == 1:
            wtr.append(("VE", st["ve_corr"]))
        for c in range(4):
            te1 = em.op("tensor", lambda c=c, pt=pt: nc.tensor.transpose(
                pt[:, 128 * c:128 * (c + 1)], xn_t[:, 128 * c:128 * (c + 1)], ident_f[:, :]),
                wait=wtr if c == 0 else (), inc="TE" if c == 3 else None)
        for c in range(4):
            v4 = em.op("vector", lambda c=c, t=t, pt=pt: nc.vector.tensor_copy(
                xnf32[:, 2048 * c + 128 * t:2048 * c + 128 * (t + 1)], pt[:, 128 * c:128 * (c + 1)]),
                wait=[("TE", te1)] if c == 0 else (), inc="VE" if c == 3 else None)
        for c in range(4, 8):
            te2 = em.op("tensor", lambda c=c, pt=pt: nc.tensor.transpose(
                pt[:, 128 * (c - 4):128 * (c - 3)], xn_t[:, 128 * c:128 * (c + 1)], ident_f[:, :]),
                wait=[("VE", v4)] if c == 4 else (), inc="TE" if c == 7 else None)
        te_tr2_prev = te2
        for c in range(4, 8):
            v5 = em.op("vector", lambda c=c, t=t, pt=pt: nc.vector.tensor_copy(
                xnf32[:, 2048 * c + 128 * t:2048 * c + 128 * (t + 1)], pt[:, 128 * (c - 4):128 * (c - 3)]),
                wait=[("TE", te2)] if c == 4 else (), inc="VE" if c == 7 else None)
        pr = pr0 if t % 2 == 0 else pr1
        for c in range(KD):
            ter = em.op("tensor", lambda c=c, t=t, pr=pr: nc.tensor.matmul(
                pr[:, 0:64],
                xnf32[:, 2048 * c + 128 * t:2048 * c + 128 * (t + 1)],
                rwT[:, 64 * c:64 * (c + 1)],
                start=(c == 0), stop=(c == 7)),
                wait=[("VE", v5), ("VE", ve_rwT), ("SE", se_sig_bank[t % 2])] if c == 0 else (),
                inc="TE" if c == 7 else None)
        se_sc = em.op("scalar", lambda pr=pr: nc.scalar.activation(
            rs[:, S0:S0 + 64], pr[:, 0:64], AF.Sigmoid),
            wait=[("TE", ter), ("VE", v6_prev)], inc="SE")
        se_sig_bank[t % 2] = se_sc
        def vop(fn, wait=(), inc=None):
            em.op("vector", lambda: nc.vector.drain())
            return em.op("vector", fn, wait=wait, inc=inc)

        def eq_of(src_ap, width, dst_off):
            vop(lambda: nc.vector.tensor_scalar(
                rs[:, dst_off:dst_off + width], src_ap, rA[:, 0:1], None, ALU.subtract))
            vop(lambda: nc.vector.tensor_scalar(
                rs[:, dst_off:dst_off + width], rs[:, dst_off:dst_off + width],
                1e12, 1.0, ALU.mult, ALU.add))
            vop(lambda: nc.vector.tensor_relu(
                rs[:, dst_off:dst_off + width], rs[:, dst_off:dst_off + width]))

        vop(lambda: nc.vector.tensor_tensor(rs[:, SF:SF + 64], rs[:, S0:S0 + 64], st["corr_rep"][:, :], ALU.add),
            wait=[("SE", se_sc), ("VE", st["ve_corr"])])
        for gi in range(8):
            sfc_g = rs[:, SF + 8 * gi:SF + 8 * (gi + 1)]
            vop(lambda sfc_g=sfc_g: nc.vector.reduce_max(rA[:, 0:1], sfc_g, axis=AXX))
            eq_of(sfc_g, 8, SX + 2)
            vop(lambda sfc_g=sfc_g: nc.vector.scalar_tensor_tensor(
                rs[:, SX + 16:SX + 24], rs[:, SX + 2:SX + 10], -1e30, sfc_g, ALU.mult, ALU.add))
            vop(lambda: nc.vector.reduce_max(rB[:, 0:1], rs[:, SX + 16:SX + 24], axis=AXX))
            vop(lambda gi=gi: nc.vector.tensor_tensor(
                rs[:, SG + gi:SG + gi + 1], rA[:, 0:1], rB[:, 0:1], ALU.add))
        vop(lambda: nc.vector.tensor_copy(rs[:, SGW:SGW + 8], rs[:, SG:SG + 8]))
        vop(lambda: nc.vector.memset(rs[:, SGM:SGM + 8], 0.0))
        for _ in range(4):
            vop(lambda: nc.vector.reduce_max(rA[:, 0:1], rs[:, SGW:SGW + 8], axis=AXX))
            eq_of(rs[:, SGW:SGW + 8], 8, SX + 2)
            vop(lambda: nc.vector.tensor_tensor(
                rs[:, SGM:SGM + 8], rs[:, SGM:SGM + 8], rs[:, SX + 2:SX + 10], ALU.add))
            vop(lambda: nc.vector.scalar_tensor_tensor(
                rs[:, SGW:SGW + 8], rs[:, SX + 2:SX + 10], -1e30, rs[:, SGW:SGW + 8], ALU.mult, ALU.add))
        for gi in range(8):
            vop(lambda gi=gi: nc.vector.tensor_scalar(
                rs[:, SEM_ + 8 * gi:SEM_ + 8 * (gi + 1)], st["ones8"][:, :],
                rs[:, SGM + gi:SGM + gi + 1], None, ALU.mult))
        vop(lambda: nc.vector.tensor_tensor(rs[:, SW:SW + 64], rs[:, SF:SF + 64], rs[:, SEM_:SEM_ + 64], ALU.mult))
        vop(lambda: nc.vector.memset(rs[:, SSEL:SSEL + 64], 0.0))
        for _ in range(8):
            vop(lambda: nc.vector.reduce_max(rA[:, 0:1], rs[:, SW:SW + 64], axis=AXX))
            eq_of(rs[:, SW:SW + 64], 64, SX + 2)
            vop(lambda: nc.vector.tensor_tensor(
                rs[:, SSEL:SSEL + 64], rs[:, SSEL:SSEL + 64], rs[:, SX + 2:SX + 66], ALU.add))
            vop(lambda: nc.vector.scalar_tensor_tensor(
                rs[:, SW:SW + 64], rs[:, SX + 2:SX + 66], -1e30, rs[:, SW:SW + 64], ALU.mult, ALU.add))
        vop(lambda: nc.vector.tensor_tensor(rs[:, SWS:SWS + 64], rs[:, SSEL:SSEL + 64], rs[:, S0:S0 + 64], ALU.mult))
        vop(lambda: nc.vector.reduce_sum(rA[:, 0:1], rs[:, SWS:SWS + 64], axis=AXX))
        vop(lambda: nc.vector.tensor_scalar(rA[:, 0:1], rA[:, 0:1], 1e-20, None, ALU.add))
        vop(lambda: nc.vector.reciprocal(rB[:, 0:1], rA[:, 0:1]))
        v6 = vop(lambda: nc.vector.tensor_scalar(
            cbf[:, :], rs[:, SWS:SWS + 64], rB[:, 0:1], 2.5, ALU.mult, ALU.mult),
            wait=[("TE", tec_prev)], inc="VE")
        v6_prev = v6
        tec = em.op("tensor", lambda: nc.tensor.transpose(pcb[:64, :], cbf[:, :], st["ident_b"][:, :]),
                    wait=[("VE", v6), ("VE", vec_prev)], inc="TE")
        tec_prev = tec
        vec = em.op("vector", lambda t=t: nc.vector.tensor_copy(
            st["combT"][:, 128 * t:128 * (t + 1)], pcb[:64, :]), wait=[("TE", tec)], inc="VE")
        vec_prev = vec
        ve_comb[t] = vec
        if t == 0 and "rs0" in st["taps"]:
            em.op("gpsimd", lambda: nc.gpsimd.dma_start(out=st["dbg"]["rs0"], in_=rs[:, :]),
                  wait=[("VE", v6)], inc="DW", n=16)

    ve_xnf = em.op("vector", lambda: nc.vector.tensor_copy(xnf[:, :], xnf32[:, :]), inc="VE")
    st["ve_xnf"] = ve_xnf
    st["ve_comb"] = ve_comb
    st["te_p0_end"] = em.v("TE")
    st["ve_p0_end"] = em.v("VE")

    if "xn" in st["taps"]:
        em.op("gpsimd", lambda: nc.gpsimd.dma_start(out=st["dbg"]["xn"], in_=xnf[:, :]),
              wait=[("VE", ve_xnf)], inc="DW", n=16)
    if "combT" in st["taps"]:
        em.op("gpsimd", lambda: nc.gpsimd.dma_start(out=st["dbg"]["combT"], in_=st["combT"][:, :]),
              wait=[("VE", ve_comb[-1])], inc="DW", n=16)
    p0.close()


CAST_PIECES = (
    [lambda st, i=i: (lambda: st["nc"].gpsimd.dma_start(
        out=st["dwq16"][128 * i:128 * (i + 1), :],
        in_=st["dWqkvg"][128 * i:128 * (i + 1), :])) for i in range(8)]
    + [lambda st, i=i: (lambda: st["nc"].gpsimd.dma_start(
        out=st["dso16"][768 * i:768 * (i + 1), :],
        in_=st["dW_so"][768 * i:768 * (i + 1), :])) for i in range(2)]
    + [lambda st, i=i: (lambda: st["nc"].gpsimd.dma_start(
        out=st["dout16"][1024 * i:1024 * (i + 1), :],
        in_=st["dW_out"][1024 * i:1024 * (i + 1), :])) for i in range(2)]
)


def build_moe(st):
    nc, em, G = st["nc"], st["em"], st
    xacc, xnf, combT, id64 = st["xacc"], st["xnf"], st["combT"], st["id64"]
    m = Scope(nc)
    wu = [m.sb("wu%d" % i, [128, 4096], BF16) for i in range(NBUF)]
    wd = [m.sb("wd%d" % i, [128, 4096], BF16) for i in range(NBUF)]
    tb = m.sb("tb", [128, 1024], BF16)
    sqb = m.sb("sqb", [128, 1024], BF16)
    h2cb = [m.sb("h2cb%d" % i, [128, 1024], BF16) for i in range(2)]
    crep_sb = m.sb("crep_sb", [128, 256], BF16)
    ph = m.ps("ph", [128, 1024], F32)
    pd = [m.ps("pd%d" % i, [128, 1024], F32) for i in range(2)]
    pcrep = [m.ps("pcrep%d" % i, [128, 512], F32) for i in range(2)]

    def up_src(u):
        if u < 4:
            return G["dshu"][:, 512 * u:512 * (u + 1)].rearrange("(c p) w -> p c w", p=128)
        return G["dup"][u - 4].rearrange("(c p) w -> p c w", p=128)

    def dn_src(u):
        if u < 4:
            return G["dshd"][512 * u:512 * (u + 1), :].rearrange("(c p) w -> p c w", p=128)
        return G["ddn"][u - 4].rearrange("(c p) w -> p c w", p=128)

    dw_done = [None] * (NU + NBUF)
    te_dn_done = [0] * (NU + NBUF)

    def emit_load(u):
        b = u % NBUF
        w = ([("TE", st["te_p0_end"]), ("VE", st["ve_p0_end"])] if u < NBUF
             else [("TE", te_dn_done[u - NBUF])])
        em.op("gpsimd", lambda u=u, b=b: nc.gpsimd.dma_start(
            out=wu[b][:, :].rearrange("p (c w) -> p c w", w=512), in_=up_src(u)),
            wait=w, inc="DW", n=16)
        dw_done[u] = em.op("gpsimd", lambda u=u, b=b: nc.gpsimd.dma_start(
            out=wd[b][:, :].rearrange("p (c w) -> p c w", w=1024), in_=dn_src(u)),
            inc="DW", n=16)

    for u in range(NBUF):
        emit_load(u)

    n_groups = NU // GACC if MOE_NG is None else MOE_NG
    ve_evacadd = 0
    se_relu_prev = 0
    ve_h2c_prev = 0
    ve_crep_prev = [0, 0]
    loads_emitted = NBUF
    pending_down = [None]

    def emit_down(u, r, b, first, last, dn_wait):
        nonlocal ve_evacadd
        wdn = [dn_wait]
        if first:
            wdn.append(("VE", ve_evacadd))
        if u == 0 and r == 0:
            wdn.append(("VE", st["ve_p0_end"]))
        te_dn = None
        for t2 in range(2):
            for ich in range(4):
                for nh in range(2):
                    te_dn = em.op("tensor", lambda t2=t2, ich=ich, nh=nh, u=u, b=b, first=first, last=last: nc.tensor.matmul(
                        pd[t2][:, 512 * nh:512 * (nh + 1)],
                        h2cb[u % 2][:, 256 * ich + 128 * t2:256 * ich + 128 * (t2 + 1)],
                        wd[b][:, 1024 * ich + 512 * nh:1024 * ich + 512 * (nh + 1)],
                        start=(first and ich == 0), stop=(last and ich == 3)),
                        wait=wdn if (t2 == 0 and ich == 0 and nh == 0) else (),
                        inc="TE" if (t2 == 1 and ich == 3 and nh == 1) else None)
        te_dn_done[u] = te_dn
        if last:
            for t2 in range(2):
                tile = 2 * r + t2
                ve_evacadd = em.op("vector", lambda t2=t2, tile=tile: nc.vector.tensor_tensor(
                    xacc[:, 1024 * tile:1024 * (tile + 1)],
                    pd[t2][:, :], xacc[:, 1024 * tile:1024 * (tile + 1)], ALU.add),
                    wait=[("TE", te_dn)], inc="VE")

    for grp in range(n_groups):
        for r in range(8):
            for gi in range(GACC):
                u = grp * GACC + gi
                e = u - 4
                b = u % NBUF
                first = gi == 0
                last = gi == GACC - 1
                te_crep = None
                ve_crep = None
                if e >= 0:
                    wcr = [("VE", st["ve_comb"][2 * r + 1]), ("VE", ve_crep_prev[u % 2])]
                    if u in (4, 5) and r == 0:
                        wcr.append(("VE", st["ve_p0_end"]))
                    te_crep = em.op("tensor", lambda e=e, r=r, u=u: nc.tensor.matmul(
                        pcrep[u % 2][:, 0:256], id64[:, e:e + 1].broadcast_to([64, 128]),
                        combT[:, 256 * r:256 * (r + 1)], start=True, stop=True),
                        wait=wcr, inc="TE")
                    ve_crep = em.op("vector", lambda u=u: nc.vector.tensor_copy(
                        crep_sb[:, :], pcrep[u % 2][:, 0:256]), wait=[("TE", te_crep)], inc="VE")
                    ve_crep_prev[u % 2] = ve_crep
                wup = [("DW", dw_done[u]), ("SE", se_relu_prev)]
                if u == 0 and r == 0:
                    wup.append(("VE", st["ve_xnf"]))
                te_up = None
                for mi in range(4):
                    for kc in range(KD):
                        te_up = em.op("tensor", lambda mi=mi, kc=kc, b=b, r=r: nc.tensor.matmul(
                            ph[:, 256 * mi:256 * (mi + 1)],
                            wu[b][:, 512 * kc + 128 * mi:512 * kc + 128 * (mi + 1)],
                            xnf[:, 2048 * kc + 256 * r:2048 * kc + 256 * (r + 1)],
                            start=(kc == 0), stop=(kc == KD - 1)),
                            wait=wup if (mi == 0 and kc == 0) else (),
                            inc="TE" if (mi == 3 and kc == KD - 1) else None)
                se_relu = em.op("scalar", lambda: nc.scalar.activation(
                    tb[:, :], ph[:, :], AF.Relu), wait=[("TE", te_up)], inc="SE")
                se_relu_prev = se_relu
                wsq = [("TE", te_dn_done[u - 2] if u >= 2 else 0), ("VE", ve_h2c_prev)]
                if e >= 0:
                    se_sq = em.op("scalar", lambda: nc.scalar.activation(
                        sqb[:, :], tb[:, :], AF.Square), wait=wsq, inc="SE")
                    em.op("vector", lambda: nc.vector.drain())
                    for mi in range(4):
                        ve_h2c = em.op("vector", lambda u=u, mi=mi: nc.vector.tensor_tensor(
                            h2cb[u % 2][:, 256 * mi:256 * (mi + 1)],
                            sqb[:, 256 * mi:256 * (mi + 1)], crep_sb[:, :], ALU.mult),
                            wait=([("SE", se_sq), ("VE", ve_crep),
                                   ("TE", te_dn_done[u - 2] if u >= 2 else 0)] if mi == 0 else ()),
                            inc="VE" if mi == 3 else None)
                    dn_wait = ("VE", ve_h2c)
                    ve_h2c_prev = ve_h2c
                else:
                    se_sq = em.op("scalar", lambda u=u: nc.scalar.activation(
                        h2cb[u % 2][:, :], tb[:, :], AF.Square), wait=wsq, inc="SE")
                    dn_wait = ("SE", se_sq)
                if pending_down[0] is not None:
                    emit_down(*pending_down[0])
                pending_down[0] = (u, r, b, first, last, dn_wait)
        for _ in range(GACC):
            if loads_emitted < NU:
                emit_load(loads_emitted)
                loads_emitted += 1
        if grp < len(CAST_PIECES):
            st["dw_cast"] = em.op("gpsimd", CAST_PIECES[grp](st), inc="DW", n=16)
            if grp == 7:
                st["dw_cast_wq"] = st["dw_cast"]
    for i in range(min(n_groups, len(CAST_PIECES)), len(CAST_PIECES)):
        st["dw_cast"] = em.op("gpsimd", CAST_PIECES[i](st), inc="DW", n=16)
        if i == 7:
            st["dw_cast_wq"] = st["dw_cast"]
    if pending_down[0] is not None:
        emit_down(*pending_down[0])
        pending_down[0] = None

    st["moe_fin"] = em.snap()
    if "ffn" in st["taps"]:
        em.op("gpsimd", lambda: nc.gpsimd.dma_start(out=st["dbg"]["ffn"], in_=xacc[:, :]),
              wait=[("VE", em.v("VE"))], inc="DW", n=16)
    m.close()
    st["pm"].close()


def head_kslices(h):
    return [(256 * h, 128, 0), (256 * h + 128, 64, 128)]


def build_attn(st):
    nc, em, G = st["nc"], st["em"], st
    xacc, xnf = st["xacc"], st["xnf"]
    moe_fin = st["moe_fin"]
    if A1_LVL < 0:
        st["xnf_scope"].close()
        st["fin"] = em.snap()
        return

    a = Scope(nc, side="right")
    xs = a.sb("xs", [128, NT * 1024], BF16)
    zseg = a.sb("zseg", [128, 2048], BF16)
    identf = a.sb("identf2", [128, 128], F32)
    lms01 = a.sb("lms01", [128, 128], F32)
    umi01 = a.sb("umi01", [128, 128], F32)
    bdm4 = a.sb("bdm4", [128, 512], BF16)
    i4b = a.sb("i4b", [128, 512], BF16)
    grw = a.sb("grw", [128, DVD], BF16)
    convw_sb = a.sb("convw_sb", [128, 64], F32)
    convb_sb = a.sb("convb_sb", [128, 16], F32)
    nA = a.sb("nA_sb", [4, 1], F32)
    biasv = a.sb("biasv_sb", [4, 1], F32)
    bbv = a.sb("bbv_sb", [4, 1], F32)
    gnw = a.sb("gnw_sb", [1, DVD], BF16)

    def gp(fn, wait=(), inc=None):
        return em.op("gpsimd", fn, wait=wait, inc=inc)

    def asel(ap, cmp, fill, cm, pat):
        return lambda: nc.gpsimd.affine_select(out=ap, in_=ap, compare_op=cmp,
                                               fill=fill, base=0, pattern=pat, channel_multiplier=cm)
    gp(lambda: nc.gpsimd.memset(identf[:, :], 0.0),
       wait=[("TE", moe_fin["TE"]), ("VE", moe_fin["VE"]), ("SE", moe_fin["SE"])])
    gp(asel(identf[:, :], ALU.not_equal, 1.0, 1, [[-1, 128]]))
    gp(lambda: nc.gpsimd.memset(lms01[:, :], 1.0))
    gp(asel(lms01[:, :], ALU.is_gt, 0.0, 1, [[-1, 128]]))
    gp(lambda: nc.gpsimd.memset(umi01[:, :], 1.0))
    gp(asel(umi01[:, :], ALU.is_ge, 0.0, -1, [[1, 128]]))
    for cp in range(4):
        for blk in range(4):
            sl_ = bdm4[:, 128 * cp + 32 * blk:128 * cp + 32 * (blk + 1)]
            gp(lambda sl_=sl_: nc.gpsimd.memset(sl_, 1.0))
            gp(lambda sl_=sl_, blk=blk: nc.gpsimd.affine_select(
                out=sl_, in_=sl_, compare_op=ALU.is_ge, fill=0.0,
                base=-32 * blk, pattern=[[0, 32]], channel_multiplier=1))
            gp(lambda sl_=sl_, blk=blk: nc.gpsimd.affine_select(
                out=sl_, in_=sl_, compare_op=ALU.is_ge, fill=0.0,
                base=32 * blk + 31, pattern=[[0, 32]], channel_multiplier=-1))
    for i in range(4):
        gp(lambda i=i: nc.gpsimd.memset(i4b[:, 128 * i:128 * (i + 1)], 0.0))
        gp(asel(i4b[:, 0:512][:, 128 * i:128 * (i + 1)], ALU.not_equal, 1.0, 1, [[-1, 128]]))
    gc_att = em.op("gpsimd", lambda: nc.gpsimd.memset(nA[:, :], 0.0), inc="GC")
    st["gc_att"] = gc_att

    wfence = [("TE", moe_fin["TE"]), ("VE", moe_fin["VE"]), ("SE", moe_fin["SE"])]
    ds = em.op("sync", lambda: nc.sync.dma_start(
        out=convw_sb[:, :].rearrange("p (c w) -> p c w", w=8),
        in_=st["dconvw"].rearrange("(c p) w -> p c w", p=128)), wait=wfence, inc="DS", n=16)
    ds = em.op("sync", lambda: nc.sync.dma_start(
        out=convb_sb[:, :].rearrange("p (c w) -> p c w", w=2),
        in_=st["dconvb"].rearrange("(c p) w -> p c w", p=128)), inc="DS", n=16)
    st["ds_conv"] = ds
    ds = em.op("sync", lambda: nc.sync.dma_start(out=nA[:, :], in_=st["dAlog"][:, :]),
               wait=[("GC", gc_att)], inc="DS", n=16)
    ds = em.op("sync", lambda: nc.sync.dma_start(out=biasv[:, :], in_=st["ddtbb"][0:4, :]), inc="DS", n=16)
    ds_sync_small = em.op("sync", lambda: nc.sync.dma_start(out=bbv[:, :], in_=st["ddtbb"][4:8, :]), inc="DS", n=16)
    ds_small = em.op("gpsimd", lambda: nc.gpsimd.dma_start(out=gnw[:1, :], in_=st["dgnw"][:, :]),
                     inc="DW", n=16)
    em.op("scalar", lambda: nc.scalar.activation(nA[:, :], nA[:, :], AF.Exp),
          wait=[("DS", ds_sync_small)])
    se_nA = em.op("scalar", lambda: nc.scalar.mul(nA[:, :], nA[:, :], -1.0), inc="SE")
    st["se_nA"] = se_nA

    build_a1(st, a, locals())
    if A2_ON:
        build_a2(st, a, locals())
    else:
        a.close()
    st["fin"] = em.snap()
    if "attn" in st["taps"]:
        em.op("gpsimd", lambda: nc.gpsimd.dma_start(out=st["dbg"]["attn"], in_=xacc[:, :]),
              wait=[("VE", em.v("VE"))], inc="DW", n=16)


def build_a1(st, a, env):
    nc, em = st["nc"], st["em"]
    xnf = st["xnf"]
    xs = env["xs"]
    convw_sb, convb_sb = env["convw_sb"], env["convb_sb"]
    moe_fin = st["moe_fin"]
    a1 = Scope(nc, side="right")
    zch = [a1.sb("zch%d" % i, [128, 2048], BF16) for i in range(2)]
    winb = [a1.sb("winb%d" % i, [128, 1024], BF16) for i in range(2)]
    xzst = [a1.sb("xzst%d" % i, [128, 2048], F32) for i in range(2)]
    cvacc = a1.sb("cvacc", [128, 2048], F32)
    pxz = [a1.ps("pxz%d" % i, [128, 2048], F32) for i in range(2)]

    dw_win = [0] * NT
    te_mm = [0] * NT
    se_evac = [0] * NT
    ve_taps = [0] * NT
    se_silu = [0] * NT
    ds_conv = st["ds_conv"]

    for m in range(NT if A1_LVL >= 1 else 0):
        b = m % 2
        w = [("TE", moe_fin["TE"]), ("VE", moe_fin["VE"])] if m < 2 else [("TE", te_mm[m - 2])]
        dw_win[m] = em.op("gpsimd", lambda m=m, b=b: nc.gpsimd.dma_start(
            out=winb[b][:, :].rearrange("p (c w) -> p c w", w=128),
            in_=st["dW_in"][:, 128 * m:128 * (m + 1)].rearrange("(c p) w -> p c w", p=128)),
            wait=w, inc="DW", n=16)
        wmm = [("DW", dw_win[m]), ("SE", se_evac[m - 2] if m >= 2 else 0)]
        if m < 2:
            wmm += [("VE", moe_fin["VE"]), ("SE", moe_fin["SE"])]
        for n in range(4):
            for kc in range(KD):
                te = em.op("tensor", lambda m=m, b=b, n=n, kc=kc: nc.tensor.matmul(
                    pxz[b][:, 512 * n:512 * (n + 1)],
                    winb[b][:, 128 * kc:128 * (kc + 1)],
                    xnf[:, 2048 * kc + 512 * n:2048 * kc + 512 * (n + 1)],
                    start=(kc == 0), stop=(kc == KD - 1)),
                    wait=wmm if (n == 0 and kc == 0) else (),
                    inc="TE" if (n == 3 and kc == KD - 1) else None)
        te_mm[m] = te
        se_evac[m] = em.op("scalar", lambda b=b: nc.scalar.activation(
            xzst[b][:, :], pxz[b][:, :], AF.Copy),
            wait=[("TE", te), ("VE", ve_taps[m - 2] if m >= 2 else 0)], inc="SE")
        if A1_LVL < 2:
            ve_taps[m] = em.op("vector", lambda b=b: nc.vector.tensor_copy(
                cvacc[:, 0:8], xzst[b][:, 0:8]), wait=[("SE", se_evac[m])], inc="VE")
            se_silu[m] = se_evac[m]
            continue
        cw = m % 8
        wofs = 32 * (m // 8)
        col = lambda j, m=m: convw_sb[:, 8 * (m % 8) + (0 if m < 8 else 4) + j:
                                      8 * (m % 8) + (0 if m < 8 else 4) + j + 1]
        em.op("vector", lambda b=b, col=col: nc.vector.tensor_scalar(
            cvacc[:, :], xzst[b][:, :], col(1), None, ALU.mult),
            wait=[("SE", se_evac[m]), ("DS", ds_conv), ("SE", env["se_nA"])])
        em.op("vector", lambda: nc.vector.drain())
        em.op("vector", lambda b=b, col=col: nc.vector.scalar_tensor_tensor(
            cvacc[:, 1:2048], xzst[b][:, 0:2047], col(0), cvacc[:, 1:2048], ALU.mult, ALU.add))
        em.op("vector", lambda: nc.vector.drain())
        em.op("vector", lambda b=b, col=col: nc.vector.scalar_tensor_tensor(
            cvacc[:, 0:2047], xzst[b][:, 1:2048], col(2), cvacc[:, 0:2047], ALU.mult, ALU.add))
        em.op("vector", lambda: nc.vector.drain())
        ve_taps[m] = em.op("vector", lambda b=b, col=col: nc.vector.scalar_tensor_tensor(
            cvacc[:, 0:2046], xzst[b][:, 2:2048], col(3), cvacc[:, 0:2046], ALU.mult, ALU.add),
            inc="VE")
        bcol = convb_sb[:, 2 * (m % 8) + (0 if m < 8 else 1):2 * (m % 8) + (0 if m < 8 else 1) + 1]
        if m < 8:
            se_silu[m] = em.op("scalar", lambda m=m, bcol=bcol: nc.scalar.activation(
                xs[:, 2048 * m:2048 * (m + 1)], cvacc[:, :], AF.Silu, bias=bcol),
                wait=[("VE", ve_taps[m])], inc="SE")
        else:
            zb = m % 2
            se_silu[m] = em.op("scalar", lambda zb=zb, bcol=bcol: nc.scalar.activation(
                zch[zb][:, :], cvacc[:, :], AF.Silu, bias=bcol),
                wait=[("VE", ve_taps[m]), ("DS", st.get("ds_zch_%d" % (m - 2), 0))], inc="SE")
            if A1_LVL >= 3:
                st["ds_zch_%d" % m] = em.op("sync", lambda m=m, zb=zb: nc.sync.dma_start(
                    out=st["zscr"][m - 8], in_=zch[zb][:, :]),
                    wait=[("SE", se_silu[m])], inc="DS", n=16)
    st["a1_se_last"] = em.v("SE")
    st["a1_te_last"] = em.v("TE")
    st["a1_ve_last"] = em.v("VE")
    st["a1_ds_last"] = em.v("DS")

    if "xs" in st["taps"]:
        em.op("gpsimd", lambda: nc.gpsimd.dma_start(out=st["dbg"]["xs"], in_=xs[:, :]),
              wait=[("SE", st["a1_se_last"])], inc="DW", n=16)
    if "zs" in st["taps"]:
        em.op("gpsimd", lambda: nc.gpsimd.dma_start(
            out=st["dbg"]["zs"].rearrange("p (c w) -> c p w", w=2048), in_=st["zscr"][:, :, :]),
              wait=[("DS", st["a1_ds_last"])], inc="DW", n=16)
    a1.close()
    st["xnf_scope"].close()


def build_a2(st, a, env):
    nc, em = st["nc"], st["em"]
    xacc = st["xacc"]
    xs, zseg = env["xs"], env["zseg"]
    identf, i4b = env["identf"], env["i4b"]
    lms01, umi01, bdm4 = env["lms01"], env["umi01"], env["bdm4"]
    grw, nA, biasv, gnw = env["grw"], env["nA"], env["biasv"], env["gnw"]
    bbv = env["bbv"]
    ident_b = st["ident_b"]

    a2 = Scope(nc, side="right")
    wqs = [a2.sb("wqs%d" % i, [128, 2048], BF16) for i in range(4)]
    idbg = None
    q_tm = a2.sb("q_tm", [128, 1536], BF16)
    k_tm = a2.sb("k_tm", [128, 1536], BF16)
    v_tm = a2.sb("v_tm", [128, 3072], BF16)
    g_tm = a2.sb("g_tm", [128, 3072], BF16)

    betf = a2.sb("betf", [4, 256], F32)
    gam = a2.sb("gam", [4, 256], F32)
    ngam = a2.sb("ngam", [4, 256], F32)
    bex = a2.sb("bex", [4, 256], F32)
    spt = a2.sb("spt", [4, 256], F32)
    bb4c = a2.sb("bb4c", [4, 128], F32)
    dec4c = a2.sb("dec4c", [4, 128], F32)
    onesfA = a2.sb("onesfA", [1, 512], F32)
    grow = a2.sb("grow", [1, 512], F32)
    colst = a2.sb("colst", [128, 24], F32)
    bC4 = a2.sb("bC4", [128, 4], F32)
    rq = a2.sb("rq", [128, 16], F32)
    rtmp = a2.sb("rtmp", [128, 448], F32)
    kf_s = a2.sb("kf_s", [128, 1024], BF16)
    qf_s = a2.sb("qf_s", [128, 1024], BF16)
    qbf_s = a2.sb("qbf_s", [128, 1024], BF16)
    qb_tm = a2.sb("qb_tm", [128, 768], BF16)
    A4 = a2.sb("A4", [128, 512], BF16)
    AT4 = a2.sb("AT4", [128, 512], BF16)
    X4 = a2.sb("X4", [128, 512], BF16)
    dtmp = a2.sb("dtmp", [128, 128], F32)
    Dsf = a2.sb("Dsf", [128, 128], F32)
    MT4 = a2.sb("MT4", [128, 512], BF16)
    ktl = a2.sb("ktl", [128, 192], BF16)
    kdec = a2.sb("kdec", [128, 192], BF16)
    bVb = a2.sb("bVb", [128, 384], BF16)
    Ub = a2.sb("Ub", [128, 384], BF16)
    Wnegb = a2.sb("Wnegb", [128, 256], BF16)
    Sa = [a2.sb("Sa%d" % h, [128, 384], F32) for h in range(4)]
    Sb = [a2.sb("Sb%d" % h, [64, 384], F32) for h in range(4)]
    S0bA = a2.sb("S0bA", [128, 4 * 384], BF16)
    S0bB = a2.sb("S0bB", [64, 4 * 384], BF16)
    sgb = a2.sb("sgb", [128, 384], BF16)
    sgg = sgb
    obf = a2.sb("obf", [128, 384], F32)
    ro = a2.sb("ro", [128, 4], F32)
    o4tm = a2.sb("o4tm", [128, 1536], BF16)
    ofm2 = a2.sb("ofm2", [128, 3072], BF16)
    ytmp = o4tm
    yfm2 = a2.sb("yfm2", [128, 2048], BF16)
    wsob = [a2.sb("wsob%d" % i, [128, 1024], BF16) for i in range(3)]
    P2a = a2.ps("P2a", [128, 1024], F32)
    P2b = a2.ps("P2b", [128, 1024], F32)
    P1a = a2.ps("P1a", [128, 512], F32)
    P1b = a2.ps("P1b", [128, 512], F32)
    P1c = a2.ps("P1c", [128, 512], F32)
    PT = a2.ps("PTb", [128, 1024], BF16)

    te = em.op("tensor", lambda: nc.tensor.matmul(
        P1a[:, 0:DVD], st["onesb"][:1, :], gnw[:1, :], start=True, stop=True),
        wait=[("DW", env["ds_small"]), ("TE", st["a1_te_last"]), ("SE", st["a1_se_last"]),
              ("VE", st["a1_ve_last"])], inc="TE")
    ve_grw = em.op("vector", lambda: nc.vector.tensor_copy(grw[:, :], P1a[:, 0:DVD]),
                   wait=[("TE", te)], inc="VE")
    gz = None
    a1fence = [("SE", st["a1_se_last"]), ("VE", st["a1_ve_last"]), ("TE", st["a1_te_last"])]
    for h in range(4):
        em.op("gpsimd", lambda h=h: nc.gpsimd.memset(Sa[h][:, :], 0.0),
              wait=a1fence if h == 0 else ())
        em.op("gpsimd", lambda h=h: nc.gpsimd.memset(Sb[h][:, :], 0.0))
    em.op("gpsimd", lambda: nc.gpsimd.memset(S0bA[:, :], 0.0))
    em.op("gpsimd", lambda: nc.gpsimd.memset(S0bB[:, :], 0.0))
    gz0 = em.op("gpsimd", lambda: nc.gpsimd.memset(onesfA[:, :], 1.0), inc="GC")
    gz = gz0
    st["gz_ds"] = 0

    def V(fn, wait=(), inc=None, drain=True):
        if drain:
            em.op("vector", lambda: nc.vector.drain())
        return em.op("vector", fn, wait=wait, inc=inc)

    SLICES = [(0, 512, q_tm, 0), (512, 256, q_tm, 512),
              (768, 512, k_tm, 0), (1280, 256, k_tm, 512),
              (1536, 512, v_tm, 0), (2048, 512, v_tm, 512), (2560, 512, v_tm, 1024),
              (3072, 512, g_tm, 0), (3584, 512, g_tm, 512), (4096, 512, g_tm, 1024)]
    PJ = [P1a, P1b, P1c, P2a]

    ve_sched = {"proj_evac": 0}
    st_fin = None
    prev = {
        "te_qk_last": 0,
        "ve_state": 0,
        "ve_out": 0,
        "te_out": [0, 0],
        "ve_ytmp": 0,
        "dw_wq": [0, 0],
        "te_proj": [0, 0],
        "ve_pj": [0, 0, 0, 0],
        "te_pj": [0, 0, 0, 0],
        "se_gkb": 0,
        "ve_gam": 0,
        "ds_stk": 0,
        "ve_colst": 0,
        "se_dec": 0,
        "ve_kf": 0, "ve_qf": 0, "ve_qbf": 0,
        "ve_A": 0, "ve_MT": 0, "ve_AT": 0,
        "ve_X": 0,
        "ve_Ub": 0, "ve_o4": 0,
        "te_state": 0, "te_O": 0, "te_U": 0,
        "ve_ofm": 0,
        "te_wblk": [0, 0, 0],
        "te_y": [0, 0],
    }

    for s in range(NSEG_RUN):
        tau = [2 * s, 2 * s + 1]
        ds_zs = em.op("sync", lambda s=s: nc.sync.dma_start(
            out=zseg[:, :].rearrange("p (c w) -> p c w", w=256),
            in_=st["zscr"][:, :, 256 * s:256 * (s + 1)].rearrange("c p w -> p c w")),
            wait=[("TE", prev["te_out"][0]), ("TE", prev["te_out"][1]),
                  ("DS", st["a1_ds_last"] if s == 0 else 0)], inc="DS", n=16)
        prev["ds_zs"] = ds_zs
        for j, (c0, wd, dst, dbase) in enumerate(SLICES):
            par = j % 2
            dwh = [None, None]
            for half in range(2):
                bq = 2 * par + half
                wld = [("TE", prev["te_proj"][par])]
                if s == 0 and j == 0:
                    wld += [("TE", st["a1_te_last"]), ("VE", st["a1_ve_last"])]
                if s == 0 and j < 4:
                    wld.append(("DW", st["dw_cast"]))
                dwh[half] = em.op("sync", lambda c0=c0, wd=wd, bq=bq, half=half: nc.sync.dma_start(
                    out=wqs[bq][:, 0:4 * wd].rearrange("p (c w) -> p c w", w=wd),
                    in_=st["dwq16"][512 * half:512 * (half + 1), c0:c0 + wd]
                        .rearrange("(c p) w -> p c w", p=128)),
                    wait=wld, inc="DS", n=16)
                prev["dw_wq"][half] = dwh[half]
            for c2 in range(2):
                pj = PJ[(2 * j + c2) % 4]
                pidx = (2 * j + c2) % 4
                wmm = [("DS", dwh[0]), ("VE", prev["ve_pj"][pidx])]
                if s == 0 and j == 0:
                    wmm.append(("VE", st["a1_ve_last"]))
                    wmm.append(("SE", st["a1_se_last"]))
                if s > 0 and j < 4:
                    wmm.append(("VE", prev["ve_out"]))
                for kc in range(KD):
                    w2 = wmm if kc == 0 else ([("DS", dwh[1])] if kc == 4 else ())
                    te = em.op("tensor", lambda c2=c2, kc=kc, wd=wd, pj=pj, s=s, par=par: nc.tensor.matmul(
                        pj[:, 0:wd],
                        xs[:, 2048 * kc + 128 * (2 * s + c2):2048 * kc + 128 * (2 * s + c2) + 128],
                        wqs[2 * par + kc // 4][:, wd * (kc % 4):wd * (kc % 4) + wd],
                        start=(kc == 0), stop=(kc == KD - 1)),
                        wait=w2, inc="TE" if kc == KD - 1 else None)
                prev["te_pj"][pidx] = te
                prev["ve_pj"][pidx] = em.op("vector", lambda dst=dst, dbase=dbase, c2=c2, wd=wd, pj=pj:
                    nc.vector.tensor_copy(
                        dst[:, (dst.shape[1] // 2) * c2 + dbase:(dst.shape[1] // 2) * c2 + dbase + wd],
                        pj[:, 0:wd]), wait=[("TE", te)], inc="VE")
            prev["te_proj"][par] = te
        dwg = em.op("sync", lambda: nc.sync.dma_start(
            out=wqs[0][:, 0:KD * 8].rearrange("p (c w) -> p c w", w=8),
            in_=st["dwq16"][:, 4608:4616].rearrange("(c p) w -> p c w", p=128)),
            wait=([("DW", st["dw_cast"])] if s == 0 else []) + [("TE", prev["te_proj"][0])],
            inc="DS", n=16)
        wmm = [("DS", dwg), ("SE", prev["se_gkb"]), ("VE", prev["ve_out"])]
        for kc in range(KD):
            em.op("tensor", lambda kc=kc, s=s: nc.tensor.matmul(
                P2b[0:4, 0:256], wqs[0][:, 8 * kc:8 * kc + 4],
                xs[:, 2048 * kc + 256 * s:2048 * kc + 256 * s + 256],
                start=(kc == 0), stop=(kc == KD - 1)),
                wait=wmm if kc == 0 else ())
            te = em.op("tensor", lambda kc=kc, s=s: nc.tensor.matmul(
                P2b[0:4, 512:768], wqs[0][:, 8 * kc + 4:8 * kc + 8],
                xs[:, 2048 * kc + 256 * s:2048 * kc + 256 * s + 256],
                start=(kc == 0), stop=(kc == KD - 1)),
                inc="TE" if kc == KD - 1 else None)
        prev["te_proj"] = [te, te]
        em.op("scalar", lambda: nc.scalar.activation(spt[:, :], P2b[0:4, 0:256], AF.Exp,
                                                     bias=biasv[:, 0:1]),
              wait=[("TE", te), ("SE", env["se_nA"]), ("VE", prev["ve_gam"])])
        em.op("scalar", lambda: nc.scalar.activation(spt[:, :], spt[:, :], AF.Ln, bias=1.0))
        se1 = em.op("scalar", lambda: nc.scalar.activation(betf[:, :], P2b[0:4, 512:768], AF.Sigmoid,
                                                           bias=bbv[:, 0:1]), inc="SE")
        prev["se_gkb"] = se1
        gkf = spt
        V(lambda: nc.vector.tensor_scalar(gkf[:, :], spt[:, :], nA[0:4, 0:1], None, ALU.mult),
          wait=[("SE", se1)])
        for c2 in range(2):
            V(lambda c2=c2: nc.vector.tensor_tensor_scan(
                gam[:, 128 * c2:128 * (c2 + 1)], gkf[:, 128 * c2:128 * (c2 + 1)],
                gkf[:, 128 * c2:128 * (c2 + 1)], 0.0, ALU.add, ALU.bypass))
        ve_gam = V(lambda: nc.vector.tensor_scalar(ngam[:, :], gam[:, :], -1.0, None, ALU.mult),
                   inc="VE")
        se_bex = em.op("scalar", lambda: nc.scalar.activation(bex[:, :], gam[:, :], AF.Exp),
                       wait=[("VE", ve_gam)], inc="SE")
        prev["ve_gam"] = ve_gam
        ds_gw = em.op("sync", lambda: nc.sync.dma_start(out=st["gscr"][:, :], in_=gam[:, :]),
                      wait=[("VE", ve_gam), ("DS", prev.get("ds_stk", 0))], inc="DS", n=16)

        for c2 in range(2):
            tt_ = tau[c2]
            ofmbase = 1536 * c2
            wstk = [("DS", ds_gw), ("GC", gz), ("TE", prev.get("te_gm", 0))]
            ds_stk = em.op("sync", lambda c2=c2: nc.sync.dma_start(
                out=grow[0:1, :].rearrange("p (h t) -> p h t", t=128),
                in_=st["gscr"][:, 128 * c2:128 * (c2 + 1)]),
                wait=wstk, inc="DS", n=16)
            prev["ds_stk"] = ds_stk
            ve_bb = V(lambda c2=c2: nc.vector.tensor_tensor(
                bb4c[:, :], betf[:, 128 * c2:128 * (c2 + 1)], bex[:, 128 * c2:128 * (c2 + 1)],
                ALU.mult), wait=[("SE", se_bex)], inc="VE")
            se_dec = em.op("scalar", lambda c2=c2: nc.scalar.activation(
                dec4c[:, :], gam[:, 128 * c2:128 * (c2 + 1)], AF.Exp, scale=-1.0,
                bias=gam[:, 128 * c2 + 127:128 * c2 + 128]), wait=[("SE", se_bex)], inc="SE")
            wtr = [("VE", ve_bb), ("SE", se_dec), ("VE", prev["ve_state"]), ("TE", prev["te_state"])]
            srcs = [betf[:, 128 * c2:128 * (c2 + 1)], bex[:, 128 * c2:128 * (c2 + 1)],
                    bb4c[:, :], dec4c[:, :], gam[:, 128 * c2:128 * (c2 + 1)],
                    ngam[:, 128 * c2:128 * (c2 + 1)]]
            for i, src in enumerate(srcs):
                te = em.op("tensor", lambda src=src, i=i: nc.tensor.transpose(
                    P1c[:, 4 * i:4 * (i + 1)], src, identf[:4, :4]),
                    wait=wtr if i == 0 else (), inc="TE" if i == 5 else None)
            ve_colst = V(lambda: nc.vector.tensor_copy(colst[:, :], P1c[:, 0:24]),
                         wait=[("TE", te)], inc="VE", drain=False)
            V(lambda: nc.vector.tensor_tensor(bC4[:, :], colst[:, 4:8], colst[:, 12:16], ALU.mult))
            prev["ve_colst"] = ve_colst
            qsl = q_tm[:, 768 * c2:768 * (c2 + 1)]
            ksl = k_tm[:, 768 * c2:768 * (c2 + 1)]
            wsq0 = [("VE", prev["ve_pj"][0]), ("VE", prev["ve_pj"][1]),
                    ("VE", prev["ve_pj"][2]), ("VE", prev["ve_pj"][3])]
            for h in range(4):
                V(lambda h=h, qsl=qsl: nc.vector.tensor_tensor(
                    rtmp[:, 0:192], qsl[:, 192 * h:192 * (h + 1)], qsl[:, 192 * h:192 * (h + 1)],
                    ALU.mult), wait=wsq0 if h == 0 else (), drain=True)
                V(lambda h=h: nc.vector.reduce_sum(rq[:, h:h + 1], rtmp[:, 0:192], axis=AXX),
                  drain=True)
            for h in range(4):
                V(lambda h=h, ksl=ksl: nc.vector.tensor_tensor(
                    rtmp[:, 0:192], ksl[:, 192 * h:192 * (h + 1)], ksl[:, 192 * h:192 * (h + 1)],
                    ALU.mult), drain=True)
                ve_n = V(lambda h=h: nc.vector.reduce_sum(rq[:, 4 + h:5 + h], rtmp[:, 0:192], axis=AXX),
                         drain=True, inc="VE" if h == 3 else None)
            em.op("scalar", lambda: nc.scalar.activation(rq[:, 8:16], rq[:, 0:8], AF.Ln,
                                                         bias=st["epss"][:, 1:2]),
                  wait=[("VE", ve_n)])
            se_rq = em.op("scalar", lambda: nc.scalar.activation(rq[:, 0:8], rq[:, 8:16], AF.Exp,
                                                                 scale=-0.5), inc="SE")
            for h in range(4):
                V(lambda h=h, qsl=qsl: nc.vector.tensor_scalar(
                    qsl[:, 192 * h:192 * (h + 1)], qsl[:, 192 * h:192 * (h + 1)],
                    rq[:, h:h + 1], None, ALU.mult), wait=[("SE", se_rq)] if h == 0 else (),
                    drain=(h == 0))
            for h in range(4):
                ve_nk = V(lambda h=h, ksl=ksl: nc.vector.tensor_scalar(
                    ksl[:, 192 * h:192 * (h + 1)], ksl[:, 192 * h:192 * (h + 1)],
                    rq[:, 4 + h:5 + h], None, ALU.mult), drain=False,
                    inc="VE" if h == 3 else None)
            for h in range(4):
                ve_qb = V(lambda h=h, qsl=qsl: nc.vector.tensor_scalar(
                    qb_tm[:, 192 * h:192 * (h + 1)], qsl[:, 192 * h:192 * (h + 1)],
                    colst[:, 4 + h:5 + h], None, ALU.mult), drain=(h == 0),
                    inc="VE" if h == 3 else None)
            for name, src, dst in (("ve_kf", ksl, kf_s), ("ve_qf", qsl, qf_s), ("ve_qbf", qb_tm, qbf_s)):
                wtp = [("VE", ve_nk if name != "ve_qbf" else ve_qb), ("VE", ve_grw)]
                if name == "ve_kf":
                    wtp.append(("VE", prev["ve_ofm"]))
                else:
                    wtp.append(("VE", prev[{"ve_qf": "ve_kf", "ve_qbf": "ve_qf"}[name]]))
                nt_ = 0
                for h4 in range(4):
                    for (col, nr, loc) in head_kslices(h4):
                        te = em.op("tensor", lambda src=src, h4=h4, col=col, nr=nr, loc=loc: nc.tensor.transpose(
                            PT[0:nr, col:col + 128], src[:, 192 * h4 + loc:192 * h4 + loc + nr],
                            ident_b[:, :]),
                            wait=wtp if nt_ == 0 else (), inc="TE" if nt_ == 7 else None)
                        nt_ += 1
                prev[name] = em.op("vector", lambda dst=dst: nc.vector.tensor_copy(
                    dst[:, :], PT[:, 0:1024]), wait=[("TE", te)], inc="VE")
            for h in range(4):
                ks_ = head_kslices(h)
                wk2 = [("VE", prev["ve_kf"]), ("VE", prev["ve_A"])]
                te_k2 = None
                for (cc, nr, loc) in ks_:
                    te_k2 = em.op("tensor", lambda cc=cc, nr=nr, first=(te_k2 is None): nc.tensor.matmul(
                        P1a[:, 0:128], kf_s[0:nr, cc:cc + 128], kf_s[0:nr, cc:cc + 128],
                        start=first, stop=not first),
                        wait=wk2 if te_k2 is None else (), inc="TE")
                wqk = [("VE", prev["ve_qf"]), ("VE", prev["ve_MT"])]
                te_qk = None
                for (cc, nr, loc) in ks_:
                    te_qk = em.op("tensor", lambda cc=cc, nr=nr, first=(te_qk is None): nc.tensor.matmul(
                        P1b[:, 0:128], kf_s[0:nr, cc:cc + 128], qf_s[0:nr, cc:cc + 128],
                        start=first, stop=not first),
                        wait=wqk if te_qk is None else (), inc="TE")
                prev["te_qk_last"] = te_qk
                wgm = [("DS", prev["ds_stk"]), ("VE", prev["ve_colst"])]
                te_gm = em.op("tensor", lambda h=h: nc.tensor.matmul(
                    P1c[:, 128:256], onesfA[:1, 0:128], grow[0:1, 128 * h:128 * (h + 1)],
                    start=True, stop=True), wait=wgm, inc="TE")
                prev["te_gm"] = te_gm
                V(lambda h=h: nc.vector.tensor_scalar(
                    dtmp[:, :], P1c[:, 128:256], -1.0, colst[:, 16 + h:17 + h],
                    ALU.mult, ALU.add), wait=[("TE", te_gm)], drain=False)
                ve_d = V(lambda: nc.vector.scalar_tensor_tensor(
                    dtmp[:, :], dtmp[:, :], 40.0, lms01[:, :], ALU.add, ALU.mult),
                    drain=True, inc="VE")
                se_ds = em.op("scalar", lambda: nc.scalar.activation(
                    Dsf[:, :], dtmp[:, :], AF.Exp, bias=st["epss"][:, 2:3]),
                    wait=[("VE", ve_d)], inc="SE")
                ve_A = V(lambda h=h: nc.vector.scalar_tensor_tensor(
                    A4[:, 128 * h:128 * (h + 1)], P1a[:, 0:128], colst[:, h:h + 1], Dsf[:, :],
                    ALU.mult, ALU.mult), wait=[("SE", se_ds), ("TE", te_k2)], inc="VE", drain=False)
                prev["ve_A"] = ve_A
                if "invd" in st["taps"] and s == 0 and c2 == 1 and h == 3:
                    idbg = v_tm
                    V(lambda: nc.vector.tensor_copy(idbg[:, 0:128], P1c[:, 128:256]), drain=True)
                    V(lambda: nc.vector.tensor_copy(idbg[:, 128:256], Dsf[:, :]), drain=False)
                    V(lambda: nc.vector.tensor_copy(idbg[:, 256:384], P1a[:, 0:128]), drain=False)
                    V(lambda h=h: nc.vector.tensor_copy(idbg[:, 384:512], A4[:, 128 * h:128 * (h + 1)]),
                      drain=False)
                    ve_idbg = V(lambda: nc.vector.tensor_copy(idbg[:, 512:536], colst[:, :]),
                                drain=False, inc="VE")
                    em.op("gpsimd", lambda: nc.gpsimd.dma_start(
                        out=st["dbg"]["invd"], in_=idbg[:, 0:2176]),
                        wait=[("VE", ve_idbg)], inc="DW", n=16)
                    prev["ve_MT"] = ve_idbg
                V(lambda h=h: nc.vector.tensor_scalar(
                    dtmp[:, :], P1c[:, 128:256], 1.0, colst[:, 20 + h:21 + h],
                    ALU.mult, ALU.add), wait=[("SE", se_ds)], drain=False)
                ve_d2 = V(lambda: nc.vector.scalar_tensor_tensor(
                    dtmp[:, :], dtmp[:, :], 40.0, umi01[:, :], ALU.add, ALU.mult),
                    drain=True, inc="VE")
                se_di = em.op("scalar", lambda: nc.scalar.activation(
                    Dsf[:, :], dtmp[:, :], AF.Exp, bias=st["epss"][:, 2:3]),
                    wait=[("VE", ve_d2)], inc="SE")
                ve_MT = V(lambda h=h: nc.vector.tensor_tensor(
                    MT4[:, 128 * h:128 * (h + 1)], P1b[:, 0:128], Dsf[:, :], ALU.mult),
                    wait=[("SE", se_di), ("TE", te_qk)], inc="VE", drain=False)
                prev["ve_MT"] = ve_MT
                te_at = em.op("tensor", lambda h=h: nc.tensor.transpose(
                    PT[:, 0:128], A4[:, 128 * h:128 * (h + 1)], ident_b[:, :]),
                    wait=[("VE", ve_A), ("VE", prev["ve_qbf"] if h == 0 else prev["ve_AT"])],
                    inc="TE")
                prev["ve_AT"] = em.op("vector", lambda h=h: nc.vector.tensor_copy(
                    AT4[:, 128 * h:128 * (h + 1)], PT[:, 0:128]), wait=[("TE", te_at)], inc="VE")
            Db4 = kf_s[:, 0:512]
            DbT4 = kf_s[:, 512:1024]
            F4 = qb_tm[:, 0:512]
            Hb = qf_s[:, 0:512]
            Gb = qf_s[:, 512:1024]
            XC = o4tm[:, 0:512]
            Tdb = kf_s[:, 0:512]
            V(lambda: nc.vector.tensor_tensor(Db4, A4[:, :], bdm4[:, :], ALU.mult),
              wait=[("VE", prev["ve_A"]), ("TE", prev["te_qk_last"]), ("VE", prev["ve_MT"])],
              drain=False)
            V(lambda: nc.vector.tensor_tensor(DbT4, AT4[:, :], bdm4[:, :], ALU.mult),
              wait=[("VE", prev["ve_AT"])], drain=False)
            V(lambda: nc.vector.tensor_tensor(F4, A4[:, :], Db4, ALU.subtract), drain=True)
            ve_X = V(lambda: nc.vector.tensor_tensor(X4[:, :], i4b[:, 0:512], DbT4, ALU.subtract),
                     drain=True, inc="VE")
            ve_sq = ve_X
            for j in range(4):
                wpow = [("VE", ve_sq)]
                if j == 0:
                    wpow.append(("VE", prev["ve_pj"][3]))
                te_pow = None
                for h in range(4):
                    te_pow = em.op("tensor", lambda h=h: nc.tensor.matmul(
                        P2a[:, 128 * h:128 * (h + 1)], DbT4[:, 128 * h:128 * (h + 1)],
                        Db4[:, 128 * h:128 * (h + 1)], start=True, stop=True),
                        wait=wpow if h == 0 else (), inc="TE" if h == 3 else None)
                if j < 3:
                    for h in range(4):
                        te_pow = em.op("tensor", lambda h=h: nc.tensor.matmul(
                            P2a[:, 512 + 128 * h:512 + 128 * (h + 1)], Db4[:, 128 * h:128 * (h + 1)],
                            DbT4[:, 128 * h:128 * (h + 1)], start=True, stop=True),
                            inc="TE" if h == 3 else None)
                if j < 3:
                    V(lambda: nc.vector.tensor_copy(Db4, P2a[:, 0:512]),
                      wait=[("TE", te_pow)], drain=False)
                    ve_sq = V(lambda: nc.vector.tensor_copy(DbT4, P2a[:, 512:1024]),
                              drain=False, inc="VE")
                else:
                    ve_sq = V(lambda: nc.vector.tensor_copy(Db4, P2a[:, 0:512]),
                              wait=[("TE", te_pow)], drain=False, inc="VE")
                te_prod = None
                for h in range(4):
                    te_prod = em.op("tensor", lambda h=h: nc.tensor.matmul(
                        P2b[:, 128 * h:128 * (h + 1)], Db4[:, 128 * h:128 * (h + 1)],
                        X4[:, 128 * h:128 * (h + 1)], start=True, stop=True),
                        wait=[("VE", ve_sq), ("VE", ve_X)] if h == 0 else (),
                        inc="TE" if h == 3 else None)
                ve_X = V(lambda: nc.vector.tensor_tensor(X4[:, :], X4[:, :], P2b[:, 0:512], ALU.add),
                         wait=[("TE", te_prod)], drain=False, inc="VE")
            te_H = None
            for h in range(4):
                te_H = em.op("tensor", lambda h=h: nc.tensor.matmul(
                    P2b[:, 128 * h:128 * (h + 1)], F4[:, 128 * h:128 * (h + 1)],
                    X4[:, 128 * h:128 * (h + 1)], start=True, stop=True),
                    wait=[("VE", ve_X)] if h == 0 else (), inc="TE" if h == 3 else None)
            te_G = None
            for h in range(4):
                te_G = em.op("tensor", lambda h=h: nc.tensor.matmul(
                    P2a[:, 128 * h:128 * (h + 1)], X4[:, 128 * h:128 * (h + 1)],
                    F4[:, 128 * h:128 * (h + 1)], start=True, stop=True),
                    inc="TE" if h == 3 else None)
            V(lambda: nc.vector.tensor_copy(Hb, P2b[:, 0:512]), wait=[("TE", te_H), ("TE", te_G)],
              drain=False)
            ve_HG = V(lambda: nc.vector.tensor_copy(Gb, P2a[:, 0:512]), drain=False, inc="VE")
            te_h2 = None
            for h in range(4):
                te_h2 = em.op("tensor", lambda h=h: nc.tensor.matmul(
                    P2b[:, 512 + 128 * h:512 + 128 * (h + 1)], Gb[:, 128 * h:128 * (h + 1)],
                    Hb[:, 128 * h:128 * (h + 1)], start=True, stop=True),
                    wait=[("VE", ve_HG)] if h == 0 else (), inc="TE" if h == 3 else None)
            te_g2 = None
            for h in range(4):
                te_g2 = em.op("tensor", lambda h=h: nc.tensor.matmul(
                    P2a[:, 512 + 128 * h:512 + 128 * (h + 1)], Hb[:, 128 * h:128 * (h + 1)],
                    Gb[:, 128 * h:128 * (h + 1)], start=True, stop=True),
                    inc="TE" if h == 3 else None)
            ve_g2 = V(lambda: nc.vector.tensor_copy(F4, P2a[:, 512:1024]),
                      wait=[("TE", te_g2)], drain=False, inc="VE")
            te_h3 = None
            for h in range(4):
                te_h3 = em.op("tensor", lambda h=h: nc.tensor.matmul(
                    P2a[:, 128 * h:128 * (h + 1)], F4[:, 128 * h:128 * (h + 1)],
                    Hb[:, 128 * h:128 * (h + 1)], start=True, stop=True),
                    wait=[("VE", ve_g2)] if h == 0 else (), inc="TE" if h == 3 else None)
            V(lambda: nc.vector.tensor_tensor(XC, i4b[:, 0:512], Hb, ALU.subtract),
              wait=[("TE", te_h3)], drain=True)
            V(lambda: nc.vector.tensor_tensor(XC, XC, P2b[:, 512:1024], ALU.add), drain=True)
            ve_xc = V(lambda: nc.vector.tensor_tensor(XC, XC, P2a[:, 0:512], ALU.subtract),
                      drain=True, inc="VE")
            te_tdt = None
            for h in range(4):
                te_tdt = em.op("tensor", lambda h=h: nc.tensor.transpose(
                    PT[:, 128 * h:128 * (h + 1)], X4[:, 128 * h:128 * (h + 1)], ident_b[:, :]),
                    wait=[("VE", ve_xc)] if h == 0 else (), inc="TE" if h == 3 else None)
            ve_td = V(lambda: nc.vector.tensor_copy(Tdb, PT[:, 0:512]),
                      wait=[("TE", te_tdt)], drain=False, inc="VE")
            te_fin = None
            for h in range(4):
                te_fin = em.op("tensor", lambda h=h: nc.tensor.matmul(
                    P2b[:, 128 * h:128 * (h + 1)], Tdb[:, 128 * h:128 * (h + 1)],
                    XC[:, 128 * h:128 * (h + 1)], start=True, stop=True),
                    wait=[("VE", ve_td)] if h == 0 else (), inc="TE" if h == 3 else None)
            ve_X = V(lambda: nc.vector.tensor_copy(X4[:, :], P2b[:, 0:512]),
                     wait=[("TE", te_fin)], drain=False, inc="VE")
            for h in range(4):
                ks_ = head_kslices(h)
                vsl = v_tm[:, 1536 * c2 + 384 * h:1536 * c2 + 384 * (h + 1)]
                gsl = g_tm[:, 1536 * c2 + 384 * h:1536 * c2 + 384 * (h + 1)]
                ve_bv = V(lambda vsl=vsl, h=h: nc.vector.tensor_scalar(
                    bVb[:, :], vsl, colst[:, h:h + 1], None, ALU.mult),
                    wait=[("VE", ve_X), ("TE", prev["te_U"])], inc="VE", drain=True)
                V(lambda h=h, ksl=ksl: nc.vector.tensor_scalar(
                    ktl[:, :], ksl[:, 192 * h:192 * (h + 1)], colst[:, 8 + h:9 + h], None, ALU.mult),
                  drain=False)
                ve_kd = V(lambda h=h, ksl=ksl: nc.vector.tensor_scalar(
                    kdec[:, :], ksl[:, 192 * h:192 * (h + 1)], colst[:, 12 + h:13 + h], None, ALU.mult),
                    drain=False, inc="VE")
                te_u1 = em.op("tensor", lambda h=h: nc.tensor.matmul(
                    P1a[:, 0:384], X4[:, 128 * h:128 * (h + 1)], bVb[:, :], start=True, stop=False),
                    wait=[("VE", ve_bv), ("VE", prev["ve_Ub"])], inc="TE")
                em.op("tensor", lambda h=h: nc.tensor.matmul(
                    P1c[:, 0:128], ktl[:, 0:128], X4[:, 128 * h:128 * (h + 1)],
                    start=True, stop=True),
                    wait=[("VE", ve_kd), ("TE", prev["te_state"]), ("VE", prev["ve_state"])])
                te_w = em.op("tensor", lambda h=h: nc.tensor.matmul(
                    P1c[0:64, 128:256], ktl[:, 128:192], X4[:, 128 * h:128 * (h + 1)],
                    start=True, stop=True), inc="TE")
                ve_wn = V(lambda: nc.vector.tensor_scalar(
                    Wnegb[:, :], P1c[:, 0:256], -1.0, None, ALU.mult),
                    wait=[("TE", te_w)], inc="VE", drain=False)
                te_u = em.op("tensor", lambda h=h: nc.tensor.matmul(
                    P1a[:, 0:384], Wnegb[:, 0:128], S0bA[:, 384 * h:384 * (h + 1)],
                    start=False, stop=False),
                    wait=[("VE", ve_wn), ("VE", prev["ve_state"])])
                te_u = em.op("tensor", lambda h=h: nc.tensor.matmul(
                    P1a[:, 0:384], Wnegb[0:64, 128:256], S0bB[:, 384 * h:384 * (h + 1)],
                    start=False, stop=True), inc="TE")
                prev["te_U"] = te_u
                ve_ub = V(lambda: nc.vector.tensor_copy(Ub[:, :], P1a[:, 0:384]),
                          wait=[("TE", te_u)], inc="VE", drain=False)
                prev["ve_Ub"] = ve_ub
                te_o = em.op("tensor", lambda h=h: nc.tensor.matmul(
                    P1b[:, 0:384], MT4[:, 128 * h:128 * (h + 1)], Ub[:, :], start=True, stop=False),
                    wait=[("VE", ve_ub), ("VE", prev["ve_o4"])])
                for i, (cc, nr, loc) in enumerate(ks_):
                    s0 = S0bA if loc == 0 else S0bB
                    te_o = em.op("tensor", lambda cc=cc, nr=nr, h=h, i=i, s0=s0: nc.tensor.matmul(
                        P1b[:, 0:384], qbf_s[0:nr, cc:cc + 128],
                        s0[0:nr, 384 * h:384 * (h + 1)],
                        start=False, stop=(i == 1)), inc="TE" if i == 1 else None)
                prev["te_O"] = te_o
                te_s1 = em.op("tensor", lambda h=h: nc.tensor.matmul(
                    P2b[:, 512:896], kdec[:, 0:128], Ub[:, :], start=True, stop=True),
                    wait=[("VE", ve_ub)])
                te_s2 = em.op("tensor", lambda h=h: nc.tensor.matmul(
                    P1c[0:64, 0:384], kdec[:, 128:192], Ub[:, :], start=True, stop=True),
                    wait=[("VE", ve_wn)], inc="TE")
                prev["te_state"] = te_s2
                V(lambda h=h: nc.vector.scalar_tensor_tensor(
                    Sa[h][:, :], Sa[h][:, :], bC4[:, h:h + 1], P2b[:, 512:896], ALU.mult, ALU.add),
                  wait=[("TE", te_s2)], drain=True)
                V(lambda h=h: nc.vector.scalar_tensor_tensor(
                    Sb[h][:, :], Sb[h][:, :], bC4[0:64, h:h + 1], P1c[0:64, 0:384], ALU.mult, ALU.add),
                  drain=False)
                V(lambda h=h: nc.vector.tensor_copy(S0bA[:, 384 * h:384 * (h + 1)], Sa[h][:, :]),
                  drain=True)
                ve_st = V(lambda h=h: nc.vector.tensor_copy(S0bB[:, 384 * h:384 * (h + 1)], Sb[h][:, :]),
                          drain=False, inc="VE")
                prev["ve_state"] = ve_st
                se_sg = em.op("scalar", lambda gsl=gsl: nc.scalar.activation(sgb[:, :], gsl, AF.Silu),
                              wait=[("VE", prev["ve_pj"][3])], inc="SE")
                V(lambda: nc.vector.tensor_tensor(sgg[:, :], sgb[:, :], grw[:, :], ALU.mult),
                  wait=[("SE", se_sg), ("VE", ve_grw)], drain=False)
                V(lambda: nc.vector.tensor_copy(obf[:, :], P1b[:, 0:384]),
                  wait=[("TE", te_o)], drain=False)
                V(lambda: nc.vector.tensor_tensor(rtmp[:, 0:384], obf[:, :], obf[:, :], ALU.mult),
                  drain=True)
                ve_ro = V(lambda: nc.vector.reduce_sum(ro[:, 0:1], rtmp[:, 0:384], axis=AXX),
                          drain=True, inc="VE")
                em.op("scalar", lambda: nc.scalar.activation(ro[:, 1:2], ro[:, 0:1], AF.Ln,
                                                             scale=1.0 / DVD, bias=st["epss"][:, 0:1]),
                      wait=[("VE", ve_ro)])
                se_ro = em.op("scalar", lambda: nc.scalar.activation(ro[:, 2:3], ro[:, 1:2], AF.Exp,
                                                                     scale=-0.5), inc="SE")
                ve_o4 = V(lambda h=h: nc.vector.scalar_tensor_tensor(
                    o4tm[:, 384 * h:384 * (h + 1)], obf[:, :], ro[:, 2:3], sgg[:, :],
                    ALU.mult, ALU.mult), wait=[("SE", se_ro)], inc="VE", drain=True)
                prev["ve_o4"] = ve_o4
            wtp = [("VE", prev["ve_o4"]), ("VE", prev["ve_AT"])]
            for i in range(8):
                te = em.op("tensor", lambda i=i: nc.tensor.transpose(
                    PT[:, 128 * i:128 * (i + 1)], o4tm[:, 128 * i:128 * (i + 1)], ident_b[:, :]),
                    wait=wtp if i == 0 else (), inc="TE" if i == 7 else None)
            ve_of = em.op("vector", lambda ofmbase=ofmbase: nc.vector.tensor_copy(
                ofm2[:, ofmbase:ofmbase + 1024], PT[:, 0:1024]), wait=[("TE", te)], inc="VE")
            for i in range(4):
                te = em.op("tensor", lambda i=i: nc.tensor.transpose(
                    PT[:, 128 * i:128 * (i + 1)], o4tm[:, 1024 + 128 * i:1024 + 128 * (i + 1)],
                    ident_b[:, :]), wait=[("VE", ve_of)] if i == 0 else (),
                    inc="TE" if i == 3 else None)
            ve_of = em.op("vector", lambda ofmbase=ofmbase: nc.vector.tensor_copy(
                ofm2[:, ofmbase + 1024:ofmbase + 1536], PT[:, 0:512]), wait=[("TE", te)], inc="VE")
            prev["ve_ofm"] = ve_of
        PY = [P2a, P2b]
        te_y = [None, None]
        te_wblk = prev["te_wblk"]

        def wload(src, kc, bq, s=s):
            w = [("TE", te_wblk[bq])]
            if s == 0:
                w.append(("DW", st["dw_cast"]))
            return em.op("sync", lambda src=src, kc=kc, bq=bq: nc.sync.dma_start(
                out=wsob[bq][:, :], in_=src[128 * kc:128 * (kc + 1), :]),
                wait=w, inc="DS", n=16)

        ds_wso = [0] * 12
        for kc in range(3):
            ds_wso[kc] = wload(st["dso16"], kc, kc)
        for kc in range(12):
            bq = kc % 3
            for c2 in range(2):
                wmm = []
                if c2 == 0:
                    wmm.append(("DS", ds_wso[kc]))
                if kc == 0:
                    wmm += [("VE", prev["ve_ofm"]), ("VE", ve_sq), ("TE", te_prod),
                            ("VE", prev["ve_Ub"]), ("TE", prev["te_state"]), ("TE", prev["te_O"]),
                            ("VE", prev["ve_state"])]
                for nh in range(2):
                    te_y[c2] = em.op("tensor", lambda kc=kc, c2=c2, nh=nh, bq=bq: nc.tensor.matmul(
                        PY[c2][:, 512 * nh:512 * (nh + 1)],
                        ofm2[:, 1536 * c2 + 128 * kc:1536 * c2 + 128 * (kc + 1)],
                        wsob[bq][:, 512 * nh:512 * (nh + 1)],
                        start=(kc == 0), stop=(kc == 11)),
                        wait=wmm if (nh == 0 and wmm) else (), inc="TE" if nh == 1 else None)
            te_wblk[bq] = em.v("TE")
            if kc + 3 < 12:
                ds_wso[kc + 3] = wload(st["dso16"], kc + 3, (kc + 3) % 3)
        prev["te_y"] = [em.v("TE"), em.v("TE")]
        ve_yf = 0
        for c2 in range(2):
            ve_yt = em.op("vector", lambda c2=c2: nc.vector.tensor_copy(ytmp[:, 0:1024], PY[c2][:, :]),
                          wait=[("TE", te_y[c2]), ("VE", prev["ve_ytmp"])], inc="VE")
            for i in range(8):
                te = em.op("tensor", lambda i=i: nc.tensor.transpose(
                    PT[:, 128 * i:128 * (i + 1)], ytmp[:, 128 * i:128 * (i + 1)], ident_b[:, :]),
                    wait=[("VE", ve_yt), ("VE", ve_yf)] if i == 0 else (),
                    inc="TE" if i == 7 else None)
            ve_yf = em.op("vector", lambda c2=c2: nc.vector.tensor_copy(
                yfm2[:, 1024 * c2:1024 * (c2 + 1)], PT[:, 0:1024]), wait=[("TE", te)], inc="VE")
            prev["ve_ytmp"] = ve_yf
        te_om = [None, None]
        ds_wout = [0] * 16
        for kc in range(3):
            ds_wout[kc] = wload(st["dout16"], kc, kc)
        for kc in range(16):
            bq = kc % 3
            for c2 in range(2):
                if kc < 8:
                    lhs = lambda kc=kc, c2=c2: yfm2[:, 1024 * c2 + 128 * kc:1024 * c2 + 128 * (kc + 1)]
                else:
                    lhs = lambda kc=kc, c2=c2: zseg[:, 256 * (kc - 8) + 128 * c2:
                                                    256 * (kc - 8) + 128 * c2 + 128]
                wmm = []
                if c2 == 0:
                    wmm.append(("DS", ds_wout[kc]))
                if kc == 0:
                    wmm.append(("VE", ve_yf))
                if kc == 8 and c2 == 0:
                    wmm.append(("DS", prev["ds_zs"]))
                for nh in range(2):
                    te_om[c2] = em.op("tensor", lambda kc=kc, c2=c2, nh=nh, bq=bq, lhs=lhs: nc.tensor.matmul(
                        PY[c2][:, 512 * nh:512 * (nh + 1)], lhs(),
                        wsob[bq][:, 512 * nh:512 * (nh + 1)],
                        start=(kc == 0), stop=(kc == 15)),
                        wait=wmm if (nh == 0 and wmm) else (), inc="TE" if nh == 1 else None)
            te_wblk[bq] = em.v("TE")
            if kc + 3 < 16:
                ds_wout[kc + 3] = wload(st["dout16"], kc + 3, (kc + 3) % 3)
        prev["te_out"] = [em.v("TE"), em.v("TE")]
        for c2 in range(2):
            ve_oadd = em.op("vector", lambda c2=c2, s=s: nc.vector.tensor_tensor(
                xacc[:, 1024 * (2 * s + c2):1024 * (2 * s + c2 + 1)], PY[c2][:, :],
                xacc[:, 1024 * (2 * s + c2):1024 * (2 * s + c2 + 1)], ALU.add),
                wait=[("TE", te_om[c2])], inc="VE")
        prev["ve_out"] = ve_oadd
    if "a2d" in st["taps"]:
        fin = st["em"].snap()
        pieces = [gam, ngam, bex, betf, colst, bC4, rq, q_tm, k_tm, v_tm, g_tm,
                  kf_s, qf_s, qbf_s, qb_tm, A4, AT4, X4, MT4, Ub, Wnegb, bVb, obf,
                  o4tm, ofm2, yfm2, Sa[0], Sb[0], S0bA, sgb]
        col = 0
        for i, tns in enumerate(pieces):
            w = tns.shape[1]
            em.op("gpsimd", lambda tns=tns, col=col, w=w: nc.gpsimd.dma_start(
                out=st["dbg"]["a2d"][0:tns.shape[0], col:col + w], in_=tns[:, :]),
                wait=[("VE", fin["VE"]), ("SE", fin["SE"]), ("TE", fin["TE"])] if i == 0 else (),
                inc="DW", n=16)
            col += w
        st["a2d_cols"] = col
    a2.close()
    a.close()



def kernel(**inputs):
    ins = prep_inputs(inputs)
    nc = build_nc(debug_taps=DEBUG_TAPS)
    in_maps = [dict(ins, x=np.ascontiguousarray(inputs["L0_attn_x"][b])) for b in range(8)]
    res = run_bass_kernel_spmd(nc, in_maps, core_ids=list(range(8)))
    return np.stack([res.results[b]["y"] for b in range(8)], axis=0)


def prep_inputs(inputs):
    P = {}
    P["W_in"] = np.ascontiguousarray(inputs["L0_attn_W_in"])
    P["convw"] = np.ascontiguousarray(np.concatenate(
        [inputs["L0_attn_conv_x_w"].reshape(D, 4), inputs["L0_attn_conv_z_w"].reshape(D, 4)], axis=1))
    P["convb"] = np.ascontiguousarray(np.stack(
        [inputs["L0_attn_conv_x_b"], inputs["L0_attn_conv_z_b"]], axis=1))
    P["Wqkvg"] = np.ascontiguousarray(np.concatenate(
        [inputs["L0_attn_W_q"], inputs["L0_attn_W_k"], inputs["L0_attn_W_v"],
         inputs["L0_attn_W_g"], inputs["L0_attn_W_gk"], inputs["L0_attn_W_b"]], axis=1))
    P["W_so"] = np.ascontiguousarray(inputs["L0_attn_W_so"])
    P["W_out"] = np.ascontiguousarray(inputs["L0_attn_W_out"])
    P["Alog"] = np.ascontiguousarray(inputs["L0_attn_A_log"].reshape(4, 1))
    P["dtbb"] = np.ascontiguousarray(np.concatenate(
        [inputs["L0_attn_dt_bias"], inputs["L0_attn_b_b"]]).reshape(8, 1))
    P["gnw"] = np.ascontiguousarray(inputs["L0_attn_g_norm_w"].reshape(1, DVD))
    P["rw"] = np.ascontiguousarray(inputs["L0_ffn_router_w"])
    P["corr"] = np.ascontiguousarray(inputs["L0_ffn_correction_bias"].reshape(1, NE))
    P["up_w"] = np.ascontiguousarray(inputs["L0_ffn_up_w"])
    P["down_w"] = np.ascontiguousarray(inputs["L0_ffn_down_w"])
    P["sh_up"] = np.ascontiguousarray(inputs["L0_ffn_shared_up_w"])
    P["sh_down"] = np.ascontiguousarray(inputs["L0_ffn_shared_down_w"])
    return P
